# version 02
# baseline (speedup 1.0000x reference)
import numpy as np
import ml_dtypes
from contextlib import ExitStack

import concourse.bass as bass
import concourse.mybir as mybir
from concourse.bass_utils import run_bass_kernel_spmd

F32 = mybir.dt.float32
BF16 = mybir.dt.bfloat16
AF = mybir.ActivationFunctionType
ALU = mybir.AluOpType
BFNP = ml_dtypes.bfloat16

B, N, D = 2, 4096, 2048
NCORES = 8
T = 1024
HALO = 128
TH = T + HALO
NDT = 16
CH = 64
GLA_H = 16
NCH = TH // CH
HCH = HALO // CH
FFN_H = 8192
AH, AKVH, AD = 32, 8, 64
EPS_LN = 1e-5
EPS_RMS = 1e-6
ROPE_THETA = 10000.0

ENGINES = ("sync", "gpsimd", "vector", "scalar", "tensor")

NSUB = 4
DBG = None


class Ch:
    def __init__(self, handle):
        self.h = handle
        self.n = 0

    def bump(self, k):
        self.n += k
        return self.n


class Sched:
    def __init__(self):
        self.q = {e: [] for e in ENGINES}

    def emit(self, eng_name, fn, wait=(), inc=()):
        inc_t = list(inc)
        for c, k in inc_t:
            c.bump(k)
        wait_t = [(c, v) for (c, v) in wait if v is not None]

        def closure(eng):
            for c, v in wait_t:
                eng.wait_ge(c.h, v)
            inst = fn(eng)
            for c, k in inc_t:
                assert inst is not None
                inst.then_inc(c.h, k)

        self.q[eng_name].append(closure)


def build_program():
    nc = bass.Bass("TRN2", target_bir_lowering=False)

    def din(name, shape, dtype=F32):
        return nc.dram_tensor(name, list(shape), dtype, kind="ExternalInput").ap()

    xT_in = din("xT", [D, T])
    xTh_in = din("xTh", [D, HALO])
    aWqT = din("aWqT", [D, D]); aWkT = din("aWkT", [D, D])
    aWvT = din("aWvT", [D, D]); aWoT = din("aWoT", [D, D])
    anw = din("anw", [128, NDT])
    bWqT = din("bWqT", [D, D]); bWkT = din("bWkT", [D, 512])
    bWvT = din("bWvT", [D, 512]); bWoT = din("bWoT", [D, D])
    fWgT = din("fWgT", [D, FFN_H]); fWuT = din("fWuT", [D, FFN_H])
    fWdT = din("fWdT", [FFN_H, D])
    maskT_in = din("maskT", [CH, CH], BF16)
    idb_in = din("idb", [128, 128], BF16)
    ones1_in = din("ones1", [128, 1], BF16)
    onesF_in = din("onesF", [1, 128], F32)
    hsum2_in = din("hsum2", [128, 2], BF16)
    cosT_in = din("cosT", [128, T], F32)
    sinT_in = din("sinT", [128, T], F32)
    fconst_in = din("fconst", [1, 4], F32)
    sgn_in = din("sgn", [128, 1], F32)

    out_d = nc.dram_tensor("oT", [D, T], F32, kind="ExternalOutput").ap()

    fwg_bf = nc.dram_tensor("fwg_bf", [D, FFN_H], BF16, kind="Internal")
    fwu_bf = nc.dram_tensor("fwu_bf", [D, FFN_H], BF16, kind="Internal")
    fwd_bf = nc.dram_tensor("fwd_bf", [FFN_H, D], BF16, kind="Internal")
    cc_in_k = nc.dram_tensor("cc_in_k", [513, T], BF16, kind="Internal")
    cc_out_k = nc.dram_tensor("cc_out_k", [4 * 513, T], BF16, kind="Internal")
    cc_in_v = nc.dram_tensor("cc_in_v", [T, 512], BF16, kind="Internal")
    cc_out_v = nc.dram_tensor("cc_out_v", [4 * T, 512], BF16, kind="Internal")
    RG = [[0, 1, 2, 3], [4, 5, 6, 7]]

    S = Sched()

    with ExitStack() as stack:
        e = stack.enter_context

        chans = {}
        for name in ("dIn", "dW", "dOut", "dG", "pe", "act", "dve", "gp", "cc"):
            chans[name] = Ch(e(nc.semaphore(name)))
        dIn, dW, dOut, dG = chans["dIn"], chans["dW"], chans["dOut"], chans["dG"]
        cPE, cACT, cDVE, cGP = chans["pe"], chans["act"], chans["dve"], chans["gp"]
        cCC = chans["cc"]

        sb = lambda name, shape, dt: e(nc.sbuf_tensor(name, list(shape), dt))
        xT = sb("xT_sb", [128, NDT * T], F32)
        xn = sb("xn_sb", [128, NDT * T], BF16)
        xnh = sb("xnh_sb", [128, NDT * HALO], BF16)
        maskT = sb("maskT_sb", [CH, CH], BF16)
        idb = sb("idb_sb", [128, 128], BF16)
        ones1 = sb("ones1_sb", [128, 1], BF16)
        onesF = sb("onesF_sb", [1, 128], F32)
        hsum2 = sb("hsum2_sb", [128, 2], BF16)
        fconst = sb("fconst_sb", [1, 4], F32)
        anw_sb = sb("anw_sb", [128, NDT], F32)
        zer64 = sb("zer64_sb", [128, CH], F32)
        sqs = [sb(f"lnsq{i}", [128, 512], BF16) for i in range(2)]
        stS = sb("ln_stS", [1, 1536], F32)
        nsc = sb("ln_nsc", [128, 512], F32)
        wslab = [sb(f"wslab{i}", [128, 16 * 256], BF16) for i in range(2)]

        ps = lambda name, shape, dt=F32: e(nc.psum_tensor(name, list(shape), dt))
        PA = ps("PA", [128, 1024])
        PB = ps("PB", [128, 1024])
        PC = ps("PC", [128, 512])
        PD = ps("PD", [128, 512])
        PE_ = ps("PE", [128, 512])
        PF = ps("PF", [128, 1024], BF16)

        def dma(engn, dch, out_ap, in_ap, extra_wait=()):
            S.emit(engn, lambda eng, o=out_ap, i=in_ap: eng.dma_start(out=o, in_=i),
                   wait=extra_wait, inc=[(dch, 16)])

        x3 = xT[:].rearrange("p (d t) -> p d t", d=NDT)
        xn3 = xn[:].rearrange("p (d t) -> p d t", d=NDT)
        xnh3 = xnh[:].rearrange("p (d t) -> p d t", d=NDT)

        allch = list(chans.values())

        def barrier():
            marks = [(c, c.n) for c in allch]
            for en in ENGINES:
                S.emit(en, lambda eng: None, wait=marks)

        ln_state = {"sums_free": None, "pa_free": None,
                    "sq_guard": [None, None], "nsc_free": None,
                    "stat_free": []}

        def emit_norm(src3, dst3, ntok, rms=False, extra_wait=()):
            first = True
            c0 = 0
            stm = stS[0:1, 0:512]
            ste = stS[0:1, 512:1024]
            stt = stS[0:1, 1024:1536]
            while c0 < ntok:
                n = min(512, ntok - c0)
                for dt in range(NDT):
                    if not rms:
                        w = list(extra_wait) if (first and dt == 0) else []
                        S.emit("vector",
                               lambda eng, d=dt, c=c0, nn=n: eng.tensor_copy(
                                   dst3[:, d, c:c + nn], src3[:, d, c:c + nn]),
                               wait=w, inc=[(cDVE, 1)])
                        cast_m = cDVE.n
                        sq_src3 = dst3
                    else:
                        cast_m = None
                        sq_src3 = src3
                    w = ([] if cast_m is None else [(cDVE, cast_m)])
                    if first and dt == 0:
                        w += list(extra_wait) if rms else []
                    if ln_state["sq_guard"][dt % 2] is not None:
                        w.append((cPE, ln_state["sq_guard"][dt % 2]))
                    S.emit("scalar",
                           lambda eng, d=dt, c=c0, nn=n, s3=sq_src3:
                           eng.activation(sqs[d % 2][:, 0:nn], s3[:, d, c:c + nn],
                                          AF.Square),
                           wait=w, inc=[(cACT, 1)])
                    sq_m = cACT.n
                    if not rms:
                        w = [(cDVE, cast_m)]
                        if dt == 0 and ln_state["sums_free"] is not None:
                            w.append(ln_state["sums_free"])
                        S.emit("tensor",
                               lambda eng, d=dt, c=c0, nn=n: eng.matmul(
                                   PC[0:1, 0:nn], ones1[:],
                                   dst3[:, d, c:c + nn],
                                   start=(d == 0), stop=(d == NDT - 1)),
                               wait=w, inc=[(cPE, 1)] if dt == NDT - 1 else [])
                        if dt == NDT - 1:
                            m_sx = cPE.n
                    w = [(cACT, sq_m)]
                    if dt == 0 and ln_state["sums_free"] is not None:
                        w.append(ln_state["sums_free"])
                    S.emit("tensor",
                           lambda eng, d=dt, nn=n: eng.matmul(
                               PD[0:1, 0:nn], ones1[:], sqs[d % 2][:, 0:nn],
                               start=(d == 0), stop=(d == NDT - 1)),
                           wait=w, inc=[(cPE, 1)])
                    ln_state["sq_guard"][dt % 2] = cPE.n
                m_sx2 = cPE.n
                first = False
                S.emit("scalar", lambda eng, nn=n: eng.mul(
                    ste[:, 0:nn], PD[0:1, 0:nn], 1.0 / D),
                       wait=[(cPE, m_sx2)] + ln_state["stat_free"],
                       inc=[(cACT, 1)])
                a2 = cACT.n
                if not rms:
                    S.emit("scalar", lambda eng, nn=n: eng.mul(
                        stm[:, 0:nn], PC[0:1, 0:nn], 1.0 / D),
                           wait=[(cPE, m_sx)] + ln_state["stat_free"],
                           inc=[(cACT, 1)])
                    a1 = cACT.n
                    ln_state["sums_free"] = (cACT, a1)
                    S.emit("vector", lambda eng, nn=n: eng.tensor_mul(
                        stt[:, 0:nn], stm[:, 0:nn], stm[:, 0:nn]),
                           wait=[(cACT, a1)], inc=[(cDVE, 1)])
                    v1 = cDVE.n
                    S.emit("vector", lambda eng, nn=n: eng.tensor_sub(
                        ste[:, 0:nn], ste[:, 0:nn], stt[:, 0:nn]),
                           wait=[(cACT, a2), (cDVE, v1)], inc=[(cDVE, 1)])
                    v2 = cDVE.n
                    eps_i = 0
                    var_ap = ste
                    wln = [(cDVE, v2)]
                else:
                    ln_state["sums_free"] = (cACT, a2)
                    eps_i = 1
                    var_ap = ste
                    wln = [(cACT, a2)]
                S.emit("scalar", lambda eng, nn=n, va=var_ap, ei=eps_i:
                       eng.activation(stt[:, 0:nn], va[:, 0:nn], AF.Ln,
                                      bias=fconst[0:1, ei:ei + 1]),
                       wait=wln, inc=[(cACT, 1)])
                a3 = cACT.n
                S.emit("scalar", lambda eng, nn=n: eng.activation(
                    ste[:, 0:nn], stt[:, 0:nn], AF.Exp, scale=-0.5),
                       wait=[(cACT, a3)], inc=[(cACT, 1)])
                a4 = cACT.n
                if not rms:
                    S.emit("vector", lambda eng, nn=n: eng.scalar_tensor_tensor(
                        stt[:, 0:nn], stm[:, 0:nn], -1.0,
                        ste[:, 0:nn], ALU.mult, ALU.mult),
                           wait=[(cACT, a4)], inc=[(cDVE, 1)])
                    v3 = cDVE.n
                w = [(cACT, a4)]
                if ln_state["pa_free"] is not None:
                    w.append(ln_state["pa_free"])
                S.emit("tensor", lambda eng, nn=n: eng.matmul(
                    PA[:, 0:nn], onesF[:], ste[:, 0:nn], start=True, stop=True),
                       wait=w, inc=[(cPE, 1)])
                p1 = cPE.n
                if not rms:
                    S.emit("tensor", lambda eng, nn=n: eng.matmul(
                        PA[:, 512:512 + nn], onesF[:], stt[:, 0:nn],
                        start=True, stop=True),
                           wait=[(cDVE, v3)], inc=[(cPE, 1)])
                    p2 = cPE.n
                ln_state["stat_free"] = [(cPE, p1)] + (
                    [(cPE, p2), (cDVE, v3)] if not rms else [])
                for dt in range(NDT):
                    w = [(cPE, p1)] if dt == 0 else []
                    if ln_state["nsc_free"] is not None:
                        w.append((cDVE, ln_state["nsc_free"]))
                    S.emit("vector",
                           lambda eng, d=dt, c=c0, nn=n: eng.tensor_mul(
                               nsc[:, 0:nn], src3[:, d, c:c + nn], PA[:, 0:nn]),
                           wait=w, inc=[(cDVE, 1)])
                    vm = cDVE.n
                    if not rms:
                        w2 = [(cDVE, vm)] + ([(cPE, p2)] if dt == 0 else [])
                        S.emit("vector",
                               lambda eng, d=dt, c=c0, nn=n: eng.tensor_add(
                                   dst3[:, d, c:c + nn], nsc[:, 0:nn],
                                   PA[:, 512:512 + nn]),
                               wait=w2, inc=[(cDVE, 1)])
                    else:
                        S.emit("vector",
                               lambda eng, d=dt, c=c0, nn=n: eng.tensor_scalar_mul(
                                   dst3[:, d, c:c + nn], nsc[:, 0:nn],
                                   anw_sb[:, d:d + 1]),
                               wait=[(cDVE, vm)], inc=[(cDVE, 1)])
                    ln_state["nsc_free"] = cDVE.n
                ln_state["pa_free"] = (cDVE, cDVE.n)
                c0 += n
            return (cDVE, cDVE.n)

        slab_st = {"mark": [None, None], "i": 0}
        wsl3 = [w[:].rearrange("p (k c) -> p k c", c=256) for w in wslab]

        def load_slab(wdram, col0, ncols, kdim=16, kstep=1):
            i = slab_st["i"]
            slab_st["i"] ^= 1
            w = []
            if slab_st["mark"][i] is not None:
                w.append(slab_st["mark"][i])
            src = wdram.rearrange("(k p) c -> p k c", p=128)[:, 0:kdim, col0:col0 + ncols]
            b = wsl3[i]
            S.emit("gpsimd",
                   lambda eng, bb=b, ss=src, kd=kdim, nn=ncols: eng.dma_start(
                       out=bb[:, 0:kd, 0:nn], in_=ss),
                   wait=w, inc=[(dW, 16)])
            return b, (dW, dW.n), i

        def slab_done(i, mark):
            slab_st["mark"][i] = mark

        pc_state = {"done": None}

        def precast_chunks():
            jobs = []
            for wsrc, wdst in ((fWgT, fwg_bf), (fWuT, fwu_bf)):
                for c0 in range(0, FFN_H, 256):
                    jobs.append((wsrc, wdst, 0, c0))
            for kq in range(4):
                for c0 in range(0, D, 256):
                    jobs.append((fWdT, fwd_bf, kq * 16, c0))
            for wsrc, wdst, k0, c0 in jobs:
                b, rdy, i = load_slab(
                    wsrc.rearrange("(a p) c -> a p c", p=128)[
                        k0:k0 + 16].rearrange("a p c -> (a p) c"), c0, 256)
                S.emit("sync",
                       lambda eng, bb=b, wd=wdst, kk=k0, cc=c0: eng.dma_start(
                           out=wd.ap().rearrange("(k p) c -> p k c", p=128)[
                               :, kk:kk + 16, cc:cc + 256],
                           in_=bb[:, 0:16, 0:256]),
                       wait=[rdy], inc=[(dG, 16)])
                slab_done(i, (dG, dG.n))
                yield
            pc_state["done"] = (dG, dG.n)

        with ExitStack() as ph1:
            xTh = ph1.enter_context(
                nc.sbuf_tensor("xTh_sb", [128, NDT * HALO], F32, side="right"))
            xh3 = xTh[:].rearrange("p (d t) -> p d t", d=NDT)
            dma("sync", dIn, x3, xT_in.rearrange("(d p) t -> p d t", p=128))
            dma("sync", dIn, xh3, xTh_in.rearrange("(d p) t -> p d t", p=128))
            for dst, src in ((maskT, maskT_in), (idb, idb_in), (ones1, ones1_in),
                             (onesF, onesF_in), (hsum2, hsum2_in),
                             (anw_sb, anw), (fconst, fconst_in)):
                dma("sync", dIn, dst[:], src)
            S.emit("vector", lambda eng: eng.memset(zer64[:], 0.0), inc=[(cDVE, 1)])
            in_done = dIn.n

            ln1_own = emit_ln_marker = emit_norm(
                x3, xn3, T, extra_wait=[(dIn, in_done)])
            ln1_halo = emit_norm(xh3, xnh3, HALO)

        dbg_hold = {}

        def emit_attn_a():
            with ExitStack() as ph:
                er = lambda nm, shape, dt: ph.enter_context(
                    nc.sbuf_tensor(nm, list(shape), dt, side="right"))
                oTt = er("gla_oT", [128, GLA_H * T], BF16)
                fT = er("gla_f", [128, 2 * TH], F32)
                nlfT = er("gla_nlf", [128, 2 * TH], F32)
                usc = er("gla_usc", [128, 288], F32)
                kT = er("gla_kT", [128, 2 * TH], BF16)
                qT = er("gla_qT", [128, 2 * T], BF16)
                vU = er("gla_vU", [128, 9 * 256], BF16)
                Sst = er("gla_S", [128, 2 * 128], BF16)
                ct = er("gla_ct", [128, 4 * CH], F32)
                eCp = er("gla_eCp", [128, 4 * CH], F32)
                eCn = er("gla_eCn", [128, 4 * CH], F32)
                qdec = er("gla_qdec", [128, 4 * CH], BF16)
                kdec = er("gla_kdec", [128, 4 * CH], BF16)
                kdecU = er("gla_kdecU", [128, 2 * 128], BF16)
                pm = er("gla_pm", [128, 2 * CH], BF16)
                qsg = er("gla_qsg", [128, 512], BF16)
                ssc = er("gla_ssc", [128, 256], BF16)
                o3 = oTt[:].rearrange("p (h t) -> p h t", h=GLA_H)
                dbg_hold["o"] = o3
                f3 = fT[:].rearrange("p (h t) -> p h t", h=2)
                nlf3 = nlfT[:].rearrange("p (h t) -> p h t", h=2)
                k3 = kT[:].rearrange("p (h t) -> p h t", h=2)
                q3 = qT[:].rearrange("p (h t) -> p h t", h=2)
                v3 = vU[:].rearrange("p (tt e) -> p tt e", e=256)

                segs = [(xnh3, 0, HALO, 0), (xn3, 0, 512, HALO),
                        (xn3, 512, 512, HALO + 512)]
                pp = [PC, PD]
                pp_mark = [None, None]
                ppi = [0]

                def proj_accum(slab3, octile, seg, extra_w=()):
                    srcT, sc0, sn, _ = seg
                    i = ppi[0]
                    ppi[0] ^= 1
                    psb = pp[i]
                    for k in range(NDT):
                        w = list(extra_w) if k == 0 else []
                        if k == 0 and pp_mark[i] is not None:
                            w.append(pp_mark[i])
                        S.emit("tensor",
                               lambda eng, kk=k, s3=slab3, oc=octile, st=srcT,
                                      c=sc0, nn=sn, pb=psb: eng.matmul(
                                   pb[:, 0:nn], s3[:, kk, oc * 128:(oc + 1) * 128],
                                   st[:, kk, c:c + nn],
                                   start=(kk == 0), stop=(kk == NDT - 1)),
                               wait=w, inc=[(cPE, 1)] if k == NDT - 1 else [])
                    return psb, i, (cPE, cPE.n)

                hm = [dict(), dict()]
                gl_pf = [None]
                glast = {"nlf_r": None, "k_r": None, "q_r": None, "v_r": None,
                         "f_r": None}

                pcg = precast_chunks()
                for g in range(8):
                    zsl, zrdy, zi = load_slab(aWkT, g * 256, 256)
                    qsl, qrdy, qi = load_slab(aWqT, g * 256, 256)
                    f_marks = []
                    for oc in range(2):
                        for seg in segs:
                            psb, i, pmk = proj_accum(zsl, oc, seg,
                                                     extra_w=[zrdy, ln1_own,
                                                              ln1_halo])
                            toff = seg[3]
                            w = [pmk]
                            if glast["f_r"] is not None:
                                w.append(glast["f_r"])
                            S.emit("scalar",
                                   lambda eng, o=oc, to=toff, nn=seg[2], pb=psb:
                                   eng.activation(f3[:, o, to:to + nn],
                                                  pb[:, 0:nn], AF.Sigmoid),
                                   wait=w, inc=[(cACT, 1)])
                            f_marks.append(cACT.n)
                            pp_mark[i] = (cACT, cACT.n)
                    slab_done(zi, (cPE, cPE.n))
                    k_marks = []
                    for hh in range(2):
                        w = [(cACT, f_marks[-1])]
                        if glast["k_r"] is not None:
                            w.append(glast["k_r"])
                        S.emit("scalar",
                               lambda eng, h=hh: eng.activation(
                                   k3[:, h, :], f3[:, h, :], AF.Sigmoid,
                                   scale=-1.0),
                               wait=w, inc=[(cACT, 1)])
                        k_marks.append(cACT.n)
                    segs_own = [(xn3, 0, 512, 0), (xn3, 512, 512, 512)]
                    qsg_free = None
                    for oc in range(2):
                        for seg in segs_own:
                            psb, i, pmk = proj_accum(qsl, oc, seg,
                                                     extra_w=[qrdy])
                            w = [pmk] + ([qsg_free] if qsg_free else [])
                            S.emit("scalar",
                                   lambda eng, nn=seg[2], pb=psb: eng.activation(
                                       qsg[:, 0:nn], pb[:, 0:nn], AF.Sigmoid),
                                   wait=w, inc=[(cACT, 1)])
                            am = cACT.n
                            w = [(cACT, am)]
                            if glast["q_r"] is not None and oc == 0 and seg[1] == 0:
                                w.append(glast["q_r"])
                            S.emit("vector",
                                   lambda eng, o=oc, to=seg[3], nn=seg[2], pb=psb:
                                   eng.tensor_mul(q3[:, o, to:to + nn],
                                                  pb[:, 0:nn], qsg[:, 0:nn]),
                                   wait=w, inc=[(cDVE, 1)])
                            qsg_free = (cDVE, cDVE.n)
                            pp_mark[i] = (cDVE, cDVE.n)
                    slab_done(qi, (cPE, cPE.n))
                    vsl, vrdy, vi = load_slab(aWvT, g * 256, 256)
                    for tt in range(9):
                        if tt == 0:
                            lhs_src, lc = xnh3, 0
                        else:
                            lhs_src, lc = xn3, (tt - 1) * 128
                        i = ppi[0]
                        ppi[0] ^= 1
                        psb = pp[i]
                        for k in range(NDT):
                            w = [vrdy] if k == 0 else []
                            if k == 0 and pp_mark[i] is not None:
                                w.append(pp_mark[i])
                            S.emit("tensor",
                                   lambda eng, kk=k, ls=lhs_src, c=lc, pb=psb,
                                          vs=vsl: eng.matmul(
                                       pb[:, 0:256], ls[:, kk, c:c + 128],
                                       vs[:, kk, 0:256],
                                       start=(kk == 0), stop=(kk == NDT - 1)),
                                   wait=w, inc=[(cPE, 1)] if k == NDT - 1 else [])
                        pmk = (cPE, cPE.n)
                        w = [pmk]
                        if glast["v_r"] is not None and tt == 0:
                            w.append(glast["v_r"])
                        S.emit("scalar",
                               lambda eng, t=tt, pb=psb: eng.copy(
                                   v3[:, t, :], pb[:, 0:256]),
                               wait=w, inc=[(cACT, 1)])
                        pp_mark[i] = (cACT, cACT.n)
                    slab_done(vi, (cPE, cPE.n))
                    v_evac_m = cACT.n
                    nlf_marks = []
                    for hh in range(2):
                        for half in range(4):
                            c0h = half * 288
                            w = [(cACT, f_marks[-1])]
                            S.emit("scalar",
                                   lambda eng, h=hh, c=c0h: eng.activation(
                                       usc[:, 0:288], f3[:, h, c:c + 288],
                                       AF.Exp, scale=-1.0),
                                   wait=w, inc=[(cACT, 1)])
                            um = cACT.n
                            w = [(cACT, um)]
                            if glast["nlf_r"] is not None:
                                w.append(glast["nlf_r"])
                            S.emit("scalar",
                                   lambda eng, h=hh, c=c0h: eng.activation(
                                       nlf3[:, h, c:c + 288], usc[:, 0:288],
                                       AF.Ln, bias=1.0),
                                   wait=w, inc=[(cACT, 1)])
                            nlf_marks.append(cACT.n)
                    nlf_m = cACT.n

                    for hh in range(2):
                        w = []
                        if hm[hh].get("S_mm") is not None:
                            w += [(cPE, hm[hh]["S_mm"]), (cDVE, hm[hh]["S_scale"]),
                                  (cPE, hm[hh]["o_mm"])]
                        S.emit("vector",
                               lambda eng, h=hh: eng.memset(
                                   Sst[:, h * 128:(h + 1) * 128], 0.0),
                               wait=w, inc=[(cDVE, 1)])
                        hm[hh]["S_scale"] = cDVE.n
                        for kk2 in ("oevac", "mask", "scores", "scores2",
                                    "o_mm", "o_mm2", "S_mm", "S_mm2", "S_add",
                                    "tr", "tr2", "kdecU_e", "S_scale2"):
                            hm[hh].setdefault(kk2, None)
                        for kk2 in ("eCp_w", "eCn_w", "kdec_w", "qdec_w"):
                            hm[hh].setdefault(kk2, [None, None])

                    pf_free = gl_pf[0]
                    for c in range(NCH):
                        own = c >= HCH
                        pr = c % 2
                        for hh in range(2):
                            m = hm[hh]
                            sl = lambda ap2, h, p: ap2[:, (h * 2 + p) * CH:
                                                       (h * 2 + p + 1) * CH]
                            ctS = ct[:, (hh * 2 + pr) * CH:(hh * 2 + pr + 1) * CH]
                            eCpS = eCp[:, (hh * 2 + pr) * CH:(hh * 2 + pr + 1) * CH]
                            eCnS = eCn[:, (hh * 2 + pr) * CH:(hh * 2 + pr + 1) * CH]
                            qdS = qdec[:, (hh * 2 + pr) * CH:(hh * 2 + pr + 1) * CH]
                            kdS = kdec[:, (hh * 2 + pr) * CH:(hh * 2 + pr + 1) * CH]
                            w = [(cACT, nlf_m)]
                            if m["eCp_w"][pr] is not None:
                                w += [(cACT, m["eCp_w"][pr]),
                                      (cACT, m["eCn_w"][pr])]
                            S.emit("vector",
                                   lambda eng, h=hh, cc=c, o=ctS:
                                   eng.tensor_tensor_scan(
                                       o, nlf3[:, h, cc * CH:(cc + 1) * CH],
                                       zer64[:, 0:CH], 0.0, ALU.add, ALU.add),
                                   wait=w, inc=[(cDVE, 1)])
                            scan_m = cDVE.n
                            w = [(cDVE, scan_m)]
                            if m["kdec_w"][pr] is not None:
                                w.append((cDVE, m["kdec_w"][pr]))
                            S.emit("scalar",
                                   lambda eng, o=eCpS, i0=ctS: eng.activation(
                                       o, i0, AF.Exp),
                                   wait=w, inc=[(cACT, 1)])
                            m["eCp_w"][pr] = cACT.n
                            w = [(cDVE, scan_m)]
                            if m["qdec_w"][pr] is not None:
                                w.append((cDVE, m["qdec_w"][pr]))
                            if m["S_scale2"] is not None:
                                w.append((cDVE, m["S_scale2"]))
                            S.emit("scalar",
                                   lambda eng, o=eCnS, i0=ctS: eng.activation(
                                       o, i0, AF.Exp, scale=-1.0),
                                   wait=w, inc=[(cACT, 1)])
                            m["eCn_w"][pr] = cACT.n
                            w = [(cACT, m["eCp_w"][pr]), (cACT, k_marks[hh])]
                            if m["tr2"] is not None:
                                w.append((cPE, m["tr2"]))
                            if m["scores2"] is not None:
                                w.append((cPE, m["scores2"]))
                            S.emit("vector",
                                   lambda eng, h=hh, cc=c, o=kdS: eng.tensor_mul(
                                       o, k3[:, h, cc * CH:(cc + 1) * CH],
                                       eCp[:, (h * 2 + cc % 2) * CH:
                                           (h * 2 + cc % 2 + 1) * CH]),
                                   wait=w, inc=[(cDVE, 1)])
                            m["kdec_w"][pr] = cDVE.n
                            w = [(cDVE, m["kdec_w"][pr]), (dIn, in_done)]
                            if pf_free is not None:
                                w.append(pf_free)
                            S.emit("tensor",
                                   lambda eng, h=hh, i0=kdS: eng.transpose(
                                       PF[0:64, h * 128:h * 128 + 128],
                                       i0, idb[:]),
                                   wait=w, inc=[(cPE, 1)])
                            m["tr2"] = m["tr"]
                            m["tr"] = cPE.n
                            bbu = pr * 64
                            w = [(cPE, m["tr"])]
                            if m["S_mm2"] is not None:
                                w.append((cPE, m["S_mm2"]))
                            S.emit("vector",
                                   lambda eng, h=hh, b=bbu: eng.tensor_copy(
                                       kdecU[b:b + 64, h * 128:(h + 1) * 128],
                                       PF[0:64, h * 128:h * 128 + 128]),
                                   wait=w, inc=[(cDVE, 1)])
                            m["kdecU_e"] = cDVE.n
                            pf_free = (cDVE, cDVE.n)
                            if own:
                                co = c - HCH
                                w = [(cACT, m["eCn_w"][pr])]
                                if m["o_mm2"] is not None:
                                    w.append((cPE, m["o_mm2"]))
                                S.emit("vector",
                                       lambda eng, h=hh, t0=co * CH, o=qdS:
                                       eng.tensor_mul(
                                           o, q3[:, h, t0:t0 + CH],
                                           eCn[:, (h * 2 + (t0 // CH + HCH) % 2)
                                               * CH:(h * 2 + (t0 // CH + HCH)
                                                     % 2 + 1) * CH]),
                                       wait=w, inc=[(cDVE, 1)])
                                m["qdec_w"][pr] = cDVE.n
                                w = [(cDVE, m["qdec_w"][pr]),
                                     (cDVE, m["kdec_w"][pr])]
                                if m["mask"] is not None:
                                    w.append((cDVE, m["mask"]))
                                S.emit("tensor",
                                       lambda eng, h=hh, i0=kdS, i1=qdS:
                                       eng.matmul(
                                           PA[0:CH, h * 512:h * 512 + CH],
                                           i0, i1, start=True, stop=True),
                                       wait=w, inc=[(cPE, 1)])
                                m["scores2"] = m["scores"]
                                m["scores"] = cPE.n
                                bb = pr * 64
                                S.emit("vector",
                                       lambda eng, h=hh: eng.tensor_mul(
                                           pm[0:CH, h * CH:(h + 1) * CH],
                                           PA[0:CH, h * 512:h * 512 + CH],
                                           maskT[:]),
                                       wait=[(cPE, m["scores"])],
                                       inc=[(cDVE, 1)])
                                m["mask"] = cDVE.n
                                if bb:
                                    S.emit("vector",
                                           lambda eng, h=hh: eng.tensor_copy(
                                               pm[64:64 + CH,
                                                  h * CH:(h + 1) * CH],
                                               pm[0:CH, h * CH:(h + 1) * CH]),
                                           wait=[(cDVE, m["mask"])],
                                           inc=[(cDVE, 1)])
                                    m["mask"] = cDVE.n
                                w = [(cDVE, m["mask"]), (cACT, v_evac_m)]
                                if m["oevac"] is not None:
                                    w.append((cACT, m["oevac"]))
                                S.emit("tensor",
                                       lambda eng, h=hh, cc=c, b=bb: eng.matmul(
                                           PA[:, h * 512 + CH:h * 512 + CH + CH],
                                           v3[b:b + 64,
                                              cc // 2, h * 128:(h + 1) * 128],
                                           pm[b:b + CH, h * CH:(h + 1) * CH],
                                           start=True, stop=False),
                                       wait=w)
                                w = [(cDVE, m["S_scale"])]
                                S.emit("tensor",
                                       lambda eng, h=hh, i1=qdS: eng.matmul(
                                           PA[:, h * 512 + CH:h * 512 + CH + CH],
                                           Sst[:, h * 128:(h + 1) * 128],
                                           i1, start=False, stop=True),
                                       wait=w, inc=[(cPE, 1)])
                                m["o_mm2"] = m["o_mm"]
                                m["o_mm"] = cPE.n
                            w = [(cDVE, m["kdecU_e"]), (cACT, v_evac_m)]
                            if m["S_add"] is not None:
                                w.append((cDVE, m["S_add"]))
                            S.emit("tensor",
                                   lambda eng, h=hh, cc=c, b=bbu: eng.matmul(
                                       PB[:, h * 512:h * 512 + 128],
                                       kdecU[b:b + 64, h * 128:(h + 1) * 128],
                                       v3[b:b + 64,
                                          cc // 2, h * 128:(h + 1) * 128],
                                       start=True, stop=True),
                                   wait=w, inc=[(cPE, 1)])
                            m["S_mm2"] = m["S_mm"]
                            m["S_mm"] = cPE.n
                            S.emit("vector",
                                   lambda eng, h=hh: eng.tensor_add(
                                       ssc[:, h * 128:(h + 1) * 128],
                                       PB[:, h * 512:h * 512 + 128],
                                       Sst[:, h * 128:(h + 1) * 128]),
                                   wait=[(cPE, m["S_mm"])], inc=[(cDVE, 1)])
                            m["S_add"] = cDVE.n
                            w = [(cDVE, m["S_add"]), (cACT, m["eCn_w"][pr])]
                            if m["o_mm"] is not None:
                                w.append((cPE, m["o_mm"]))
                            S.emit("vector",
                                   lambda eng, h=hh, p=pr: eng.tensor_scalar_mul(
                                       Sst[:, h * 128:(h + 1) * 128],
                                       ssc[:, h * 128:(h + 1) * 128],
                                       eCn[:, (h * 2 + p) * CH + CH - 1:
                                           (h * 2 + p) * CH + CH]),
                                   wait=w, inc=[(cDVE, 1)])
                            m["S_scale2"] = m["S_scale"]
                            m["S_scale"] = cDVE.n
                            if own:
                                S.emit("scalar",
                                       lambda eng, h=hh, gg=g, t0=(c - HCH) * CH:
                                       eng.copy(
                                           o3[:, 2 * gg + h, t0:t0 + CH],
                                           PA[:, h * 512 + CH:h * 512 + 2 * CH]),
                                       wait=[(cPE, m["o_mm"])], inc=[(cACT, 1)])
                                m["oevac"] = cACT.n
                    gl_pf[0] = pf_free
                    glast["f_r"] = (cACT, cACT.n)
                    glast["k_r"] = (cDVE, cDVE.n)
                    glast["q_r"] = (cDVE, cDVE.n)
                    glast["v_r"] = (cPE, cPE.n)
                    glast["nlf_r"] = (cDVE, cDVE.n)
                    for _ in range(12):
                        next(pcg, None)

                for _ in range(200):
                    if next(pcg, None) is None:
                        break
                ln_state["sums_free"] = (cACT, cACT.n)
                ln_state["pa_free"] = (cACT, cACT.n)
                gla_done = [(cACT, cACT.n), (cDVE, cDVE.n), (cPE, cPE.n)]
                rms_m = emit_norm(o3, xn3, T, rms=True, extra_wait=gla_done)

                pp_mark2 = [None, None]
                ppj = [0]
                for oc in range(NDT):
                    osl, ordy, oi = load_slab(aWoT, oc * 128, 128)
                    for sc in (0, 512):
                        i = ppj[0]
                        ppj[0] ^= 1
                        psb = pp[i]
                        for k in range(NDT):
                            w = [ordy, rms_m] if k == 0 else []
                            if k == 0 and pp_mark2[i] is not None:
                                w.append(pp_mark2[i])
                            if k == 0 and pp_mark[i] is not None:
                                w.append(pp_mark[i])
                                pp_mark[i] = None
                            S.emit("tensor",
                                   lambda eng, kk=k, o=oc, c=sc, pb=psb, os_=osl:
                                   eng.matmul(
                                       pb[:, 0:512],
                                       os_[:, kk, 0:128],
                                       xn3[:, kk, c:c + 512],
                                       start=(kk == 0), stop=(kk == NDT - 1)),
                                   wait=w, inc=[(cPE, 1)] if k == NDT - 1 else [])
                        pmk = (cPE, cPE.n)
                        S.emit("vector",
                               lambda eng, o=oc, c=sc, pb=psb: eng.tensor_add(
                                   x3[:, o, c:c + 512], x3[:, o, c:c + 512],
                                   pb[:, 0:512]),
                               wait=[pmk], inc=[(cDVE, 1)])
                        pp_mark2[i] = (cDVE, cDVE.n)
                    slab_done(oi, (cPE, cPE.n))

        ffn_ctr = [0]

        def emit_ffn():
            ffn_ctr[0] += 1
            sfx = str(ffn_ctr[0])
            with ExitStack() as ph:
                er = lambda nm, shape, dt: ph.enter_context(
                    nc.sbuf_tensor(nm + sfx, list(shape), dt, side="right"))
                gated = er("ffn_gated", [128, 64 * 512], BF16)
                dsl = er("ffn_dsl", [128, 2 * 8 * 256], BF16)
                gs = er("ffn_gs", [128, 512], BF16)
                g3 = gated[:].rearrange("p (h t) -> p h t", t=512)
                d4 = dsl[:].rearrange("p (i k c) -> p i k c", i=2, k=8)
                ln_m = emit_norm(x3, xn3, T,
                                 extra_wait=[(cPE, cPE.n), (cDVE, cDVE.n)])
                rot = [PC, PD, PE_]
                rot_mark = [None, None, None]
                gu_slots = [
                    (wsl3[0], 0), (wsl3[0], 128), (wsl3[1], 0), (wsl3[1], 128)]
                slot_mark = [None, None, None, None]
                slot_rdy = [None, None, None, None]
                dsl_mark = [None, None]
                pa_rot = [PA, PB]
                pa_mark = [None, None]
                gs_free = [None]
                ri = [0]
                for half in range(2):
                    hc0 = half * 512
                    for hto in range(64):
                        sg = (hto % 2) * 2
                        su = sg + 1
                        for slot, wd in ((sg, fwg_bf), (su, fwu_bf)):
                            sbuf3, c0s = gu_slots[slot]
                            w = [slot_mark[slot]] if slot_mark[slot] else []
                            if half == 0 and hto < 2:
                                w.append(pc_state["done"])
                            src = wd.ap().rearrange("(k p) c -> p k c", p=128)[
                                :, :, hto * 128:(hto + 1) * 128]
                            S.emit("sync",
                                   lambda eng, b3=sbuf3, cc=c0s, ss=src:
                                   eng.dma_start(out=b3[:, :, cc:cc + 128],
                                                 in_=ss),
                                   wait=w, inc=[(dW, 16)])
                            slot_rdy[slot] = (dW, dW.n)
                        jg = ri[0] % 3
                        ju = (ri[0] + 1) % 3
                        ri[0] += 2
                        marks = {}
                        for slot, j in ((sg, jg), (su, ju)):
                            sbuf3, c0s = gu_slots[slot]
                            pb = rot[j]
                            for k in range(NDT):
                                w = []
                                if k == 0:
                                    w.append(slot_rdy[slot])
                                    w.append(ln_m)
                                    if rot_mark[j] is not None:
                                        w.append(rot_mark[j])
                                S.emit("tensor",
                                       lambda eng, kk=k, b3=sbuf3, cc=c0s,
                                              pbb=pb, c=hc0: eng.matmul(
                                           pbb[:, 0:512],
                                           b3[:, kk, cc:cc + 128],
                                           xn3[:, kk, c:c + 512],
                                           start=(kk == 0),
                                           stop=(kk == NDT - 1)),
                                       wait=w,
                                       inc=[(cPE, 1)] if k == NDT - 1 else [])
                            marks[slot] = (cPE, cPE.n)
                            slot_mark[slot] = (cPE, cPE.n)
                        wsil = [marks[sg]]
                        if gs_free[0] is not None:
                            wsil.append(gs_free[0])
                        S.emit("scalar",
                               lambda eng, pbb=rot[jg]: eng.activation(
                                   gs[:], pbb[:, 0:512], AF.Silu),
                               wait=wsil, inc=[(cACT, 1)])
                        am = cACT.n
                        rot_mark[jg] = (cACT, am)
                        wmul = [(cACT, am), marks[su]]
                        if half == 1 and hto == 0:
                            wmul.append((cPE, cPE.n))
                            if DBG == "gated":
                                wmul.append(dbg_hold["gated_done"])
                        S.emit("vector",
                               lambda eng, h=hto, pbb=rot[ju]: eng.tensor_mul(
                                   g3[:, h, :], gs[:], pbb[:, 0:512]),
                               wait=wmul, inc=[(cDVE, 1)])
                        rot_mark[ju] = (cDVE, cDVE.n)
                        gs_free[0] = (cDVE, cDVE.n)
                        if hto == 63:
                            gated_m = (cDVE, cDVE.n)
                    if DBG == "gated" and half == 0:
                        og = out_d.rearrange("(g p) t -> p g t", p=128)
                        S.emit("gpsimd", lambda eng, o=og: eng.dma_start(
                            out=o[:, :, 0:512], in_=g3[:, 0:16, :]),
                            wait=[gated_m], inc=[(dG, 16)])
                        S.emit("gpsimd", lambda eng, o=og: eng.dma_start(
                            out=o[:, :, 512:1024], in_=g3[:, 16:32, :]),
                            wait=[gated_m], inc=[(dG, 16)])
                        dbg_hold["gated_done"] = (dG, dG.n)
                    for op_ in range(8):
                        pi = op_ % 2
                        pab = pa_rot[pi]
                        dn_marks = []
                        for kc in range(8):
                            di = kc % 2
                            w = [dsl_mark[di]] if dsl_mark[di] else []
                            src = fwd_bf.ap().rearrange(
                                "(k p) c -> p k c", p=128)[
                                :, kc * 8:(kc + 1) * 8,
                                op_ * 256:(op_ + 1) * 256]
                            S.emit("sync",
                                   lambda eng, dd=di, ss=src: eng.dma_start(
                                       out=d4[:, dd, :, :], in_=ss),
                                   wait=w, inc=[(dW, 16)])
                            drdy = (dW, dW.n)
                            for oc2 in range(2):
                                for kk in range(8):
                                    hid = kc * 8 + kk
                                    w = []
                                    if kk == 0 and oc2 == 0:
                                        w.append(drdy)
                                        w.append(gated_m)
                                    if hid == 0 and kk == 0 and oc2 == 0 \
                                            and pa_mark[pi] is not None:
                                        w.append(pa_mark[pi])
                                    lastk = (oc2 == 1 and kk == 7)
                                    S.emit("tensor",
                                           lambda eng, dd=di, k2=kk, o2=oc2,
                                                  hh=hid, pbb=pab: eng.matmul(
                                               pbb[:, o2 * 512:o2 * 512 + 512],
                                               d4[:, dd, k2, o2 * 128:
                                                  o2 * 128 + 128],
                                               g3[:, hh, :],
                                               start=(hh == 0),
                                               stop=(hh == 63)),
                                           wait=w,
                                           inc=[(cPE, 1)] if (lastk or hid == 63)
                                           else [])
                                    if hid == 63:
                                        dn_marks.append((cPE, cPE.n))
                            dsl_mark[di] = (cPE, cPE.n)
                        adds = []
                        for oc2 in range(2):
                            oc = op_ * 2 + oc2
                            S.emit("vector",
                                   lambda eng, o=oc, o2=oc2, c=hc0, pbb=pab:
                                   eng.tensor_add(
                                       x3[:, o, c:c + 512],
                                       x3[:, o, c:c + 512],
                                       pbb[:, o2 * 512:o2 * 512 + 512]),
                                   wait=[dn_marks[oc2]], inc=[(cDVE, 1)])
                            adds.append(cDVE.n)
                        pa_mark[pi] = (cDVE, adds[-1])

        def emit_attn_b():
            with ExitStack() as ph:
                er = lambda nm, shape, dt: ph.enter_context(
                    nc.sbuf_tensor(nm, list(shape), dt, side="right"))
                cosT = er("b_cosT", [128, T], F32)
                sinT = er("b_sinT", [128, T], F32)
                sgn = er("b_sgn", [128, 1], F32)
                QrT = er("b_QrT", [128, 16 * T], BF16)
                KTb = er("b_KTb", [128, 4096], BF16)
                vcb = er("b_vcb", [128, 32 * 192], BF16)
                pTb = er("b_pT", [128, 2 * 1024], BF16)
                qraw = er("b_qraw", [128, 1024], BF16)
                t1 = er("b_t1", [128, 1024], BF16)
                t2 = er("b_t2", [128, 1024], BF16)
                rs = er("b_rs", [128, 512], F32)
                fin = er("b_fin", [128, 512], BF16)
                u_sc = er("b_usc", [1, 64], F32)
                negU = er("b_negU", [128, 4], F32)
                kbr = er("b_kbr", [1, 32], BF16)

                qr3 = QrT[:].rearrange("p (h t) -> p h t", h=16)
                kt3 = KTb[:].rearrange("p (x t) -> p x t", x=4)
                vc3 = vcb[:].rearrange("p (x c) -> p x c", c=192)
                vc8 = vcb[:].rearrange("p (x c) -> p x c", c=768)
                pt3 = pTb[:].rearrange("p (i t) -> p i t", i=2)

                dma("sync", dIn, cosT[:], cosT_in)
                dma("sync", dIn, sinT[:], sinT_in)
                dma("sync", dIn, sgn[:], sgn_in)
                tbl_m = (dIn, dIn.n)

                ln_m = emit_norm(x3, xn3, T,
                                 extra_wait=[(cPE, cPE.n), (cDVE, cDVE.n)])

                pp = [PA, PB]
                pp_mark = [None, None]
                ppi = [0]

                def proj16(slab3, octile, wait0):
                    i = ppi[0]
                    ppi[0] ^= 1
                    psb = pp[i]
                    for k in range(NDT):
                        w = list(wait0) if k == 0 else []
                        if k == 0 and pp_mark[i] is not None:
                            w.append(pp_mark[i])
                        S.emit("tensor",
                               lambda eng, kk=k, s3=slab3, oc=octile, pb=psb:
                               eng.matmul(
                                   pb[:, 0:512],
                                   s3[:, kk, oc * 128:(oc + 1) * 128],
                                   xn3[:, kk, 0:512],
                                   start=(kk == 0), stop=False),
                               wait=w)
                        S.emit("tensor",
                               lambda eng, kk=k, s3=slab3, oc=octile, pb=psb:
                               eng.matmul(
                                   pb[:, 512:1024],
                                   s3[:, kk, oc * 128:(oc + 1) * 128],
                                   xn3[:, kk, 512:1024],
                                   start=(kk == 0), stop=(kk == NDT - 1)),
                               wait=[], inc=[(cPE, 1)] if k == NDT - 1 else [])
                    return psb, i, (cPE, cPE.n)

                rst = {"m": []}

                def emit_rope(psrc, pmk, dsts):
                    w = [pmk] + list(rst["m"])
                    S.emit("scalar", lambda eng, pb=psrc: eng.copy(
                        qraw[:], pb[:, 0:T]), wait=w, inc=[(cACT, 1)])
                    ev = cACT.n
                    S.emit("vector", lambda eng: eng.tensor_mul(
                        t1[:], qraw[:], cosT[:]),
                           wait=[(cACT, ev), tbl_m], inc=[(cDVE, 1)])
                    S.emit("vector", lambda eng: eng.tensor_mul(
                        t2[:], qraw[:], sinT[:]),
                           wait=[(cACT, ev)], inc=[(cDVE, 1)])
                    m2 = cDVE.n
                    for j in range(4):
                        sb0 = (j ^ 1) * 32
                        S.emit("vector",
                               lambda eng, jj=j, sq=sb0: eng.tensor_copy(
                                   qraw[jj * 32:jj * 32 + 32, :],
                                   t2[sq:sq + 32, :]),
                               wait=[(cDVE, m2)], inc=[(cDVE, 1)])
                    m3 = cDVE.n
                    S.emit("vector", lambda eng: eng.scalar_tensor_tensor(
                        t2[:], qraw[:], sgn[:, 0:1], t1[:],
                        ALU.mult, ALU.add),
                           wait=[(cDVE, m3)], inc=[(cDVE, 1)])
                    mr = cDVE.n
                    for hh in range(2):
                        S.emit("vector",
                               lambda eng, h=hh, ad=dsts[hh]: eng.tensor_copy(
                                   ad, t2[h * 64:h * 64 + 64, :]),
                               wait=[(cDVE, mr)], inc=[(cDVE, 1)])
                    rst["m"] = [(cDVE, cDVE.n)]
                    return (cDVE, cDVE.n)

                for ov in range(2):
                    ksl, krdy, ki = load_slab(bWkT, ov * 256, 256)
                    for oc2 in range(2):
                        ot = ov * 2 + oc2
                        psb, i, pmk = proj16(ksl, oc2, [krdy, ln_m])
                        mk = emit_rope(psb, pmk,
                                       [kt3[0:64, ot, :], kt3[64:128, ot, :]])
                        pp_mark[i] = mk
                    slab_done(ki, (cPE, cPE.n))
                kr_done = rst["m"][0]
                k2m = []
                for ot in range(4):
                    w = [kr_done] if ot == 0 else [(cDVE, k2m[-1])]
                    S.emit("scalar", lambda eng, o=ot: eng.activation(
                        t1[:], kt3[:, o, :], AF.Square),
                        wait=w, inc=[(cACT, 1)])
                    sqm = cACT.n
                    wpe = [(cACT, sqm)]
                    if ot > 0:
                        wpe.append((cDVE, k2m[-1]))
                    S.emit("tensor", lambda eng: eng.matmul(
                        PE_[0:1, 0:512], ones1[:], t1[:, 0:512],
                        start=True, stop=False), wait=wpe)
                    S.emit("tensor", lambda eng: eng.matmul(
                        PE_[0:1, 0:512], ones1[:], t1[:, 512:1024],
                        start=False, stop=True), inc=[(cPE, 1)])
                    pm2 = cPE.n
                    S.emit("vector", lambda eng, o=ot: eng.reduce_max(
                        u_sc[0:1, 16 + o:17 + o], PE_[0:1, 0:512],
                        axis=mybir.AxisListType.X),
                        wait=[(cPE, pm2)], inc=[(cDVE, 1)])
                    k2m.append(cDVE.n)
                S.emit("vector", lambda eng: eng.tensor_copy(
                    kbr[0:1, 0:4], u_sc[0:1, 16:20]),
                    wait=[(cDVE, k2m[-1])], inc=[(cDVE, 1)])
                kbr_m = (cDVE, cDVE.n)

                pf_free2 = None
                vtr_last = [None]
                vlast_sq = (cDVE, cDVE.n)
                for ov in range(2):
                    vsl, vrdy, vi = load_slab(bWvT, ov * 256, 256)
                    for oc2 in range(2):
                        ot = ov * 2 + oc2
                        psb, i, pmk = proj16(vsl, oc2, [vrdy, ln_m])
                        w = [pmk, kr_done, (cACT, cACT.n)]
                        if vtr_last[0] is not None:
                            w.append(vtr_last[0])
                        S.emit("scalar", lambda eng, pb=psb: eng.copy(
                            t2[:], pb[:, 0:T]), wait=w, inc=[(cACT, 1)])
                        evm = cACT.n
                        pp_mark[i] = (cACT, evm)
                        for tt in range(8):
                            w = [(cACT, evm), (dIn, in_done)]
                            if pf_free2 is not None:
                                w.append(pf_free2)
                            S.emit("tensor",
                                   lambda eng, t=tt: eng.transpose(
                                       PF[:, 0:128],
                                       t2[:, t * 128:(t + 1) * 128], idb[:]),
                                   wait=w, inc=[(cPE, 1)])
                            trm = cPE.n
                            vtr_last[0] = (cPE, trm)
                            S.emit("vector",
                                   lambda eng, t=tt, o=ot: eng.tensor_copy(
                                       vc8[:, t, o * 128:(o + 1) * 128],
                                       PF[:, 0:128]),
                                   wait=[(cPE, trm)], inc=[(cDVE, 1)])
                            pf_free2 = (cDVE, cDVE.n)
                    slab_done(vi, (cPE, cPE.n))
                v_done = (cDVE, cDVE.n)
                rst["m"] = [vtr_last[0], (cDVE, cDVE.n)]

                S.emit("sync", lambda eng: eng.dma_start(
                    out=cc_in_k.ap()[0:512, :].rearrange(
                        "(x p) t -> p x t", p=128),
                    in_=kt3[:, :, :]), wait=[kr_done], inc=[(dG, 16)])
                S.emit("sync", lambda eng: eng.dma_start(
                    out=cc_in_k.ap()[512:513, 0:4], in_=kbr[0:1, 0:4]),
                    wait=[kbr_m], inc=[(dG, 16)])
                S.emit("sync", lambda eng: eng.dma_start(
                    out=cc_in_v.ap().rearrange("(x p) c -> p x c", p=128),
                    in_=vc8[:, :, 0:512]),
                    wait=[v_done], inc=[(dG, 16)])
                gm = dG.n
                S.emit("gpsimd", lambda eng: eng.collective_compute(
                    "AllGather", ALU.bypass, replica_groups=RG,
                    ins=[cc_in_k.ap().opt()], outs=[cc_out_k.ap().opt()]),
                    wait=[(dG, gm)], inc=[(cCC, 1)])
                S.emit("gpsimd", lambda eng: eng.collective_compute(
                    "AllGather", ALU.bypass, replica_groups=RG,
                    ins=[cc_in_v.ap().opt()], outs=[cc_out_v.ap().opt()]),
                    inc=[(cCC, 1)])
                cc_m = (cCC, cCC.n)

                q2m = []
                for oq in range(8):
                    qsl, qrdy, qi = load_slab(bWqT, oq * 256, 256)
                    for oc2 in range(2):
                        ot = oq * 2 + oc2
                        dsts = []
                        for hh in range(2):
                            q = 2 * ot + hh
                            tile = (q // 8) * 4 + (q % 4)
                            base = ((q // 4) % 2) * 64
                            dsts.append(qr3[base:base + 64, tile, :])
                        psb, i, pmk = proj16(qsl, oc2, [qrdy, ln_m])
                        mk = emit_rope(psb, pmk, dsts)
                        pp_mark[i] = mk
                        S.emit("scalar", lambda eng: eng.activation(
                            t1[:], t2[:], AF.Square),
                            wait=[mk], inc=[(cACT, 1)])
                        sqm = cACT.n
                        wq2 = [(cACT, sqm)]
                        if q2m:
                            wq2.append((cDVE, q2m[-1]))
                        else:
                            wq2.append((cDVE, k2m[-1]))
                        S.emit("tensor", lambda eng: eng.matmul(
                            PE_[0:1, 0:512], ones1[:], t1[:, 0:512],
                            start=True, stop=False), wait=wq2)
                        S.emit("tensor", lambda eng: eng.matmul(
                            PE_[0:1, 0:512], ones1[:], t1[:, 512:1024],
                            start=False, stop=True), inc=[(cPE, 1)])
                        pm2 = cPE.n
                        S.emit("vector", lambda eng, o=ot: eng.reduce_max(
                            u_sc[0:1, o:o + 1], PE_[0:1, 0:512],
                            axis=mybir.AxisListType.X),
                            wait=[(cPE, pm2)], inc=[(cDVE, 1)])
                        q2m.append(cDVE.n)
                        rst["m"] = [(cACT, sqm), (cDVE, cDVE.n)]
                    slab_done(qi, (cPE, cPE.n))
                q_done = (cDVE, cDVE.n)

                S.emit("sync", lambda eng: eng.dma_start(
                    out=kbr[0:1, 0:16].rearrange("p (r c) -> p r c", r=4),
                    in_=cc_out_k.ap().rearrange("(r x) t -> x r t", r=4)[
                        512:513, :, 0:4]),
                    wait=[cc_m], inc=[(dG, 16)])
                kb_m = (dG, dG.n)
                S.emit("vector", lambda eng: eng.tensor_max(
                    u_sc[0:1, 52:56], kbr[0:1, 0:4], kbr[0:1, 4:8]),
                    wait=[kb_m, (cDVE, k2m[-1])], inc=[(cDVE, 1)])
                S.emit("vector", lambda eng: eng.tensor_max(
                    u_sc[0:1, 56:60], kbr[0:1, 8:12], kbr[0:1, 12:16]),
                    wait=[(cDVE, cDVE.n)], inc=[(cDVE, 1)])
                S.emit("vector", lambda eng: eng.tensor_max(
                    u_sc[0:1, 16:20], u_sc[0:1, 52:56], u_sc[0:1, 56:60]),
                    wait=[(cDVE, cDVE.n)], inc=[(cDVE, 1)])
                q2v = u_sc[0:1, 0:16].rearrange("p (u t) -> p u t", u=4)
                S.emit("vector", lambda eng: eng.tensor_max(
                    u_sc[0:1, 20:24], q2v[:, :, 0], q2v[:, :, 1]),
                    wait=[(cDVE, q2m[-1]), (cDVE, cDVE.n)], inc=[(cDVE, 1)])
                S.emit("vector", lambda eng: eng.tensor_max(
                    u_sc[0:1, 24:28], q2v[:, :, 2], q2v[:, :, 3]),
                    wait=[(cDVE, cDVE.n)], inc=[(cDVE, 1)])
                S.emit("vector", lambda eng: eng.tensor_max(
                    u_sc[0:1, 28:32], u_sc[0:1, 20:24], u_sc[0:1, 24:28]),
                    wait=[(cDVE, cDVE.n)], inc=[(cDVE, 1)])
                S.emit("scalar", lambda eng: eng.activation(
                    u_sc[0:1, 32:36], u_sc[0:1, 28:32], AF.Ln),
                    wait=[(cDVE, cDVE.n)], inc=[(cACT, 1)])
                S.emit("scalar", lambda eng: eng.activation(
                    u_sc[0:1, 36:40], u_sc[0:1, 16:20], AF.Ln),
                    wait=[(cACT, cACT.n)], inc=[(cACT, 1)])
                S.emit("vector", lambda eng: eng.tensor_add(
                    u_sc[0:1, 40:44], u_sc[0:1, 32:36], u_sc[0:1, 36:40]),
                    wait=[(cACT, cACT.n)], inc=[(cDVE, 1)])
                S.emit("scalar", lambda eng: eng.activation(
                    u_sc[0:1, 44:48], u_sc[0:1, 40:44], AF.Exp, scale=0.5),
                    wait=[(cDVE, cDVE.n)], inc=[(cACT, 1)])
                S.emit("vector", lambda eng: eng.tensor_scalar_mul(
                    u_sc[0:1, 48:52], u_sc[0:1, 44:48], -0.125),
                    wait=[(cACT, cACT.n)], inc=[(cDVE, 1)])
                nu_m = cDVE.n
                S.emit("tensor", lambda eng: eng.matmul(
                    PE_[:, 0:4], onesF[:], u_sc[0:1, 48:52],
                    start=True, stop=True),
                    wait=[(cDVE, nu_m)], inc=[(cPE, 1)])
                S.emit("scalar", lambda eng: eng.copy(negU[:], PE_[:, 0:4]),
                    wait=[(cPE, cPE.n)], inc=[(cACT, 1)])
                nu_done = (cACT, cACT.n)

                SP = [PA, PB]
                sp_mark = [(cACT, cACT.n), (cACT, cACT.n)]
                pt_mark = [None, None]
                oacc_mark = [None, None]
                kv_mark = None
                wo_marks = []
                wo_pp = [PC, PD]
                wo_mark = [None, None]
                woi = [0]
                for qt in range(2):
                    att_done = []
                    for u in range(4):
                        w = [cc_m, (cDVE, k2m[-1]), kbr_m]
                        if kv_mark is not None:
                            w.append(kv_mark)
                        for par in range(2):
                            kvh = 2 * u + par
                            S.emit("sync",
                                   lambda eng, v=kvh, b=par * 64: eng.dma_start(
                                       out=KTb[b:b + 64, :].rearrange(
                                           "p (r t) -> p r t", r=4),
                                       in_=cc_out_k.ap().rearrange(
                                           "(r x) t -> x r t", r=4)[
                                           v * 64:(v + 1) * 64, :, :]),
                                   wait=w if par == 0 else [],
                                   inc=[(dG, 16)])
                            S.emit("sync",
                                   lambda eng, v=kvh, c0=par * 128:
                                   eng.dma_start(
                                       out=vc3[:, :, c0:c0 + 64] if c0 == 0
                                       else vc3[:, :, 128:192],
                                       in_=cc_out_v.ap().rearrange(
                                           "(x p) c -> p x c", p=128)[
                                           :, :, v * 64:(v + 1) * 64]),
                                   wait=[], inc=[(dG, 16)])
                        kv_rdy = (dG, dG.n)
                        if qt == 0 and u == 0:
                            S.emit("vector", lambda eng: eng.memset(
                                vc3[:, :, 64:128], 1.0),
                                wait=[v_done, (dG, gm)], inc=[(cDVE, 1)])
                        ones_m = (cDVE, cDVE.n)
                        for sp_ in range(4):
                            qa = 8 * u + sp_
                            tile = u * 4 + sp_
                            for kt in range(32):
                                pj = kt % 2
                                spb = SP[pj]
                                w = [kv_rdy, ones_m, q_done, nu_done]
                                if sp_mark[pj] is not None:
                                    w.append(sp_mark[pj])
                                S.emit("tensor",
                                       lambda eng, k2=kt, tl=tile, pb=spb,
                                              c=qt * 512: eng.matmul(
                                           pb[:, 0:512],
                                           KTb[0:64, k2 * 128:(k2 + 1) * 128],
                                           qr3[0:64, tl, c:c + 512],
                                           start=True, stop=True),
                                       wait=w)
                                S.emit("tensor",
                                       lambda eng, k2=kt, tl=tile, pb=spb,
                                              c=qt * 512: eng.matmul(
                                           pb[:, 512:1024],
                                           KTb[64:128, k2 * 128:(k2 + 1) * 128],
                                           qr3[64:128, tl, c:c + 512],
                                           start=True, stop=True),
                                       wait=[], inc=[(cPE, 1)])
                                mm1_m = cPE.n
                                w = [(cPE, mm1_m)]
                                if pt_mark[pj] is not None:
                                    w.append(pt_mark[pj])
                                S.emit("scalar",
                                       lambda eng, pb=spb, pj2=pj, uu=u:
                                       eng.activation(
                                           pt3[:, pj2, :], pb[:, 0:1024],
                                           AF.Exp, scale=0.125,
                                           bias=negU[:, uu:uu + 1]),
                                       wait=w, inc=[(cACT, 1)])
                                exp_m = cACT.n
                                sp_mark[pj] = (cACT, exp_m)
                                w = [(cACT, exp_m)]
                                if kt == 0 and oacc_mark[0] is not None:
                                    w += [oacc_mark[0], oacc_mark[1]]
                                S.emit("tensor",
                                       lambda eng, k2=kt, pj2=pj: eng.matmul(
                                           PC[:, 0:512],
                                           vc3[:, k2, 0:128],
                                           pt3[:, pj2, 0:512],
                                           start=(k2 == 0), stop=(k2 == 31)),
                                       wait=w)
                                S.emit("tensor",
                                       lambda eng, k2=kt, pj2=pj: eng.matmul(
                                           PD[:, 0:512],
                                           vc3[:, k2, 64:192],
                                           pt3[:, pj2, 512:1024],
                                           start=(k2 == 0), stop=(k2 == 31)),
                                       wait=[],
                                       inc=[(cPE, 1)] if kt == 31 else [])
                                pt_mark[pj] = (cPE, cPE.n)
                            mm2_m = (cPE, cPE.n)
                            for hh, (pb, orow, srow) in enumerate(
                                    ((PC, 0, 64), (PD, 64, 0))):
                                q = qa + 4 * hh
                                S.emit("vector",
                                       lambda eng, p2=pb, orw=orow, sr=srow:
                                       eng.reciprocal(
                                           rs[orw:orw + 64, :],
                                           p2[sr:sr + 64, 0:512]),
                                       wait=[mm2_m], inc=[(cDVE, 1)])
                                rm = cDVE.n
                                S.emit("vector",
                                       lambda eng, p2=pb, orw=orow:
                                       eng.tensor_mul(
                                           fin[orw:orw + 64, :],
                                           p2[orw:orw + 64, 0:512],
                                           rs[orw:orw + 64, :]),
                                       wait=[(cDVE, rm)], inc=[(cDVE, 1)])
                                fm = cDVE.n
                                nt, nb = q // 2, (q % 2) * 64
                                S.emit("vector",
                                       lambda eng, t3=nt, b3=nb, orw=orow,
                                              c=qt * 512: eng.tensor_copy(
                                           xn3[b3:b3 + 64, t3, c:c + 512],
                                           fin[orw:orw + 64, :]),
                                       wait=[(cDVE, fm)], inc=[(cDVE, 1)])
                                att_done.append(cDVE.n)
                            oacc_mark[0] = (cDVE, cDVE.n)
                            oacc_mark[1] = (cDVE, cDVE.n)
                        kv_mark = (cPE, cPE.n)
                    att_m = (cDVE, att_done[-1])
                    for oc in range(NDT):
                        osl, ordy, oi = load_slab(bWoT, oc * 128, 128)
                        i = woi[0]
                        woi[0] ^= 1
                        psb = wo_pp[i]
                        for k in range(NDT):
                            w = [ordy, att_m] if k == 0 else []
                            if k == 0 and wo_mark[i] is not None:
                                w.append(wo_mark[i])
                            S.emit("tensor",
                                   lambda eng, kk=k, pb=psb, os_=osl,
                                          c=qt * 512: eng.matmul(
                                       pb[:, 0:512], os_[:, kk, 0:128],
                                       xn3[:, kk, c:c + 512],
                                       start=(kk == 0), stop=(kk == NDT - 1)),
                                   wait=w, inc=[(cPE, 1)] if k == NDT - 1 else [])
                        pmk = (cPE, cPE.n)
                        S.emit("vector",
                               lambda eng, o=oc, pb=psb, c=qt * 512:
                               eng.tensor_add(
                                   x3[:, o, c:c + 512], x3[:, o, c:c + 512],
                                   pb[:, 0:512]),
                               wait=[pmk], inc=[(cDVE, 1)])
                        wo_mark[i] = (cDVE, cDVE.n)
                        slab_done(oi, pmk)
                        wo_marks.append(cDVE.n)

        if NSUB == -1:
            barrier()
            emit_ffn()
        if NSUB >= 1:
            barrier()
            emit_attn_a()
        if NSUB >= 2:
            barrier()
            emit_ffn()
        if NSUB >= 3:
            barrier()
            emit_attn_b()
        if NSUB >= 4:
            barrier()
            emit_ffn()

        if DBG == "attnT":
            S.emit("gpsimd", lambda eng: eng.dma_start(
                out=out_d.rearrange("(d p) t -> p d t", p=128), in_=xn3),
                wait=[(c, c.n) for c in allch], inc=[(dOut, 16)])
            S.emit("sync", lambda eng: None, wait=[(dOut, dOut.n)])
        elif DBG == "gated":
            S.emit("sync", lambda eng: None,
                   wait=[(c, c.n) for c in allch] + [(dG, dG.n)])
        elif DBG in ("xn", "o", "q", "k", "nlf", "v"):
            src_dbg = {"xn": xn3}.get(DBG)
            if src_dbg is None:
                src_dbg = dbg_hold[DBG]
            S.emit("gpsimd", lambda eng, sd=src_dbg: eng.dma_start(
                out=out_d.rearrange("(d p) t -> p d t", p=128), in_=sd),
                wait=[(c, c.n) for c in allch], inc=[(dOut, 16)])
        else:
            S.emit("sync", lambda eng: eng.dma_start(
                out=out_d.rearrange("(d p) t -> p d t", p=128), in_=x3),
                wait=[(c, c.n) for c in allch], inc=[(dOut, 16)])
        S.emit("sync", lambda eng: None, wait=[(dOut, dOut.n)])

        with nc.Block() as block:
            @block.sync
            def _(eng):
                for f in S.q["sync"]:
                    f(eng)

            @block.gpsimd
            def _(eng):
                for f in S.q["gpsimd"]:
                    f(eng)

            @block.vector
            def _(eng):
                for f in S.q["vector"]:
                    f(eng)

            @block.scalar
            def _(eng):
                for f in S.q["scalar"]:
                    f(eng)

            @block.tensor
            def _(eng):
                for f in S.q["tensor"]:
                    f(eng)

    return nc


def _host_inputs(L0_attn_a_x, weights):
    x = np.asarray(L0_attn_a_x)
    s, t = np.meshgrid(np.arange(CH), np.arange(CH), indexing="ij")
    maskT = (t >= s).astype(BFNP)
    idb = np.eye(128, dtype=BFNP)
    ones1 = np.ones((128, 1), dtype=BFNP)
    onesF = np.ones((1, 128), dtype=np.float32)
    hsum2 = np.zeros((128, 2), dtype=BFNP)
    hsum2[0:64, 0] = 1
    hsum2[64:128, 1] = 1
    inv_freq = 1.0 / ROPE_THETA ** (np.arange(0, AD, 2, dtype=np.float32) / AD)

    in_maps = []
    for c in range(NCORES):
        b, q = c // 4, c % 4
        t0 = q * T
        own = x[b, t0:t0 + T]
        if t0 == 0:
            halo = np.zeros((HALO, D), dtype=np.float32)
        else:
            halo = x[b, t0 - HALO:t0]
        pos = np.arange(t0, t0 + T, dtype=np.float32)
        ang = np.outer(inv_freq, pos)
        cosT = np.tile(np.concatenate([np.cos(ang), np.cos(ang)], 0), (2, 1))
        sinT = np.tile(np.concatenate([np.sin(ang), np.sin(ang)], 0), (2, 1))
        m = {
            "xT": np.ascontiguousarray(own.T),
            "xTh": np.ascontiguousarray(halo.T),
            "maskT": maskT, "idb": idb, "ones1": ones1, "onesF": onesF,
            "hsum2": hsum2,
            "cosT": np.ascontiguousarray(cosT, dtype=np.float32),
            "sinT": np.ascontiguousarray(sinT, dtype=np.float32),
            "fconst": np.array([[EPS_LN, EPS_RMS, 0.0, 0.0]], dtype=np.float32),
            "sgn": np.ascontiguousarray(
                np.where((np.arange(128) % 64) < 32, -1.0, 1.0)
                .astype(np.float32).reshape(128, 1)),
        }
        m.update(weights)
        in_maps.append(m)
    return in_maps


def kernel(L0_attn_a_x, L0_attn_a_Wq, L0_attn_a_Wk, L0_attn_a_Wv, L0_attn_a_Wo,
           L0_attn_a_norm_w, L0_attn_b_Wq, L0_attn_b_Wk, L0_attn_b_Wv,
           L0_attn_b_Wo, L0_ffn_w_gate, L0_ffn_w_up, L0_ffn_w_down):
    tr = lambda w: np.ascontiguousarray(np.asarray(w).T)
    weights = {
        "aWqT": tr(L0_attn_a_Wq), "aWkT": tr(L0_attn_a_Wk),
        "aWvT": tr(L0_attn_a_Wv), "aWoT": tr(L0_attn_a_Wo),
        "anw": np.ascontiguousarray(
            np.asarray(L0_attn_a_norm_w).reshape(NDT, 128).T),
        "bWqT": tr(L0_attn_b_Wq), "bWkT": tr(L0_attn_b_Wk),
        "bWvT": tr(L0_attn_b_Wv), "bWoT": tr(L0_attn_b_Wo),
        "fWgT": tr(L0_ffn_w_gate), "fWuT": tr(L0_ffn_w_up),
        "fWdT": tr(L0_ffn_w_down),
    }
    in_maps = _host_inputs(L0_attn_a_x, weights)
    nc = build_program()
    res = run_bass_kernel_spmd(nc, in_maps, core_ids=list(range(NCORES)))
    out = np.empty((B, N, D), dtype=np.float32)
    for c in range(NCORES):
        b, q = c // 4, c % 4
        out[b, q * T:(q + 1) * T] = res.results[c]["oT"].T
    return out


# version 03
# speedup vs baseline: 1.0210x; 1.0210x over previous
import numpy as np
import ml_dtypes
from contextlib import ExitStack

import concourse.bass as bass
import concourse.mybir as mybir
from concourse.bass_utils import run_bass_kernel_spmd

F32 = mybir.dt.float32
BF16 = mybir.dt.bfloat16
AF = mybir.ActivationFunctionType
ALU = mybir.AluOpType
BFNP = ml_dtypes.bfloat16

B, N, D = 2, 4096, 2048
NCORES = 8
T = 1024
HALO = 128
TH = T + HALO
NDT = 16
CH = 64
GLA_H = 16
NCH = TH // CH
HCH = HALO // CH
FFN_H = 8192
AH, AKVH, AD = 32, 8, 64
EPS_LN = 1e-5
EPS_RMS = 1e-6
ROPE_THETA = 10000.0

ENGINES = ("sync", "gpsimd", "vector", "scalar", "tensor")

NSUB = 4
DBG = None


class Ch:
    def __init__(self, handle):
        self.h = handle
        self.n = 0

    def bump(self, k):
        self.n += k
        return self.n


class Sched:
    def __init__(self):
        self.q = {e: [] for e in ENGINES}

    def emit(self, eng_name, fn, wait=(), inc=()):
        inc_t = list(inc)
        for c, k in inc_t:
            c.bump(k)
        wait_t = [(c, v) for (c, v) in wait if v is not None]

        def closure(eng):
            for c, v in wait_t:
                eng.wait_ge(c.h, v)
            inst = fn(eng)
            for c, k in inc_t:
                assert inst is not None
                inst.then_inc(c.h, k)

        self.q[eng_name].append(closure)


def build_program():
    nc = bass.Bass("TRN2", target_bir_lowering=False)

    def din(name, shape, dtype=F32):
        return nc.dram_tensor(name, list(shape), dtype, kind="ExternalInput").ap()

    xT_in = din("xT", [D, T])
    xTh_in = din("xTh", [D, HALO])
    aWqT = din("aWqT", [D, D]); aWkT = din("aWkT", [D, D])
    aWvT = din("aWvT", [D, D]); aWoT = din("aWoT", [D, D])
    anw = din("anw", [128, NDT])
    bWqT = din("bWqT", [D, D]); bWkT = din("bWkT", [D, 512])
    bWvT = din("bWvT", [D, 512]); bWoT = din("bWoT", [D, D])
    fWgT = din("fWgT", [D, FFN_H]); fWuT = din("fWuT", [D, FFN_H])
    fWdT = din("fWdT", [FFN_H, D])
    maskT_in = din("maskT", [CH, CH], BF16)
    idb_in = din("idb", [128, 128], BF16)
    ones1_in = din("ones1", [128, 1], BF16)
    onesF_in = din("onesF", [1, 128], F32)
    hsum2_in = din("hsum2", [128, 2], BF16)
    cosT_in = din("cosT", [128, T], F32)
    sinT_in = din("sinT", [128, T], F32)
    fconst_in = din("fconst", [1, 4], F32)
    sgn_in = din("sgn", [128, 1], F32)

    out_d = nc.dram_tensor("oT", [D, T], F32, kind="ExternalOutput").ap()

    fwg_bf = nc.dram_tensor("fwg_bf", [D, FFN_H], BF16, kind="Internal")
    fwu_bf = nc.dram_tensor("fwu_bf", [D, FFN_H], BF16, kind="Internal")
    fwd_bf = nc.dram_tensor("fwd_bf", [FFN_H, D], BF16, kind="Internal")
    cc_in_k = nc.dram_tensor("cc_in_k", [513, T], BF16, kind="Internal")
    cc_out_k = nc.dram_tensor("cc_out_k", [4 * 513, T], BF16, kind="Internal")
    cc_in_v = nc.dram_tensor("cc_in_v", [T, 512], BF16, kind="Internal")
    cc_out_v = nc.dram_tensor("cc_out_v", [4 * T, 512], BF16, kind="Internal")
    RG = [[0, 1, 2, 3], [4, 5, 6, 7]]

    S = Sched()

    with ExitStack() as stack:
        e = stack.enter_context

        chans = {}
        for name in ("dIn", "dW", "dOut", "dG", "pe", "act", "dve", "gp", "cc"):
            chans[name] = Ch(e(nc.semaphore(name)))
        dIn, dW, dOut, dG = chans["dIn"], chans["dW"], chans["dOut"], chans["dG"]
        cPE, cACT, cDVE, cGP = chans["pe"], chans["act"], chans["dve"], chans["gp"]
        cCC = chans["cc"]

        sb = lambda name, shape, dt: e(nc.sbuf_tensor(name, list(shape), dt))
        xT = sb("xT_sb", [128, NDT * T], F32)
        xn = sb("xn_sb", [128, NDT * T], BF16)
        xnh = sb("xnh_sb", [128, NDT * HALO], BF16)
        maskT = sb("maskT_sb", [CH, CH], BF16)
        idb = sb("idb_sb", [128, 128], BF16)
        ones1 = sb("ones1_sb", [128, 1], BF16)
        onesF = sb("onesF_sb", [1, 128], F32)
        hsum2 = sb("hsum2_sb", [128, 2], BF16)
        fconst = sb("fconst_sb", [1, 4], F32)
        anw_sb = sb("anw_sb", [128, NDT], F32)
        zer64 = sb("zer64_sb", [128, CH], F32)
        sqs = [sb(f"lnsq{i}", [128, 512], BF16) for i in range(2)]
        stS = sb("ln_stS", [1, 1536], F32)
        nsc = sb("ln_nsc", [128, 512], F32)
        wslab = [sb(f"wslab{i}", [128, 16 * 256], BF16) for i in range(2)]

        ps = lambda name, shape, dt=F32: e(nc.psum_tensor(name, list(shape), dt))
        PA = ps("PA", [128, 1024])
        PB = ps("PB", [128, 1024])
        PC = ps("PC", [128, 512])
        PD = ps("PD", [128, 512])
        PE_ = ps("PE", [128, 512])
        PF = ps("PF", [128, 1024], BF16)

        def dma(engn, dch, out_ap, in_ap, extra_wait=()):
            S.emit(engn, lambda eng, o=out_ap, i=in_ap: eng.dma_start(out=o, in_=i),
                   wait=extra_wait, inc=[(dch, 16)])

        x3 = xT[:].rearrange("p (d t) -> p d t", d=NDT)
        xn3 = xn[:].rearrange("p (d t) -> p d t", d=NDT)
        xnh3 = xnh[:].rearrange("p (d t) -> p d t", d=NDT)

        allch = list(chans.values())

        def barrier():
            marks = [(c, c.n) for c in allch]
            for en in ENGINES:
                S.emit(en, lambda eng: None, wait=marks)

        ln_state = {"sums_free": None, "pa_free": None,
                    "sq_guard": [None, None], "nsc_free": None,
                    "stat_free": []}

        def emit_norm(src3, dst3, ntok, rms=False, extra_wait=()):
            first = True
            c0 = 0
            stm = stS[0:1, 0:512]
            ste = stS[0:1, 512:1024]
            stt = stS[0:1, 1024:1536]
            while c0 < ntok:
                n = min(512, ntok - c0)
                for dt in range(NDT):
                    if not rms:
                        w = list(extra_wait) if (first and dt == 0) else []
                        S.emit("vector",
                               lambda eng, d=dt, c=c0, nn=n: eng.tensor_copy(
                                   dst3[:, d, c:c + nn], src3[:, d, c:c + nn]),
                               wait=w, inc=[(cDVE, 1)])
                        cast_m = cDVE.n
                        sq_src3 = dst3
                    else:
                        cast_m = None
                        sq_src3 = src3
                    w = ([] if cast_m is None else [(cDVE, cast_m)])
                    if first and dt == 0:
                        w += list(extra_wait) if rms else []
                    if ln_state["sq_guard"][dt % 2] is not None:
                        w.append((cPE, ln_state["sq_guard"][dt % 2]))
                    S.emit("scalar",
                           lambda eng, d=dt, c=c0, nn=n, s3=sq_src3:
                           eng.activation(sqs[d % 2][:, 0:nn], s3[:, d, c:c + nn],
                                          AF.Square),
                           wait=w, inc=[(cACT, 1)])
                    sq_m = cACT.n
                    if not rms:
                        w = [(cDVE, cast_m)]
                        if dt == 0 and ln_state["sums_free"] is not None:
                            w.append(ln_state["sums_free"])
                        S.emit("tensor",
                               lambda eng, d=dt, c=c0, nn=n: eng.matmul(
                                   PC[0:1, 0:nn], ones1[:],
                                   dst3[:, d, c:c + nn],
                                   start=(d == 0), stop=(d == NDT - 1)),
                               wait=w, inc=[(cPE, 1)] if dt == NDT - 1 else [])
                        if dt == NDT - 1:
                            m_sx = cPE.n
                    w = [(cACT, sq_m)]
                    if dt == 0 and ln_state["sums_free"] is not None:
                        w.append(ln_state["sums_free"])
                    S.emit("tensor",
                           lambda eng, d=dt, nn=n: eng.matmul(
                               PD[0:1, 0:nn], ones1[:], sqs[d % 2][:, 0:nn],
                               start=(d == 0), stop=(d == NDT - 1)),
                           wait=w, inc=[(cPE, 1)])
                    ln_state["sq_guard"][dt % 2] = cPE.n
                m_sx2 = cPE.n
                first = False
                S.emit("scalar", lambda eng, nn=n: eng.mul(
                    ste[:, 0:nn], PD[0:1, 0:nn], 1.0 / D),
                       wait=[(cPE, m_sx2)] + ln_state["stat_free"],
                       inc=[(cACT, 1)])
                a2 = cACT.n
                if not rms:
                    S.emit("scalar", lambda eng, nn=n: eng.mul(
                        stm[:, 0:nn], PC[0:1, 0:nn], 1.0 / D),
                           wait=[(cPE, m_sx)] + ln_state["stat_free"],
                           inc=[(cACT, 1)])
                    a1 = cACT.n
                    ln_state["sums_free"] = (cACT, a1)
                    S.emit("vector", lambda eng, nn=n: eng.tensor_mul(
                        stt[:, 0:nn], stm[:, 0:nn], stm[:, 0:nn]),
                           wait=[(cACT, a1)], inc=[(cDVE, 1)])
                    v1 = cDVE.n
                    S.emit("vector", lambda eng, nn=n: eng.tensor_sub(
                        ste[:, 0:nn], ste[:, 0:nn], stt[:, 0:nn]),
                           wait=[(cACT, a2), (cDVE, v1)], inc=[(cDVE, 1)])
                    v2 = cDVE.n
                    eps_i = 0
                    var_ap = ste
                    wln = [(cDVE, v2)]
                else:
                    ln_state["sums_free"] = (cACT, a2)
                    eps_i = 1
                    var_ap = ste
                    wln = [(cACT, a2)]
                S.emit("scalar", lambda eng, nn=n, va=var_ap, ei=eps_i:
                       eng.activation(stt[:, 0:nn], va[:, 0:nn], AF.Ln,
                                      bias=fconst[0:1, ei:ei + 1]),
                       wait=wln, inc=[(cACT, 1)])
                a3 = cACT.n
                S.emit("scalar", lambda eng, nn=n: eng.activation(
                    ste[:, 0:nn], stt[:, 0:nn], AF.Exp, scale=-0.5),
                       wait=[(cACT, a3)], inc=[(cACT, 1)])
                a4 = cACT.n
                if not rms:
                    S.emit("vector", lambda eng, nn=n: eng.scalar_tensor_tensor(
                        stt[:, 0:nn], stm[:, 0:nn], -1.0,
                        ste[:, 0:nn], ALU.mult, ALU.mult),
                           wait=[(cACT, a4)], inc=[(cDVE, 1)])
                    v3 = cDVE.n
                w = [(cACT, a4)]
                if ln_state["pa_free"] is not None:
                    w.append(ln_state["pa_free"])
                S.emit("tensor", lambda eng, nn=n: eng.matmul(
                    PA[:, 0:nn], onesF[:], ste[:, 0:nn], start=True, stop=True),
                       wait=w, inc=[(cPE, 1)])
                p1 = cPE.n
                if not rms:
                    S.emit("tensor", lambda eng, nn=n: eng.matmul(
                        PA[:, 512:512 + nn], onesF[:], stt[:, 0:nn],
                        start=True, stop=True),
                           wait=[(cDVE, v3)], inc=[(cPE, 1)])
                    p2 = cPE.n
                ln_state["stat_free"] = [(cPE, p1)] + (
                    [(cPE, p2), (cDVE, v3)] if not rms else [])
                for dt in range(NDT):
                    w = [(cPE, p1)] if dt == 0 else []
                    if ln_state["nsc_free"] is not None:
                        w.append((cDVE, ln_state["nsc_free"]))
                    S.emit("vector",
                           lambda eng, d=dt, c=c0, nn=n: eng.tensor_mul(
                               nsc[:, 0:nn], src3[:, d, c:c + nn], PA[:, 0:nn]),
                           wait=w, inc=[(cDVE, 1)])
                    vm = cDVE.n
                    if not rms:
                        w2 = [(cDVE, vm)] + ([(cPE, p2)] if dt == 0 else [])
                        S.emit("vector",
                               lambda eng, d=dt, c=c0, nn=n: eng.tensor_add(
                                   dst3[:, d, c:c + nn], nsc[:, 0:nn],
                                   PA[:, 512:512 + nn]),
                               wait=w2, inc=[(cDVE, 1)])
                    else:
                        S.emit("vector",
                               lambda eng, d=dt, c=c0, nn=n: eng.tensor_scalar_mul(
                                   dst3[:, d, c:c + nn], nsc[:, 0:nn],
                                   anw_sb[:, d:d + 1]),
                               wait=[(cDVE, vm)], inc=[(cDVE, 1)])
                    ln_state["nsc_free"] = cDVE.n
                ln_state["pa_free"] = (cDVE, cDVE.n)
                c0 += n
            return (cDVE, cDVE.n)

        slab_st = {"mark": [None, None], "i": 0}
        wsl3 = [w[:].rearrange("p (k c) -> p k c", c=256) for w in wslab]

        def load_slab(wdram, col0, ncols, kdim=16, kstep=1):
            i = slab_st["i"]
            slab_st["i"] ^= 1
            w = []
            if slab_st["mark"][i] is not None:
                w.append(slab_st["mark"][i])
            src = wdram.rearrange("(k p) c -> p k c", p=128)[:, 0:kdim, col0:col0 + ncols]
            b = wsl3[i]
            S.emit("gpsimd",
                   lambda eng, bb=b, ss=src, kd=kdim, nn=ncols: eng.dma_start(
                       out=bb[:, 0:kd, 0:nn], in_=ss),
                   wait=w, inc=[(dW, 16)])
            return b, (dW, dW.n), i

        def slab_done(i, mark):
            slab_st["mark"][i] = mark

        pc_state = {"done": None}

        def precast_chunks():
            jobs = []
            for wsrc, wdst in ((fWgT, fwg_bf), (fWuT, fwu_bf)):
                for c0 in range(0, FFN_H, 256):
                    jobs.append((wsrc, wdst, 0, c0))
            for kq in range(4):
                for c0 in range(0, D, 256):
                    jobs.append((fWdT, fwd_bf, kq * 16, c0))
            for wsrc, wdst, k0, c0 in jobs:
                b, rdy, i = load_slab(
                    wsrc.rearrange("(a p) c -> a p c", p=128)[
                        k0:k0 + 16].rearrange("a p c -> (a p) c"), c0, 256)
                S.emit("sync",
                       lambda eng, bb=b, wd=wdst, kk=k0, cc=c0: eng.dma_start(
                           out=wd.ap().rearrange("(k p) c -> p k c", p=128)[
                               :, kk:kk + 16, cc:cc + 256],
                           in_=bb[:, 0:16, 0:256]),
                       wait=[rdy], inc=[(dG, 16)])
                slab_done(i, (dG, dG.n))
                yield
            pc_state["done"] = (dG, dG.n)

        with ExitStack() as ph1:
            xTh = ph1.enter_context(
                nc.sbuf_tensor("xTh_sb", [128, NDT * HALO], F32, side="right"))
            xh3 = xTh[:].rearrange("p (d t) -> p d t", d=NDT)
            dma("sync", dIn, x3, xT_in.rearrange("(d p) t -> p d t", p=128))
            dma("sync", dIn, xh3, xTh_in.rearrange("(d p) t -> p d t", p=128))
            for dst, src in ((maskT, maskT_in), (idb, idb_in), (ones1, ones1_in),
                             (onesF, onesF_in), (hsum2, hsum2_in),
                             (anw_sb, anw), (fconst, fconst_in)):
                dma("sync", dIn, dst[:], src)
            S.emit("vector", lambda eng: eng.memset(zer64[:], 0.0), inc=[(cDVE, 1)])
            in_done = dIn.n

            ln1_own = emit_ln_marker = emit_norm(
                x3, xn3, T, extra_wait=[(dIn, in_done)])
            ln1_halo = emit_norm(xh3, xnh3, HALO)

        dbg_hold = {}

        def emit_attn_a():
            with ExitStack() as ph:
                er = lambda nm, shape, dt: ph.enter_context(
                    nc.sbuf_tensor(nm, list(shape), dt, side="right"))
                oTt = er("gla_oT", [128, GLA_H * T], BF16)
                fT = er("gla_f", [128, 2 * TH], F32)
                nlfT = er("gla_nlf", [128, 2 * TH], F32)
                usc = er("gla_usc", [128, 288], F32)
                kT = er("gla_kT", [128, 2 * TH], BF16)
                qT = er("gla_qT", [128, 2 * T], BF16)
                vU = er("gla_vU", [128, 9 * 256], BF16)
                Sst = er("gla_S", [128, 2 * 128], BF16)
                ct = er("gla_ct", [128, 4 * CH], F32)
                eCp = er("gla_eCp", [128, 4 * CH], F32)
                eCn = er("gla_eCn", [128, 4 * CH], F32)
                qdec = er("gla_qdec", [128, 4 * CH], BF16)
                kdec = er("gla_kdec", [128, 4 * CH], BF16)
                kdecU = er("gla_kdecU", [128, 2 * 128], BF16)
                pm = er("gla_pm", [128, 2 * CH], BF16)
                qsg = er("gla_qsg", [128, 512], BF16)
                ssc = er("gla_ssc", [128, 256], BF16)
                o3 = oTt[:].rearrange("p (h t) -> p h t", h=GLA_H)
                dbg_hold["o"] = o3
                f3 = fT[:].rearrange("p (h t) -> p h t", h=2)
                nlf3 = nlfT[:].rearrange("p (h t) -> p h t", h=2)
                k3 = kT[:].rearrange("p (h t) -> p h t", h=2)
                q3 = qT[:].rearrange("p (h t) -> p h t", h=2)
                v3 = vU[:].rearrange("p (tt e) -> p tt e", e=256)

                segs = [(xnh3, 0, HALO, 0), (xn3, 0, 512, HALO),
                        (xn3, 512, 512, HALO + 512)]
                pp = [PC, PD]
                pp_mark = [None, None]
                ppi = [0]

                def proj_accum(slab3, octile, seg, extra_w=()):
                    srcT, sc0, sn, _ = seg
                    i = ppi[0]
                    ppi[0] ^= 1
                    psb = pp[i]
                    for k in range(NDT):
                        w = list(extra_w) if k == 0 else []
                        if k == 0 and pp_mark[i] is not None:
                            w.append(pp_mark[i])
                        S.emit("tensor",
                               lambda eng, kk=k, s3=slab3, oc=octile, st=srcT,
                                      c=sc0, nn=sn, pb=psb: eng.matmul(
                                   pb[:, 0:nn], s3[:, kk, oc * 128:(oc + 1) * 128],
                                   st[:, kk, c:c + nn],
                                   start=(kk == 0), stop=(kk == NDT - 1)),
                               wait=w, inc=[(cPE, 1)] if k == NDT - 1 else [])
                    return psb, i, (cPE, cPE.n)

                hm = [dict(), dict()]
                gl_pf = [None]
                glast = {"nlf_r": None, "k_r": None, "q_r": None, "v_r": None,
                         "f_r": None}

                pcg = precast_chunks()
                for g in range(8):
                    zsl, zrdy, zi = load_slab(aWkT, g * 256, 256)
                    qsl, qrdy, qi = load_slab(aWqT, g * 256, 256)
                    f_marks = []
                    for oc in range(2):
                        for seg in segs:
                            psb, i, pmk = proj_accum(zsl, oc, seg,
                                                     extra_w=[zrdy, ln1_own,
                                                              ln1_halo])
                            toff = seg[3]
                            w = [pmk]
                            if glast["f_r"] is not None:
                                w.append(glast["f_r"])
                            S.emit("scalar",
                                   lambda eng, o=oc, to=toff, nn=seg[2], pb=psb:
                                   eng.activation(f3[:, o, to:to + nn],
                                                  pb[:, 0:nn], AF.Sigmoid),
                                   wait=w, inc=[(cACT, 1)])
                            f_marks.append(cACT.n)
                            pp_mark[i] = (cACT, cACT.n)
                    slab_done(zi, (cPE, cPE.n))
                    k_marks = []
                    for hh in range(2):
                        w = [(cACT, f_marks[-1])]
                        if glast["k_r"] is not None:
                            w.append(glast["k_r"])
                        S.emit("scalar",
                               lambda eng, h=hh: eng.activation(
                                   k3[:, h, :], f3[:, h, :], AF.Sigmoid,
                                   scale=-1.0),
                               wait=w, inc=[(cACT, 1)])
                        k_marks.append(cACT.n)
                    segs_own = [(xn3, 0, 512, 0), (xn3, 512, 512, 512)]
                    qsg_free = None
                    for oc in range(2):
                        for seg in segs_own:
                            psb, i, pmk = proj_accum(qsl, oc, seg,
                                                     extra_w=[qrdy])
                            w = [pmk] + ([qsg_free] if qsg_free else [])
                            S.emit("scalar",
                                   lambda eng, nn=seg[2], pb=psb: eng.activation(
                                       qsg[:, 0:nn], pb[:, 0:nn], AF.Sigmoid),
                                   wait=w, inc=[(cACT, 1)])
                            am = cACT.n
                            w = [(cACT, am)]
                            if glast["q_r"] is not None and oc == 0 and seg[1] == 0:
                                w.append(glast["q_r"])
                            S.emit("vector",
                                   lambda eng, o=oc, to=seg[3], nn=seg[2], pb=psb:
                                   eng.tensor_mul(q3[:, o, to:to + nn],
                                                  pb[:, 0:nn], qsg[:, 0:nn]),
                                   wait=w, inc=[(cDVE, 1)])
                            qsg_free = (cDVE, cDVE.n)
                            pp_mark[i] = (cDVE, cDVE.n)
                    slab_done(qi, (cPE, cPE.n))
                    vsl, vrdy, vi = load_slab(aWvT, g * 256, 256)
                    for tt in range(9):
                        if tt == 0:
                            lhs_src, lc = xnh3, 0
                        else:
                            lhs_src, lc = xn3, (tt - 1) * 128
                        i = ppi[0]
                        ppi[0] ^= 1
                        psb = pp[i]
                        for k in range(NDT):
                            w = [vrdy] if k == 0 else []
                            if k == 0 and pp_mark[i] is not None:
                                w.append(pp_mark[i])
                            S.emit("tensor",
                                   lambda eng, kk=k, ls=lhs_src, c=lc, pb=psb,
                                          vs=vsl: eng.matmul(
                                       pb[:, 0:256], ls[:, kk, c:c + 128],
                                       vs[:, kk, 0:256],
                                       start=(kk == 0), stop=(kk == NDT - 1)),
                                   wait=w, inc=[(cPE, 1)] if k == NDT - 1 else [])
                        pmk = (cPE, cPE.n)
                        w = [pmk]
                        if glast["v_r"] is not None and tt == 0:
                            w.append(glast["v_r"])
                        S.emit("scalar",
                               lambda eng, t=tt, pb=psb: eng.copy(
                                   v3[:, t, :], pb[:, 0:256]),
                               wait=w, inc=[(cACT, 1)])
                        pp_mark[i] = (cACT, cACT.n)
                    slab_done(vi, (cPE, cPE.n))
                    v_evac_m = cACT.n
                    nlf_marks = []
                    for hh in range(2):
                        for half in range(4):
                            c0h = half * 288
                            w = [(cACT, f_marks[-1])]
                            S.emit("scalar",
                                   lambda eng, h=hh, c=c0h: eng.activation(
                                       usc[:, 0:288], f3[:, h, c:c + 288],
                                       AF.Exp, scale=-1.0),
                                   wait=w, inc=[(cACT, 1)])
                            um = cACT.n
                            w = [(cACT, um)]
                            if glast["nlf_r"] is not None:
                                w.append(glast["nlf_r"])
                            S.emit("scalar",
                                   lambda eng, h=hh, c=c0h: eng.activation(
                                       nlf3[:, h, c:c + 288], usc[:, 0:288],
                                       AF.Ln, bias=1.0),
                                   wait=w, inc=[(cACT, 1)])
                            nlf_marks.append(cACT.n)
                    nlf_m = cACT.n

                    for hh in range(2):
                        w = []
                        if hm[hh].get("S_mm") is not None:
                            w += [(cPE, hm[hh]["S_mm"]), (cDVE, hm[hh]["S_scale"]),
                                  (cPE, hm[hh]["o_mm"])]
                        S.emit("vector",
                               lambda eng, h=hh: eng.memset(
                                   Sst[:, h * 128:(h + 1) * 128], 0.0),
                               wait=w, inc=[(cDVE, 1)])
                        hm[hh]["S_scale"] = cDVE.n
                        for kk2 in ("oevac", "mask", "scores", "o_mm",
                                    "S_mm", "S_add", "tr", "tr2"):
                            hm[hh].setdefault(kk2, None)
                        for kk2 in ("eCp_w", "eCn_w", "kdec_w", "qdec_w",
                                    "kdecU_e"):
                            hm[hh].setdefault(kk2, [None, None])

                    pf_free = gl_pf

                    def emit_A(c, hh):
                        pr = c % 2
                        m = hm[hh]
                        own = c >= HCH
                        base = (hh * 2 + pr) * CH
                        ctS = ct[:, base:base + CH]
                        eCpS = eCp[:, base:base + CH]
                        eCnS = eCn[:, base:base + CH]
                        qdS = qdec[:, base:base + CH]
                        kdS = kdec[:, base:base + CH]
                        w = [(cACT, nlf_m)]
                        if m["eCp_w"][pr] is not None:
                            w += [(cACT, m["eCp_w"][pr]), (cACT, m["eCn_w"][pr])]
                        S.emit("vector",
                               lambda eng, h=hh, cc=c, o=ctS:
                               eng.tensor_tensor_scan(
                                   o, nlf3[:, h, cc * CH:(cc + 1) * CH],
                                   zer64[:, 0:CH], 0.0, ALU.add, ALU.add),
                               wait=w, inc=[(cDVE, 1)])
                        scan_m = cDVE.n
                        w = [(cDVE, scan_m)]
                        if m["kdec_w"][pr] is not None:
                            w.append((cDVE, m["kdec_w"][pr]))
                        S.emit("scalar",
                               lambda eng, o=eCpS, i0=ctS: eng.activation(
                                   o, i0, AF.Exp),
                               wait=w, inc=[(cACT, 1)])
                        m["eCp_w"][pr] = cACT.n
                        w = [(cDVE, scan_m)]
                        if m["qdec_w"][pr] is not None:
                            w.append((cDVE, m["qdec_w"][pr]))
                        if m["S_scale"] is not None:
                            w.append((cDVE, m["S_scale"]))
                        S.emit("scalar",
                               lambda eng, o=eCnS, i0=ctS: eng.activation(
                                   o, i0, AF.Exp, scale=-1.0),
                               wait=w, inc=[(cACT, 1)])
                        m["eCn_w"][pr] = cACT.n
                        w = [(cACT, m["eCp_w"][pr]), (cACT, k_marks[hh])]
                        if m["tr2"] is not None:
                            w.append((cPE, m["tr2"]))
                        if m["scores"] is not None:
                            w.append((cPE, m["scores"]))
                        S.emit("vector",
                               lambda eng, h=hh, cc=c, o=kdS, i1=eCpS:
                               eng.tensor_mul(
                                   o, k3[:, h, cc * CH:(cc + 1) * CH], i1),
                               wait=w, inc=[(cDVE, 1)])
                        m["kdec_w"][pr] = cDVE.n
                        w = [(cDVE, m["kdec_w"][pr]), (dIn, in_done)]
                        if pf_free[0] is not None:
                            w.append(pf_free[0])
                        S.emit("tensor",
                               lambda eng, h=hh, i0=kdS: eng.transpose(
                                   PF[0:64, h * 128:h * 128 + 128], i0, idb[:]),
                               wait=w, inc=[(cPE, 1)])
                        m["tr2"] = m["tr"]
                        m["tr"] = cPE.n
                        bbu = pr * 64
                        w = [(cPE, m["tr"])]
                        if m["S_mm"] is not None:
                            w.append((cPE, m["S_mm"]))
                        S.emit("vector",
                               lambda eng, h=hh, b=bbu: eng.tensor_copy(
                                   kdecU[b:b + 64, h * 128:(h + 1) * 128],
                                   PF[0:64, h * 128:h * 128 + 128]),
                               wait=w, inc=[(cDVE, 1)])
                        m["kdecU_e"][pr] = cDVE.n
                        pf_free[0] = (cDVE, cDVE.n)
                        if own:
                            co = c - HCH
                            w = [(cACT, m["eCn_w"][pr])]
                            if m["o_mm"] is not None:
                                w.append((cPE, m["o_mm"]))
                            S.emit("vector",
                                   lambda eng, h=hh, t0=co * CH, o=qdS,
                                          i1=eCnS: eng.tensor_mul(
                                       o, q3[:, h, t0:t0 + CH], i1),
                                   wait=w, inc=[(cDVE, 1)])
                            m["qdec_w"][pr] = cDVE.n

                    def emit_BC(c, hh):
                        pr = c % 2
                        m = hm[hh]
                        own = c >= HCH
                        base = (hh * 2 + pr) * CH
                        eCnS = eCn[:, base:base + CH]
                        qdS = qdec[:, base:base + CH]
                        kdS = kdec[:, base:base + CH]
                        bbu = pr * 64
                        if own:
                            w = [(cDVE, m["qdec_w"][pr]), (cDVE, m["kdec_w"][pr])]
                            if m["mask"] is not None:
                                w.append((cDVE, m["mask"]))
                            S.emit("tensor",
                                   lambda eng, h=hh, i0=kdS, i1=qdS: eng.matmul(
                                       PA[0:CH, h * 512:h * 512 + CH],
                                       i0, i1, start=True, stop=True),
                                   wait=w, inc=[(cPE, 1)])
                            m["scores"] = cPE.n
                            S.emit("vector",
                                   lambda eng, h=hh: eng.tensor_mul(
                                       pm[0:CH, h * CH:(h + 1) * CH],
                                       PA[0:CH, h * 512:h * 512 + CH],
                                       maskT[:]),
                                   wait=[(cPE, m["scores"])], inc=[(cDVE, 1)])
                            m["mask"] = cDVE.n
                            if bbu:
                                S.emit("vector",
                                       lambda eng, h=hh: eng.tensor_copy(
                                           pm[64:64 + CH, h * CH:(h + 1) * CH],
                                           pm[0:CH, h * CH:(h + 1) * CH]),
                                       wait=[(cDVE, m["mask"])],
                                       inc=[(cDVE, 1)])
                                m["mask"] = cDVE.n
                            w = [(cDVE, m["mask"]), (cACT, v_evac_m)]
                            if m["oevac"] is not None:
                                w.append((cACT, m["oevac"]))
                            S.emit("tensor",
                                   lambda eng, h=hh, cc=c, b=bbu: eng.matmul(
                                       PA[:, h * 512 + CH:h * 512 + 2 * CH],
                                       v3[b:b + 64, cc // 2,
                                          h * 128:(h + 1) * 128],
                                       pm[b:b + CH, h * CH:(h + 1) * CH],
                                       start=True, stop=False),
                                   wait=w)
                            S.emit("tensor",
                                   lambda eng, h=hh, i1=qdS: eng.matmul(
                                       PA[:, h * 512 + CH:h * 512 + 2 * CH],
                                       Sst[:, h * 128:(h + 1) * 128],
                                       i1, start=False, stop=True),
                                   wait=[(cDVE, m["S_scale"])],
                                   inc=[(cPE, 1)])
                            m["o_mm"] = cPE.n
                        w = [(cDVE, m["kdecU_e"][pr]), (cACT, v_evac_m)]
                        if m["S_add"] is not None:
                            w.append((cDVE, m["S_add"]))
                        S.emit("tensor",
                               lambda eng, h=hh, cc=c, b=bbu: eng.matmul(
                                   PB[:, h * 512:h * 512 + 128],
                                   kdecU[b:b + 64, h * 128:(h + 1) * 128],
                                   v3[b:b + 64, cc // 2,
                                      h * 128:(h + 1) * 128],
                                   start=True, stop=True),
                               wait=w, inc=[(cPE, 1)])
                        m["S_mm"] = cPE.n
                        S.emit("vector",
                               lambda eng, h=hh: eng.tensor_add(
                                   ssc[:, h * 128:(h + 1) * 128],
                                   PB[:, h * 512:h * 512 + 128],
                                   Sst[:, h * 128:(h + 1) * 128]),
                               wait=[(cPE, m["S_mm"])], inc=[(cDVE, 1)])
                        m["S_add"] = cDVE.n
                        w = [(cDVE, m["S_add"]), (cACT, m["eCn_w"][pr])]
                        if m["o_mm"] is not None:
                            w.append((cPE, m["o_mm"]))
                        S.emit("vector",
                               lambda eng, h=hh, i1=eCnS[:, CH - 1:CH]:
                               eng.tensor_scalar_mul(
                                   Sst[:, h * 128:(h + 1) * 128],
                                   ssc[:, h * 128:(h + 1) * 128], i1),
                               wait=w, inc=[(cDVE, 1)])
                        m["S_scale"] = cDVE.n
                        if own:
                            S.emit("scalar",
                                   lambda eng, h=hh, gg=g, t0=(c - HCH) * CH:
                                   eng.copy(
                                       o3[:, 2 * gg + h, t0:t0 + CH],
                                       PA[:, h * 512 + CH:h * 512 + 2 * CH]),
                                   wait=[(cPE, m["o_mm"])], inc=[(cACT, 1)])
                            m["oevac"] = cACT.n

                    for c in range(NCH + 1):
                        for hh in range(2):
                            if c < NCH:
                                emit_A(c, hh)
                            if c >= 1:
                                emit_BC(c - 1, hh)
                        if 2 <= c < 14:
                            next(pcg, None)
                    glast["f_r"] = (cACT, cACT.n)
                    glast["k_r"] = (cDVE, cDVE.n)
                    glast["q_r"] = (cDVE, cDVE.n)
                    glast["v_r"] = (cPE, cPE.n)
                    glast["nlf_r"] = (cDVE, cDVE.n)

                for _ in range(200):
                    if next(pcg, None) is None:
                        break
                ln_state["sums_free"] = (cACT, cACT.n)
                ln_state["pa_free"] = (cACT, cACT.n)
                gla_done = [(cACT, cACT.n), (cDVE, cDVE.n), (cPE, cPE.n)]
                rms_m = emit_norm(o3, xn3, T, rms=True, extra_wait=gla_done)

                pp_mark2 = [None, None]
                ppj = [0]
                for oc in range(NDT):
                    osl, ordy, oi = load_slab(aWoT, oc * 128, 128)
                    for sc in (0, 512):
                        i = ppj[0]
                        ppj[0] ^= 1
                        psb = pp[i]
                        for k in range(NDT):
                            w = [ordy, rms_m] if k == 0 else []
                            if k == 0 and pp_mark2[i] is not None:
                                w.append(pp_mark2[i])
                            if k == 0 and pp_mark[i] is not None:
                                w.append(pp_mark[i])
                                pp_mark[i] = None
                            S.emit("tensor",
                                   lambda eng, kk=k, o=oc, c=sc, pb=psb, os_=osl:
                                   eng.matmul(
                                       pb[:, 0:512],
                                       os_[:, kk, 0:128],
                                       xn3[:, kk, c:c + 512],
                                       start=(kk == 0), stop=(kk == NDT - 1)),
                                   wait=w, inc=[(cPE, 1)] if k == NDT - 1 else [])
                        pmk = (cPE, cPE.n)
                        S.emit("vector",
                               lambda eng, o=oc, c=sc, pb=psb: eng.tensor_add(
                                   x3[:, o, c:c + 512], x3[:, o, c:c + 512],
                                   pb[:, 0:512]),
                               wait=[pmk], inc=[(cDVE, 1)])
                        pp_mark2[i] = (cDVE, cDVE.n)
                    slab_done(oi, (cPE, cPE.n))

        ffn_ctr = [0]

        def emit_ffn():
            ffn_ctr[0] += 1
            sfx = str(ffn_ctr[0])
            with ExitStack() as ph:
                er = lambda nm, shape, dt: ph.enter_context(
                    nc.sbuf_tensor(nm + sfx, list(shape), dt, side="right"))
                gated = er("ffn_gated", [128, 64 * 512], BF16)
                dsl = er("ffn_dsl", [128, 2 * 8 * 256], BF16)
                gs = er("ffn_gs", [128, 512], BF16)
                g3 = gated[:].rearrange("p (h t) -> p h t", t=512)
                d4 = dsl[:].rearrange("p (i k c) -> p i k c", i=2, k=8)
                ln_m = emit_norm(x3, xn3, T,
                                 extra_wait=[(cPE, cPE.n), (cDVE, cDVE.n)])
                rot = [PC, PD, PE_]
                rot_mark = [None, None, None]
                gu_slots = [
                    (wsl3[0], 0), (wsl3[0], 128), (wsl3[1], 0), (wsl3[1], 128)]
                slot_mark = [None, None, None, None]
                slot_rdy = [None, None, None, None]
                dsl_mark = [None, None]
                pa_rot = [PA, PB]
                pa_mark = [None, None]
                gs_free = [None]
                ri = [0]
                for half in range(2):
                    hc0 = half * 512
                    for hto in range(64):
                        sg = (hto % 2) * 2
                        su = sg + 1
                        for slot, wd in ((sg, fwg_bf), (su, fwu_bf)):
                            sbuf3, c0s = gu_slots[slot]
                            w = [slot_mark[slot]] if slot_mark[slot] else []
                            if half == 0 and hto < 2:
                                w.append(pc_state["done"])
                            src = wd.ap().rearrange("(k p) c -> p k c", p=128)[
                                :, :, hto * 128:(hto + 1) * 128]
                            S.emit("sync",
                                   lambda eng, b3=sbuf3, cc=c0s, ss=src:
                                   eng.dma_start(out=b3[:, :, cc:cc + 128],
                                                 in_=ss),
                                   wait=w, inc=[(dW, 16)])
                            slot_rdy[slot] = (dW, dW.n)
                        jg = ri[0] % 3
                        ju = (ri[0] + 1) % 3
                        ri[0] += 2
                        marks = {}
                        for slot, j in ((sg, jg), (su, ju)):
                            sbuf3, c0s = gu_slots[slot]
                            pb = rot[j]
                            for k in range(NDT):
                                w = []
                                if k == 0:
                                    w.append(slot_rdy[slot])
                                    w.append(ln_m)
                                    if rot_mark[j] is not None:
                                        w.append(rot_mark[j])
                                S.emit("tensor",
                                       lambda eng, kk=k, b3=sbuf3, cc=c0s,
                                              pbb=pb, c=hc0: eng.matmul(
                                           pbb[:, 0:512],
                                           b3[:, kk, cc:cc + 128],
                                           xn3[:, kk, c:c + 512],
                                           start=(kk == 0),
                                           stop=(kk == NDT - 1)),
                                       wait=w,
                                       inc=[(cPE, 1)] if k == NDT - 1 else [])
                            marks[slot] = (cPE, cPE.n)
                            slot_mark[slot] = (cPE, cPE.n)
                        wsil = [marks[sg]]
                        if gs_free[0] is not None:
                            wsil.append(gs_free[0])
                        S.emit("scalar",
                               lambda eng, pbb=rot[jg]: eng.activation(
                                   gs[:], pbb[:, 0:512], AF.Silu),
                               wait=wsil, inc=[(cACT, 1)])
                        am = cACT.n
                        rot_mark[jg] = (cACT, am)
                        wmul = [(cACT, am), marks[su]]
                        if half == 1 and hto == 0:
                            wmul.append((cPE, cPE.n))
                            if DBG == "gated":
                                wmul.append(dbg_hold["gated_done"])
                        S.emit("vector",
                               lambda eng, h=hto, pbb=rot[ju]: eng.tensor_mul(
                                   g3[:, h, :], gs[:], pbb[:, 0:512]),
                               wait=wmul, inc=[(cDVE, 1)])
                        rot_mark[ju] = (cDVE, cDVE.n)
                        gs_free[0] = (cDVE, cDVE.n)
                        if hto == 63:
                            gated_m = (cDVE, cDVE.n)
                    if DBG == "gated" and half == 0:
                        og = out_d.rearrange("(g p) t -> p g t", p=128)
                        S.emit("gpsimd", lambda eng, o=og: eng.dma_start(
                            out=o[:, :, 0:512], in_=g3[:, 0:16, :]),
                            wait=[gated_m], inc=[(dG, 16)])
                        S.emit("gpsimd", lambda eng, o=og: eng.dma_start(
                            out=o[:, :, 512:1024], in_=g3[:, 16:32, :]),
                            wait=[gated_m], inc=[(dG, 16)])
                        dbg_hold["gated_done"] = (dG, dG.n)
                    for op_ in range(8):
                        pi = op_ % 2
                        pab = pa_rot[pi]
                        dn_marks = []
                        for kc in range(8):
                            di = kc % 2
                            w = [dsl_mark[di]] if dsl_mark[di] else []
                            src = fwd_bf.ap().rearrange(
                                "(k p) c -> p k c", p=128)[
                                :, kc * 8:(kc + 1) * 8,
                                op_ * 256:(op_ + 1) * 256]
                            S.emit("sync",
                                   lambda eng, dd=di, ss=src: eng.dma_start(
                                       out=d4[:, dd, :, :], in_=ss),
                                   wait=w, inc=[(dW, 16)])
                            drdy = (dW, dW.n)
                            for oc2 in range(2):
                                for kk in range(8):
                                    hid = kc * 8 + kk
                                    w = []
                                    if kk == 0 and oc2 == 0:
                                        w.append(drdy)
                                        w.append(gated_m)
                                    if hid == 0 and kk == 0 and oc2 == 0 \
                                            and pa_mark[pi] is not None:
                                        w.append(pa_mark[pi])
                                    lastk = (oc2 == 1 and kk == 7)
                                    S.emit("tensor",
                                           lambda eng, dd=di, k2=kk, o2=oc2,
                                                  hh=hid, pbb=pab: eng.matmul(
                                               pbb[:, o2 * 512:o2 * 512 + 512],
                                               d4[:, dd, k2, o2 * 128:
                                                  o2 * 128 + 128],
                                               g3[:, hh, :],
                                               start=(hh == 0),
                                               stop=(hh == 63)),
                                           wait=w,
                                           inc=[(cPE, 1)] if (lastk or hid == 63)
                                           else [])
                                    if hid == 63:
                                        dn_marks.append((cPE, cPE.n))
                            dsl_mark[di] = (cPE, cPE.n)
                        adds = []
                        for oc2 in range(2):
                            oc = op_ * 2 + oc2
                            S.emit("vector",
                                   lambda eng, o=oc, o2=oc2, c=hc0, pbb=pab:
                                   eng.tensor_add(
                                       x3[:, o, c:c + 512],
                                       x3[:, o, c:c + 512],
                                       pbb[:, o2 * 512:o2 * 512 + 512]),
                                   wait=[dn_marks[oc2]], inc=[(cDVE, 1)])
                            adds.append(cDVE.n)
                        pa_mark[pi] = (cDVE, adds[-1])

        def emit_attn_b():
            with ExitStack() as ph:
                er = lambda nm, shape, dt: ph.enter_context(
                    nc.sbuf_tensor(nm, list(shape), dt, side="right"))
                cosT = er("b_cosT", [128, T], F32)
                sinT = er("b_sinT", [128, T], F32)
                sgn = er("b_sgn", [128, 1], F32)
                QrT = er("b_QrT", [128, 16 * T], BF16)
                KTb = er("b_KTb", [128, 4096], BF16)
                vcb = er("b_vcb", [128, 32 * 192], BF16)
                pTb = er("b_pT", [128, 2 * 1024], BF16)
                qraw = er("b_qraw", [128, 1024], BF16)
                t1 = er("b_t1", [128, 1024], BF16)
                t2 = er("b_t2", [128, 1024], BF16)
                rs = er("b_rs", [128, 512], F32)
                fin = er("b_fin", [128, 512], BF16)
                u_sc = er("b_usc", [1, 64], F32)
                negU = er("b_negU", [128, 4], F32)
                kbr = er("b_kbr", [1, 32], BF16)

                qr3 = QrT[:].rearrange("p (h t) -> p h t", h=16)
                kt3 = KTb[:].rearrange("p (x t) -> p x t", x=4)
                vc3 = vcb[:].rearrange("p (x c) -> p x c", c=192)
                vc8 = vcb[:].rearrange("p (x c) -> p x c", c=768)
                pt3 = pTb[:].rearrange("p (i t) -> p i t", i=2)

                dma("sync", dIn, cosT[:], cosT_in)
                dma("sync", dIn, sinT[:], sinT_in)
                dma("sync", dIn, sgn[:], sgn_in)
                tbl_m = (dIn, dIn.n)

                ln_m = emit_norm(x3, xn3, T,
                                 extra_wait=[(cPE, cPE.n), (cDVE, cDVE.n)])

                pp = [PA, PB]
                pp_mark = [None, None]
                ppi = [0]

                def proj16(slab3, octile, wait0):
                    i = ppi[0]
                    ppi[0] ^= 1
                    psb = pp[i]
                    for k in range(NDT):
                        w = list(wait0) if k == 0 else []
                        if k == 0 and pp_mark[i] is not None:
                            w.append(pp_mark[i])
                        S.emit("tensor",
                               lambda eng, kk=k, s3=slab3, oc=octile, pb=psb:
                               eng.matmul(
                                   pb[:, 0:512],
                                   s3[:, kk, oc * 128:(oc + 1) * 128],
                                   xn3[:, kk, 0:512],
                                   start=(kk == 0), stop=False),
                               wait=w)
                        S.emit("tensor",
                               lambda eng, kk=k, s3=slab3, oc=octile, pb=psb:
                               eng.matmul(
                                   pb[:, 512:1024],
                                   s3[:, kk, oc * 128:(oc + 1) * 128],
                                   xn3[:, kk, 512:1024],
                                   start=(kk == 0), stop=(kk == NDT - 1)),
                               wait=[], inc=[(cPE, 1)] if k == NDT - 1 else [])
                    return psb, i, (cPE, cPE.n)

                rst = {"m": []}

                def emit_rope(psrc, pmk, dsts):
                    w = [pmk] + list(rst["m"])
                    S.emit("scalar", lambda eng, pb=psrc: eng.copy(
                        qraw[:], pb[:, 0:T]), wait=w, inc=[(cACT, 1)])
                    ev = cACT.n
                    S.emit("vector", lambda eng: eng.tensor_mul(
                        t1[:], qraw[:], cosT[:]),
                           wait=[(cACT, ev), tbl_m], inc=[(cDVE, 1)])
                    S.emit("vector", lambda eng: eng.tensor_mul(
                        t2[:], qraw[:], sinT[:]),
                           wait=[(cACT, ev)], inc=[(cDVE, 1)])
                    m2 = cDVE.n
                    for j in range(4):
                        sb0 = (j ^ 1) * 32
                        S.emit("vector",
                               lambda eng, jj=j, sq=sb0: eng.tensor_copy(
                                   qraw[jj * 32:jj * 32 + 32, :],
                                   t2[sq:sq + 32, :]),
                               wait=[(cDVE, m2)], inc=[(cDVE, 1)])
                    m3 = cDVE.n
                    S.emit("vector", lambda eng: eng.scalar_tensor_tensor(
                        t2[:], qraw[:], sgn[:, 0:1], t1[:],
                        ALU.mult, ALU.add),
                           wait=[(cDVE, m3)], inc=[(cDVE, 1)])
                    mr = cDVE.n
                    for hh in range(2):
                        S.emit("vector",
                               lambda eng, h=hh, ad=dsts[hh]: eng.tensor_copy(
                                   ad, t2[h * 64:h * 64 + 64, :]),
                               wait=[(cDVE, mr)], inc=[(cDVE, 1)])
                    rst["m"] = [(cDVE, cDVE.n)]
                    return (cDVE, cDVE.n)

                for ov in range(2):
                    ksl, krdy, ki = load_slab(bWkT, ov * 256, 256)
                    for oc2 in range(2):
                        ot = ov * 2 + oc2
                        psb, i, pmk = proj16(ksl, oc2, [krdy, ln_m])
                        mk = emit_rope(psb, pmk,
                                       [kt3[0:64, ot, :], kt3[64:128, ot, :]])
                        pp_mark[i] = mk
                    slab_done(ki, (cPE, cPE.n))
                kr_done = rst["m"][0]
                k2m = []
                for ot in range(4):
                    w = [kr_done] if ot == 0 else [(cDVE, k2m[-1])]
                    S.emit("scalar", lambda eng, o=ot: eng.activation(
                        t1[:], kt3[:, o, :], AF.Square),
                        wait=w, inc=[(cACT, 1)])
                    sqm = cACT.n
                    wpe = [(cACT, sqm)]
                    if ot > 0:
                        wpe.append((cDVE, k2m[-1]))
                    S.emit("tensor", lambda eng: eng.matmul(
                        PE_[0:1, 0:512], ones1[:], t1[:, 0:512],
                        start=True, stop=False), wait=wpe)
                    S.emit("tensor", lambda eng: eng.matmul(
                        PE_[0:1, 0:512], ones1[:], t1[:, 512:1024],
                        start=False, stop=True), inc=[(cPE, 1)])
                    pm2 = cPE.n
                    S.emit("vector", lambda eng, o=ot: eng.reduce_max(
                        u_sc[0:1, 16 + o:17 + o], PE_[0:1, 0:512],
                        axis=mybir.AxisListType.X),
                        wait=[(cPE, pm2)], inc=[(cDVE, 1)])
                    k2m.append(cDVE.n)
                S.emit("vector", lambda eng: eng.tensor_copy(
                    kbr[0:1, 0:4], u_sc[0:1, 16:20]),
                    wait=[(cDVE, k2m[-1])], inc=[(cDVE, 1)])
                kbr_m = (cDVE, cDVE.n)

                pf_free2 = None
                vtr_last = [None]
                vlast_sq = (cDVE, cDVE.n)
                for ov in range(2):
                    vsl, vrdy, vi = load_slab(bWvT, ov * 256, 256)
                    for oc2 in range(2):
                        ot = ov * 2 + oc2
                        psb, i, pmk = proj16(vsl, oc2, [vrdy, ln_m])
                        w = [pmk, kr_done, (cACT, cACT.n)]
                        if vtr_last[0] is not None:
                            w.append(vtr_last[0])
                        S.emit("scalar", lambda eng, pb=psb: eng.copy(
                            t2[:], pb[:, 0:T]), wait=w, inc=[(cACT, 1)])
                        evm = cACT.n
                        pp_mark[i] = (cACT, evm)
                        for tt in range(8):
                            w = [(cACT, evm), (dIn, in_done)]
                            if pf_free2 is not None:
                                w.append(pf_free2)
                            S.emit("tensor",
                                   lambda eng, t=tt: eng.transpose(
                                       PF[:, 0:128],
                                       t2[:, t * 128:(t + 1) * 128], idb[:]),
                                   wait=w, inc=[(cPE, 1)])
                            trm = cPE.n
                            vtr_last[0] = (cPE, trm)
                            S.emit("vector",
                                   lambda eng, t=tt, o=ot: eng.tensor_copy(
                                       vc8[:, t, o * 128:(o + 1) * 128],
                                       PF[:, 0:128]),
                                   wait=[(cPE, trm)], inc=[(cDVE, 1)])
                            pf_free2 = (cDVE, cDVE.n)
                    slab_done(vi, (cPE, cPE.n))
                v_done = (cDVE, cDVE.n)
                rst["m"] = [vtr_last[0], (cDVE, cDVE.n)]

                S.emit("sync", lambda eng: eng.dma_start(
                    out=cc_in_k.ap()[0:512, :].rearrange(
                        "(x p) t -> p x t", p=128),
                    in_=kt3[:, :, :]), wait=[kr_done], inc=[(dG, 16)])
                S.emit("sync", lambda eng: eng.dma_start(
                    out=cc_in_k.ap()[512:513, 0:4], in_=kbr[0:1, 0:4]),
                    wait=[kbr_m], inc=[(dG, 16)])
                S.emit("sync", lambda eng: eng.dma_start(
                    out=cc_in_v.ap().rearrange("(x p) c -> p x c", p=128),
                    in_=vc8[:, :, 0:512]),
                    wait=[v_done], inc=[(dG, 16)])
                gm = dG.n
                S.emit("gpsimd", lambda eng: eng.collective_compute(
                    "AllGather", ALU.bypass, replica_groups=RG,
                    ins=[cc_in_k.ap().opt()], outs=[cc_out_k.ap().opt()]),
                    wait=[(dG, gm)], inc=[(cCC, 1)])
                S.emit("gpsimd", lambda eng: eng.collective_compute(
                    "AllGather", ALU.bypass, replica_groups=RG,
                    ins=[cc_in_v.ap().opt()], outs=[cc_out_v.ap().opt()]),
                    inc=[(cCC, 1)])
                cc_m = (cCC, cCC.n)

                q2m = []
                for oq in range(8):
                    qsl, qrdy, qi = load_slab(bWqT, oq * 256, 256)
                    for oc2 in range(2):
                        ot = oq * 2 + oc2
                        dsts = []
                        for hh in range(2):
                            q = 2 * ot + hh
                            tile = (q // 8) * 4 + (q % 4)
                            base = ((q // 4) % 2) * 64
                            dsts.append(qr3[base:base + 64, tile, :])
                        psb, i, pmk = proj16(qsl, oc2, [qrdy, ln_m])
                        mk = emit_rope(psb, pmk, dsts)
                        pp_mark[i] = mk
                        S.emit("scalar", lambda eng: eng.activation(
                            t1[:], t2[:], AF.Square),
                            wait=[mk], inc=[(cACT, 1)])
                        sqm = cACT.n
                        wq2 = [(cACT, sqm)]
                        if q2m:
                            wq2.append((cDVE, q2m[-1]))
                        else:
                            wq2.append((cDVE, k2m[-1]))
                        S.emit("tensor", lambda eng: eng.matmul(
                            PE_[0:1, 0:512], ones1[:], t1[:, 0:512],
                            start=True, stop=False), wait=wq2)
                        S.emit("tensor", lambda eng: eng.matmul(
                            PE_[0:1, 0:512], ones1[:], t1[:, 512:1024],
                            start=False, stop=True), inc=[(cPE, 1)])
                        pm2 = cPE.n
                        S.emit("vector", lambda eng, o=ot: eng.reduce_max(
                            u_sc[0:1, o:o + 1], PE_[0:1, 0:512],
                            axis=mybir.AxisListType.X),
                            wait=[(cPE, pm2)], inc=[(cDVE, 1)])
                        q2m.append(cDVE.n)
                        rst["m"] = [(cACT, sqm), (cDVE, cDVE.n)]
                    slab_done(qi, (cPE, cPE.n))
                q_done = (cDVE, cDVE.n)

                S.emit("sync", lambda eng: eng.dma_start(
                    out=kbr[0:1, 0:16].rearrange("p (r c) -> p r c", r=4),
                    in_=cc_out_k.ap().rearrange("(r x) t -> x r t", r=4)[
                        512:513, :, 0:4]),
                    wait=[cc_m], inc=[(dG, 16)])
                kb_m = (dG, dG.n)
                S.emit("vector", lambda eng: eng.tensor_max(
                    u_sc[0:1, 52:56], kbr[0:1, 0:4], kbr[0:1, 4:8]),
                    wait=[kb_m, (cDVE, k2m[-1])], inc=[(cDVE, 1)])
                S.emit("vector", lambda eng: eng.tensor_max(
                    u_sc[0:1, 56:60], kbr[0:1, 8:12], kbr[0:1, 12:16]),
                    wait=[(cDVE, cDVE.n)], inc=[(cDVE, 1)])
                S.emit("vector", lambda eng: eng.tensor_max(
                    u_sc[0:1, 16:20], u_sc[0:1, 52:56], u_sc[0:1, 56:60]),
                    wait=[(cDVE, cDVE.n)], inc=[(cDVE, 1)])
                q2v = u_sc[0:1, 0:16].rearrange("p (u t) -> p u t", u=4)
                S.emit("vector", lambda eng: eng.tensor_max(
                    u_sc[0:1, 20:24], q2v[:, :, 0], q2v[:, :, 1]),
                    wait=[(cDVE, q2m[-1]), (cDVE, cDVE.n)], inc=[(cDVE, 1)])
                S.emit("vector", lambda eng: eng.tensor_max(
                    u_sc[0:1, 24:28], q2v[:, :, 2], q2v[:, :, 3]),
                    wait=[(cDVE, cDVE.n)], inc=[(cDVE, 1)])
                S.emit("vector", lambda eng: eng.tensor_max(
                    u_sc[0:1, 28:32], u_sc[0:1, 20:24], u_sc[0:1, 24:28]),
                    wait=[(cDVE, cDVE.n)], inc=[(cDVE, 1)])
                S.emit("scalar", lambda eng: eng.activation(
                    u_sc[0:1, 32:36], u_sc[0:1, 28:32], AF.Ln),
                    wait=[(cDVE, cDVE.n)], inc=[(cACT, 1)])
                S.emit("scalar", lambda eng: eng.activation(
                    u_sc[0:1, 36:40], u_sc[0:1, 16:20], AF.Ln),
                    wait=[(cACT, cACT.n)], inc=[(cACT, 1)])
                S.emit("vector", lambda eng: eng.tensor_add(
                    u_sc[0:1, 40:44], u_sc[0:1, 32:36], u_sc[0:1, 36:40]),
                    wait=[(cACT, cACT.n)], inc=[(cDVE, 1)])
                S.emit("scalar", lambda eng: eng.activation(
                    u_sc[0:1, 44:48], u_sc[0:1, 40:44], AF.Exp, scale=0.5),
                    wait=[(cDVE, cDVE.n)], inc=[(cACT, 1)])
                S.emit("vector", lambda eng: eng.tensor_scalar_mul(
                    u_sc[0:1, 48:52], u_sc[0:1, 44:48], -0.125),
                    wait=[(cACT, cACT.n)], inc=[(cDVE, 1)])
                nu_m = cDVE.n
                S.emit("tensor", lambda eng: eng.matmul(
                    PE_[:, 0:4], onesF[:], u_sc[0:1, 48:52],
                    start=True, stop=True),
                    wait=[(cDVE, nu_m)], inc=[(cPE, 1)])
                S.emit("scalar", lambda eng: eng.copy(negU[:], PE_[:, 0:4]),
                    wait=[(cPE, cPE.n)], inc=[(cACT, 1)])
                nu_done = (cACT, cACT.n)

                SP = [PA, PB]
                sp_mark = [(cACT, cACT.n), (cACT, cACT.n)]
                pt_mark = [None, None]
                oacc_mark = [None, None]
                kv_mark = None
                wo_marks = []
                wo_pp = [PC, PD]
                wo_mark = [None, None]
                woi = [0]
                for qt in range(2):
                    att_done = []
                    for u in range(4):
                        w = [cc_m, (cDVE, k2m[-1]), kbr_m]
                        if kv_mark is not None:
                            w.append(kv_mark)
                        for par in range(2):
                            kvh = 2 * u + par
                            S.emit("sync",
                                   lambda eng, v=kvh, b=par * 64: eng.dma_start(
                                       out=KTb[b:b + 64, :].rearrange(
                                           "p (r t) -> p r t", r=4),
                                       in_=cc_out_k.ap().rearrange(
                                           "(r x) t -> x r t", r=4)[
                                           v * 64:(v + 1) * 64, :, :]),
                                   wait=w if par == 0 else [],
                                   inc=[(dG, 16)])
                            S.emit("sync",
                                   lambda eng, v=kvh, c0=par * 128:
                                   eng.dma_start(
                                       out=vc3[:, :, c0:c0 + 64] if c0 == 0
                                       else vc3[:, :, 128:192],
                                       in_=cc_out_v.ap().rearrange(
                                           "(x p) c -> p x c", p=128)[
                                           :, :, v * 64:(v + 1) * 64]),
                                   wait=[], inc=[(dG, 16)])
                        kv_rdy = (dG, dG.n)
                        if qt == 0 and u == 0:
                            S.emit("vector", lambda eng: eng.memset(
                                vc3[:, :, 64:128], 1.0),
                                wait=[v_done, (dG, gm)], inc=[(cDVE, 1)])
                        ones_m = (cDVE, cDVE.n)
                        for sp_ in range(4):
                            qa = 8 * u + sp_
                            tile = u * 4 + sp_
                            for kt in range(32):
                                pj = kt % 2
                                spb = SP[pj]
                                w = [kv_rdy, ones_m, q_done, nu_done]
                                if sp_mark[pj] is not None:
                                    w.append(sp_mark[pj])
                                S.emit("tensor",
                                       lambda eng, k2=kt, tl=tile, pb=spb,
                                              c=qt * 512: eng.matmul(
                                           pb[:, 0:512],
                                           KTb[0:64, k2 * 128:(k2 + 1) * 128],
                                           qr3[0:64, tl, c:c + 512],
                                           start=True, stop=True),
                                       wait=w)
                                S.emit("tensor",
                                       lambda eng, k2=kt, tl=tile, pb=spb,
                                              c=qt * 512: eng.matmul(
                                           pb[:, 512:1024],
                                           KTb[64:128, k2 * 128:(k2 + 1) * 128],
                                           qr3[64:128, tl, c:c + 512],
                                           start=True, stop=True),
                                       wait=[], inc=[(cPE, 1)])
                                mm1_m = cPE.n
                                w = [(cPE, mm1_m)]
                                if pt_mark[pj] is not None:
                                    w.append(pt_mark[pj])
                                S.emit("scalar",
                                       lambda eng, pb=spb, pj2=pj, uu=u:
                                       eng.activation(
                                           pt3[:, pj2, :], pb[:, 0:1024],
                                           AF.Exp, scale=0.125,
                                           bias=negU[:, uu:uu + 1]),
                                       wait=w, inc=[(cACT, 1)])
                                exp_m = cACT.n
                                sp_mark[pj] = (cACT, exp_m)
                                w = [(cACT, exp_m)]
                                if kt == 0 and oacc_mark[0] is not None:
                                    w += [oacc_mark[0], oacc_mark[1]]
                                S.emit("tensor",
                                       lambda eng, k2=kt, pj2=pj: eng.matmul(
                                           PC[:, 0:512],
                                           vc3[:, k2, 0:128],
                                           pt3[:, pj2, 0:512],
                                           start=(k2 == 0), stop=(k2 == 31)),
                                       wait=w)
                                S.emit("tensor",
                                       lambda eng, k2=kt, pj2=pj: eng.matmul(
                                           PD[:, 0:512],
                                           vc3[:, k2, 64:192],
                                           pt3[:, pj2, 512:1024],
                                           start=(k2 == 0), stop=(k2 == 31)),
                                       wait=[],
                                       inc=[(cPE, 1)] if kt == 31 else [])
                                pt_mark[pj] = (cPE, cPE.n)
                            mm2_m = (cPE, cPE.n)
                            for hh, (pb, orow, srow) in enumerate(
                                    ((PC, 0, 64), (PD, 64, 0))):
                                q = qa + 4 * hh
                                S.emit("vector",
                                       lambda eng, p2=pb, orw=orow, sr=srow:
                                       eng.reciprocal(
                                           rs[orw:orw + 64, :],
                                           p2[sr:sr + 64, 0:512]),
                                       wait=[mm2_m], inc=[(cDVE, 1)])
                                rm = cDVE.n
                                S.emit("vector",
                                       lambda eng, p2=pb, orw=orow:
                                       eng.tensor_mul(
                                           fin[orw:orw + 64, :],
                                           p2[orw:orw + 64, 0:512],
                                           rs[orw:orw + 64, :]),
                                       wait=[(cDVE, rm)], inc=[(cDVE, 1)])
                                fm = cDVE.n
                                nt, nb = q // 2, (q % 2) * 64
                                S.emit("vector",
                                       lambda eng, t3=nt, b3=nb, orw=orow,
                                              c=qt * 512: eng.tensor_copy(
                                           xn3[b3:b3 + 64, t3, c:c + 512],
                                           fin[orw:orw + 64, :]),
                                       wait=[(cDVE, fm)], inc=[(cDVE, 1)])
                                att_done.append(cDVE.n)
                            oacc_mark[0] = (cDVE, cDVE.n)
                            oacc_mark[1] = (cDVE, cDVE.n)
                        kv_mark = (cPE, cPE.n)
                    att_m = (cDVE, att_done[-1])
                    for oc in range(NDT):
                        osl, ordy, oi = load_slab(bWoT, oc * 128, 128)
                        i = woi[0]
                        woi[0] ^= 1
                        psb = wo_pp[i]
                        for k in range(NDT):
                            w = [ordy, att_m] if k == 0 else []
                            if k == 0 and wo_mark[i] is not None:
                                w.append(wo_mark[i])
                            S.emit("tensor",
                                   lambda eng, kk=k, pb=psb, os_=osl,
                                          c=qt * 512: eng.matmul(
                                       pb[:, 0:512], os_[:, kk, 0:128],
                                       xn3[:, kk, c:c + 512],
                                       start=(kk == 0), stop=(kk == NDT - 1)),
                                   wait=w, inc=[(cPE, 1)] if k == NDT - 1 else [])
                        pmk = (cPE, cPE.n)
                        S.emit("vector",
                               lambda eng, o=oc, pb=psb, c=qt * 512:
                               eng.tensor_add(
                                   x3[:, o, c:c + 512], x3[:, o, c:c + 512],
                                   pb[:, 0:512]),
                               wait=[pmk], inc=[(cDVE, 1)])
                        wo_mark[i] = (cDVE, cDVE.n)
                        slab_done(oi, pmk)
                        wo_marks.append(cDVE.n)

        if NSUB == -1:
            barrier()
            emit_ffn()
        if NSUB >= 1:
            barrier()
            emit_attn_a()
        if NSUB >= 2:
            barrier()
            emit_ffn()
        if NSUB >= 3:
            barrier()
            emit_attn_b()
        if NSUB >= 4:
            barrier()
            emit_ffn()

        if DBG == "attnT":
            S.emit("gpsimd", lambda eng: eng.dma_start(
                out=out_d.rearrange("(d p) t -> p d t", p=128), in_=xn3),
                wait=[(c, c.n) for c in allch], inc=[(dOut, 16)])
            S.emit("sync", lambda eng: None, wait=[(dOut, dOut.n)])
        elif DBG == "gated":
            S.emit("sync", lambda eng: None,
                   wait=[(c, c.n) for c in allch] + [(dG, dG.n)])
        elif DBG in ("xn", "o", "q", "k", "nlf", "v"):
            src_dbg = {"xn": xn3}.get(DBG)
            if src_dbg is None:
                src_dbg = dbg_hold[DBG]
            S.emit("gpsimd", lambda eng, sd=src_dbg: eng.dma_start(
                out=out_d.rearrange("(d p) t -> p d t", p=128), in_=sd),
                wait=[(c, c.n) for c in allch], inc=[(dOut, 16)])
        else:
            S.emit("sync", lambda eng: eng.dma_start(
                out=out_d.rearrange("(d p) t -> p d t", p=128), in_=x3),
                wait=[(c, c.n) for c in allch], inc=[(dOut, 16)])
        S.emit("sync", lambda eng: None, wait=[(dOut, dOut.n)])

        with nc.Block() as block:
            @block.sync
            def _(eng):
                for f in S.q["sync"]:
                    f(eng)

            @block.gpsimd
            def _(eng):
                for f in S.q["gpsimd"]:
                    f(eng)

            @block.vector
            def _(eng):
                for f in S.q["vector"]:
                    f(eng)

            @block.scalar
            def _(eng):
                for f in S.q["scalar"]:
                    f(eng)

            @block.tensor
            def _(eng):
                for f in S.q["tensor"]:
                    f(eng)

    return nc


def _host_inputs(L0_attn_a_x, weights):
    x = np.asarray(L0_attn_a_x)
    s, t = np.meshgrid(np.arange(CH), np.arange(CH), indexing="ij")
    maskT = (t >= s).astype(BFNP)
    idb = np.eye(128, dtype=BFNP)
    ones1 = np.ones((128, 1), dtype=BFNP)
    onesF = np.ones((1, 128), dtype=np.float32)
    hsum2 = np.zeros((128, 2), dtype=BFNP)
    hsum2[0:64, 0] = 1
    hsum2[64:128, 1] = 1
    inv_freq = 1.0 / ROPE_THETA ** (np.arange(0, AD, 2, dtype=np.float32) / AD)

    in_maps = []
    for c in range(NCORES):
        b, q = c // 4, c % 4
        t0 = q * T
        own = x[b, t0:t0 + T]
        if t0 == 0:
            halo = np.zeros((HALO, D), dtype=np.float32)
        else:
            halo = x[b, t0 - HALO:t0]
        pos = np.arange(t0, t0 + T, dtype=np.float32)
        ang = np.outer(inv_freq, pos)
        cosT = np.tile(np.concatenate([np.cos(ang), np.cos(ang)], 0), (2, 1))
        sinT = np.tile(np.concatenate([np.sin(ang), np.sin(ang)], 0), (2, 1))
        m = {
            "xT": np.ascontiguousarray(own.T),
            "xTh": np.ascontiguousarray(halo.T),
            "maskT": maskT, "idb": idb, "ones1": ones1, "onesF": onesF,
            "hsum2": hsum2,
            "cosT": np.ascontiguousarray(cosT, dtype=np.float32),
            "sinT": np.ascontiguousarray(sinT, dtype=np.float32),
            "fconst": np.array([[EPS_LN, EPS_RMS, 0.0, 0.0]], dtype=np.float32),
            "sgn": np.ascontiguousarray(
                np.where((np.arange(128) % 64) < 32, -1.0, 1.0)
                .astype(np.float32).reshape(128, 1)),
        }
        m.update(weights)
        in_maps.append(m)
    return in_maps


def kernel(L0_attn_a_x, L0_attn_a_Wq, L0_attn_a_Wk, L0_attn_a_Wv, L0_attn_a_Wo,
           L0_attn_a_norm_w, L0_attn_b_Wq, L0_attn_b_Wk, L0_attn_b_Wv,
           L0_attn_b_Wo, L0_ffn_w_gate, L0_ffn_w_up, L0_ffn_w_down):
    tr = lambda w: np.ascontiguousarray(np.asarray(w).T)
    weights = {
        "aWqT": tr(L0_attn_a_Wq), "aWkT": tr(L0_attn_a_Wk),
        "aWvT": tr(L0_attn_a_Wv), "aWoT": tr(L0_attn_a_Wo),
        "anw": np.ascontiguousarray(
            np.asarray(L0_attn_a_norm_w).reshape(NDT, 128).T),
        "bWqT": tr(L0_attn_b_Wq), "bWkT": tr(L0_attn_b_Wk),
        "bWvT": tr(L0_attn_b_Wv), "bWoT": tr(L0_attn_b_Wo),
        "fWgT": tr(L0_ffn_w_gate), "fWuT": tr(L0_ffn_w_up),
        "fWdT": tr(L0_ffn_w_down),
    }
    in_maps = _host_inputs(L0_attn_a_x, weights)
    nc = build_program()
    res = run_bass_kernel_spmd(nc, in_maps, core_ids=list(range(NCORES)))
    out = np.empty((B, N, D), dtype=np.float32)
    for c in range(NCORES):
        b, q = c // 4, c % 4
        out[b, q * T:(q + 1) * T] = res.results[c]["oT"].T
    return out


# version 04
# speedup vs baseline: 1.2464x; 1.2207x over previous
import numpy as np
import ml_dtypes
from contextlib import ExitStack

import concourse.bass as bass
import concourse.mybir as mybir
from concourse.bass_utils import run_bass_kernel_spmd

F32 = mybir.dt.float32
BF16 = mybir.dt.bfloat16
AF = mybir.ActivationFunctionType
ALU = mybir.AluOpType
BFNP = ml_dtypes.bfloat16

B, N, D = 2, 4096, 2048
NCORES = 8
T = 1024
HALO = 128
TH = T + HALO
NDT = 16
CH = 64
GLA_H = 16
NCH = TH // CH
HCH = HALO // CH
FFN_H = 8192
AH, AKVH, AD = 32, 8, 64
EPS_LN = 1e-5
EPS_RMS = 1e-6
ROPE_THETA = 10000.0

ENGINES = ("sync", "gpsimd", "vector", "scalar", "tensor")

NSUB = 4
DBG = None


class Ch:
    def __init__(self, handle):
        self.h = handle
        self.n = 0

    def bump(self, k):
        self.n += k
        return self.n


class Sched:
    def __init__(self):
        self.q = {e: [] for e in ENGINES}

    def emit(self, eng_name, fn, wait=(), inc=()):
        inc_t = list(inc)
        for c, k in inc_t:
            c.bump(k)
        wait_t = [(c, v) for (c, v) in wait if v is not None]

        def closure(eng):
            for c, v in wait_t:
                eng.wait_ge(c.h, v)
            inst = fn(eng)
            for c, k in inc_t:
                assert inst is not None
                inst.then_inc(c.h, k)

        self.q[eng_name].append(closure)


def build_program():
    nc = bass.Bass("TRN2", target_bir_lowering=False)

    def din(name, shape, dtype=F32):
        return nc.dram_tensor(name, list(shape), dtype, kind="ExternalInput").ap()

    xT_in = din("xT", [D, T])
    xTh_in = din("xTh", [D, HALO])
    aWqT = din("aWqT", [D, D]); aWkT = din("aWkT", [D, D])
    aWvT = din("aWvT", [D, D]); aWoT = din("aWoT", [D, D])
    anw = din("anw", [128, NDT])
    bWqT = din("bWqT", [D, D]); bWkT = din("bWkT", [D, 512])
    bWvT = din("bWvT", [D, 512]); bWoT = din("bWoT", [D, D])
    fWgT = din("fWgT", [D, FFN_H]); fWuT = din("fWuT", [D, FFN_H])
    fWdT = din("fWdT", [FFN_H, D])
    maskT_in = din("maskT", [CH, CH], BF16)
    idb_in = din("idb", [128, 128], BF16)
    ones1_in = din("ones1", [128, 1], BF16)
    onesF_in = din("onesF", [1, 128], F32)
    hsum2_in = din("hsum2", [128, 2], BF16)
    cosT_in = din("cosT", [128, T], F32)
    sinT_in = din("sinT", [128, T], F32)
    fconst_in = din("fconst", [1, 4], F32)
    sgn_in = din("sgn", [128, 1], F32)

    out_d = nc.dram_tensor("oT", [D, T], F32, kind="ExternalOutput").ap()

    fwg_bf = nc.dram_tensor("fwg_bf", [D, FFN_H], BF16, kind="Internal")
    fwu_bf = nc.dram_tensor("fwu_bf", [D, FFN_H], BF16, kind="Internal")
    fwd_bf = nc.dram_tensor("fwd_bf", [FFN_H, D], BF16, kind="Internal")
    cc_in_k = nc.dram_tensor("cc_in_k", [513, T], BF16, kind="Internal")
    cc_out_k = nc.dram_tensor("cc_out_k", [4 * 513, T], BF16, kind="Internal")
    cc_in_v = nc.dram_tensor("cc_in_v", [T, 512], BF16, kind="Internal")
    cc_out_v = nc.dram_tensor("cc_out_v", [4 * T, 512], BF16, kind="Internal")
    RG = [[0, 1, 2, 3], [4, 5, 6, 7]]

    S = Sched()

    with ExitStack() as stack:
        e = stack.enter_context

        chans = {}
        for name in ("dIn", "dW", "dOut", "dG", "pe", "act", "dve", "gp", "cc"):
            chans[name] = Ch(e(nc.semaphore(name)))
        dIn, dW, dOut, dG = chans["dIn"], chans["dW"], chans["dOut"], chans["dG"]
        cPE, cACT, cDVE, cGP = chans["pe"], chans["act"], chans["dve"], chans["gp"]
        cCC = chans["cc"]

        sb = lambda name, shape, dt: e(nc.sbuf_tensor(name, list(shape), dt))
        xT = sb("xT_sb", [128, NDT * T], F32)
        xn = sb("xn_sb", [128, NDT * T], BF16)
        xnh = sb("xnh_sb", [128, NDT * HALO], BF16)
        maskT = sb("maskT_sb", [CH, CH], BF16)
        idb = sb("idb_sb", [128, 128], BF16)
        ones1 = sb("ones1_sb", [128, 1], BF16)
        onesF = sb("onesF_sb", [1, 128], F32)
        hsum2 = sb("hsum2_sb", [128, 2], BF16)
        fconst = sb("fconst_sb", [1, 4], F32)
        anw_sb = sb("anw_sb", [128, NDT], F32)
        zer64 = sb("zer64_sb", [128, CH], F32)
        sqs = [sb(f"lnsq{i}", [128, 512], BF16) for i in range(2)]
        stS = sb("ln_stS", [1, 1536], F32)
        nsc = sb("ln_nsc", [128, 512], F32)
        wslab = [sb(f"wslab{i}", [128, 16 * 256], BF16) for i in range(2)]

        ps = lambda name, shape, dt=F32: e(nc.psum_tensor(name, list(shape), dt))
        PA = ps("PA", [128, 1024])
        PB = ps("PB", [128, 1024])
        PC = ps("PC", [128, 512])
        PD = ps("PD", [128, 512])
        PE_ = ps("PE", [128, 512])
        PF = ps("PF", [128, 1024], BF16)

        def dma(engn, dch, out_ap, in_ap, extra_wait=()):
            S.emit(engn, lambda eng, o=out_ap, i=in_ap: eng.dma_start(out=o, in_=i),
                   wait=extra_wait, inc=[(dch, 16)])

        x3 = xT[:].rearrange("p (d t) -> p d t", d=NDT)
        xn3 = xn[:].rearrange("p (d t) -> p d t", d=NDT)
        xnh3 = xnh[:].rearrange("p (d t) -> p d t", d=NDT)

        allch = list(chans.values())

        def barrier():
            marks = [(c, c.n) for c in allch]
            for en in ENGINES:
                S.emit(en, lambda eng: None, wait=marks)

        ln_state = {"sums_free": None, "pa_free": None,
                    "sq_guard": [None, None], "nsc_free": None,
                    "stat_free": []}

        def emit_norm(src3, dst3, ntok, rms=False, extra_wait=()):
            first = True
            c0 = 0
            stm = stS[0:1, 0:512]
            ste = stS[0:1, 512:1024]
            stt = stS[0:1, 1024:1536]
            while c0 < ntok:
                n = min(512, ntok - c0)
                for dt in range(NDT):
                    if not rms:
                        w = list(extra_wait) if (first and dt == 0) else []
                        S.emit("vector",
                               lambda eng, d=dt, c=c0, nn=n: eng.tensor_copy(
                                   dst3[:, d, c:c + nn], src3[:, d, c:c + nn]),
                               wait=w, inc=[(cDVE, 1)])
                        cast_m = cDVE.n
                        sq_src3 = dst3
                    else:
                        cast_m = None
                        sq_src3 = src3
                    w = ([] if cast_m is None else [(cDVE, cast_m)])
                    if first and dt == 0:
                        w += list(extra_wait) if rms else []
                    if ln_state["sq_guard"][dt % 2] is not None:
                        w.append((cPE, ln_state["sq_guard"][dt % 2]))
                    S.emit("scalar",
                           lambda eng, d=dt, c=c0, nn=n, s3=sq_src3:
                           eng.activation(sqs[d % 2][:, 0:nn], s3[:, d, c:c + nn],
                                          AF.Square),
                           wait=w, inc=[(cACT, 1)])
                    sq_m = cACT.n
                    if not rms:
                        w = [(cDVE, cast_m)]
                        if dt == 0 and ln_state["sums_free"] is not None:
                            w.append(ln_state["sums_free"])
                        S.emit("tensor",
                               lambda eng, d=dt, c=c0, nn=n: eng.matmul(
                                   PC[0:1, 0:nn], ones1[:],
                                   dst3[:, d, c:c + nn],
                                   start=(d == 0), stop=(d == NDT - 1)),
                               wait=w, inc=[(cPE, 1)] if dt == NDT - 1 else [])
                        if dt == NDT - 1:
                            m_sx = cPE.n
                    w = [(cACT, sq_m)]
                    if dt == 0 and ln_state["sums_free"] is not None:
                        w.append(ln_state["sums_free"])
                    S.emit("tensor",
                           lambda eng, d=dt, nn=n: eng.matmul(
                               PD[0:1, 0:nn], ones1[:], sqs[d % 2][:, 0:nn],
                               start=(d == 0), stop=(d == NDT - 1)),
                           wait=w, inc=[(cPE, 1)])
                    ln_state["sq_guard"][dt % 2] = cPE.n
                m_sx2 = cPE.n
                first = False
                S.emit("scalar", lambda eng, nn=n: eng.mul(
                    ste[:, 0:nn], PD[0:1, 0:nn], 1.0 / D),
                       wait=[(cPE, m_sx2)] + ln_state["stat_free"],
                       inc=[(cACT, 1)])
                a2 = cACT.n
                if not rms:
                    S.emit("scalar", lambda eng, nn=n: eng.mul(
                        stm[:, 0:nn], PC[0:1, 0:nn], 1.0 / D),
                           wait=[(cPE, m_sx)] + ln_state["stat_free"],
                           inc=[(cACT, 1)])
                    a1 = cACT.n
                    ln_state["sums_free"] = (cACT, a1)
                    S.emit("vector", lambda eng, nn=n: eng.tensor_mul(
                        stt[:, 0:nn], stm[:, 0:nn], stm[:, 0:nn]),
                           wait=[(cACT, a1)], inc=[(cDVE, 1)])
                    v1 = cDVE.n
                    S.emit("vector", lambda eng, nn=n: eng.tensor_sub(
                        ste[:, 0:nn], ste[:, 0:nn], stt[:, 0:nn]),
                           wait=[(cACT, a2), (cDVE, v1)], inc=[(cDVE, 1)])
                    v2 = cDVE.n
                    eps_i = 0
                    var_ap = ste
                    wln = [(cDVE, v2)]
                else:
                    ln_state["sums_free"] = (cACT, a2)
                    eps_i = 1
                    var_ap = ste
                    wln = [(cACT, a2)]
                S.emit("scalar", lambda eng, nn=n, va=var_ap, ei=eps_i:
                       eng.activation(stt[:, 0:nn], va[:, 0:nn], AF.Ln,
                                      bias=fconst[0:1, ei:ei + 1]),
                       wait=wln, inc=[(cACT, 1)])
                a3 = cACT.n
                S.emit("scalar", lambda eng, nn=n: eng.activation(
                    ste[:, 0:nn], stt[:, 0:nn], AF.Exp, scale=-0.5),
                       wait=[(cACT, a3)], inc=[(cACT, 1)])
                a4 = cACT.n
                if not rms:
                    S.emit("vector", lambda eng, nn=n: eng.scalar_tensor_tensor(
                        stt[:, 0:nn], stm[:, 0:nn], -1.0,
                        ste[:, 0:nn], ALU.mult, ALU.mult),
                           wait=[(cACT, a4)], inc=[(cDVE, 1)])
                    v3 = cDVE.n
                w = [(cACT, a4)]
                if ln_state["pa_free"] is not None:
                    w.append(ln_state["pa_free"])
                S.emit("tensor", lambda eng, nn=n: eng.matmul(
                    PA[:, 0:nn], onesF[:], ste[:, 0:nn], start=True, stop=True),
                       wait=w, inc=[(cPE, 1)])
                p1 = cPE.n
                if not rms:
                    S.emit("tensor", lambda eng, nn=n: eng.matmul(
                        PA[:, 512:512 + nn], onesF[:], stt[:, 0:nn],
                        start=True, stop=True),
                           wait=[(cDVE, v3)], inc=[(cPE, 1)])
                    p2 = cPE.n
                ln_state["stat_free"] = [(cPE, p1)] + (
                    [(cPE, p2), (cDVE, v3)] if not rms else [])
                for dt in range(NDT):
                    w = [(cPE, p1)] if dt == 0 else []
                    if ln_state["nsc_free"] is not None:
                        w.append((cDVE, ln_state["nsc_free"]))
                    S.emit("vector",
                           lambda eng, d=dt, c=c0, nn=n: eng.tensor_mul(
                               nsc[:, 0:nn], src3[:, d, c:c + nn], PA[:, 0:nn]),
                           wait=w, inc=[(cDVE, 1)])
                    vm = cDVE.n
                    if not rms:
                        w2 = [(cDVE, vm)] + ([(cPE, p2)] if dt == 0 else [])
                        S.emit("vector",
                               lambda eng, d=dt, c=c0, nn=n: eng.tensor_add(
                                   dst3[:, d, c:c + nn], nsc[:, 0:nn],
                                   PA[:, 512:512 + nn]),
                               wait=w2, inc=[(cDVE, 1)])
                    else:
                        S.emit("vector",
                               lambda eng, d=dt, c=c0, nn=n: eng.tensor_scalar_mul(
                                   dst3[:, d, c:c + nn], nsc[:, 0:nn],
                                   anw_sb[:, d:d + 1]),
                               wait=[(cDVE, vm)], inc=[(cDVE, 1)])
                    ln_state["nsc_free"] = cDVE.n
                ln_state["pa_free"] = (cDVE, cDVE.n)
                c0 += n
            return (cDVE, cDVE.n)

        slab_st = {"mark": [None, None], "i": 0}
        wsl3 = [w[:].rearrange("p (k c) -> p k c", c=256) for w in wslab]

        def load_slab(wdram, col0, ncols, kdim=16, kstep=1):
            i = slab_st["i"]
            slab_st["i"] ^= 1
            w = []
            if slab_st["mark"][i] is not None:
                w.append(slab_st["mark"][i])
            src = wdram.rearrange("(k p) c -> p k c", p=128)[:, 0:kdim, col0:col0 + ncols]
            b = wsl3[i]
            S.emit("gpsimd",
                   lambda eng, bb=b, ss=src, kd=kdim, nn=ncols: eng.dma_start(
                       out=bb[:, 0:kd, 0:nn], in_=ss),
                   wait=w, inc=[(dW, 16)])
            return b, (dW, dW.n), i

        def slab_done(i, mark):
            slab_st["mark"][i] = mark

        pc_state = {"done": None}

        def precast_chunks():
            jobs = []
            for wsrc, wdst in ((fWgT, fwg_bf), (fWuT, fwu_bf)):
                for c0 in range(0, FFN_H, 256):
                    jobs.append((wsrc, wdst, 0, c0))
            for kq in range(4):
                for c0 in range(0, D, 256):
                    jobs.append((fWdT, fwd_bf, kq * 16, c0))
            for wsrc, wdst, k0, c0 in jobs:
                b, rdy, i = load_slab(
                    wsrc.rearrange("(a p) c -> a p c", p=128)[
                        k0:k0 + 16].rearrange("a p c -> (a p) c"), c0, 256)
                S.emit("sync",
                       lambda eng, bb=b, wd=wdst, kk=k0, cc=c0: eng.dma_start(
                           out=wd.ap().rearrange("(k p) c -> p k c", p=128)[
                               :, kk:kk + 16, cc:cc + 256],
                           in_=bb[:, 0:16, 0:256]),
                       wait=[rdy], inc=[(dG, 16)])
                slab_done(i, (dG, dG.n))
                yield
            pc_state["done"] = (dG, dG.n)

        with ExitStack() as ph1:
            xTh = ph1.enter_context(
                nc.sbuf_tensor("xTh_sb", [128, NDT * HALO], F32, side="right"))
            xh3 = xTh[:].rearrange("p (d t) -> p d t", d=NDT)
            dma("sync", dIn, x3, xT_in.rearrange("(d p) t -> p d t", p=128))
            dma("sync", dIn, xh3, xTh_in.rearrange("(d p) t -> p d t", p=128))
            for dst, src in ((maskT, maskT_in), (idb, idb_in), (ones1, ones1_in),
                             (onesF, onesF_in), (hsum2, hsum2_in),
                             (anw_sb, anw), (fconst, fconst_in)):
                dma("sync", dIn, dst[:], src)
            S.emit("vector", lambda eng: eng.memset(zer64[:], 0.0), inc=[(cDVE, 1)])
            in_done = dIn.n

            ln1_own = emit_ln_marker = emit_norm(
                x3, xn3, T, extra_wait=[(dIn, in_done)])
            ln1_halo = emit_norm(xh3, xnh3, HALO)

        dbg_hold = {}

        def emit_attn_a():
            with ExitStack() as ph:
                er = lambda nm, shape, dt: ph.enter_context(
                    nc.sbuf_tensor(nm, list(shape), dt, side="right"))
                oTt = er("gla_oT", [128, GLA_H * T], BF16)
                fT = er("gla_f", [128, 2 * TH], F32)
                nlfT = er("gla_nlf", [128, 2 * TH], F32)
                usc = er("gla_usc", [128, 288], F32)
                kT = er("gla_kT", [128, 2 * TH], BF16)
                qT = er("gla_qT", [128, 2 * T], BF16)
                vU = er("gla_vU", [128, 9 * 256], BF16)
                Sst = er("gla_S", [128, 2 * 128], BF16)
                ct = er("gla_ct", [128, 8 * CH], F32)
                eCp = er("gla_eCp", [128, 8 * CH], F32)
                eCn = er("gla_eCn", [128, 8 * CH], F32)
                qdec = er("gla_qdec", [128, 8 * CH], BF16)
                kdec = er("gla_kdec", [128, 8 * CH], BF16)
                kdecU = er("gla_kdecU", [128, 4 * 128], BF16)
                eClb = er("gla_eClb", [128, 16], F32)
                pm = er("gla_pm", [128, 4 * CH], BF16)
                qsg = er("gla_qsg", [128, 512], BF16)
                ssc = er("gla_ssc", [128, 256], BF16)
                o3 = oTt[:].rearrange("p (h t) -> p h t", h=GLA_H)
                dbg_hold["o"] = o3
                f3 = fT[:].rearrange("p (h t) -> p h t", h=2)
                nlf3 = nlfT[:].rearrange("p (h t) -> p h t", h=2)
                k3 = kT[:].rearrange("p (h t) -> p h t", h=2)
                q3 = qT[:].rearrange("p (h t) -> p h t", h=2)
                v3 = vU[:].rearrange("p (tt e) -> p tt e", e=256)

                segs = [(xnh3, 0, HALO, 0), (xn3, 0, 512, HALO),
                        (xn3, 512, 512, HALO + 512)]
                pp = [PC, PD]
                pp_mark = [None, None]
                ppi = [0]

                def proj_accum(slab3, octile, seg, extra_w=()):
                    srcT, sc0, sn, _ = seg
                    i = ppi[0]
                    ppi[0] ^= 1
                    psb = pp[i]
                    for k in range(NDT):
                        w = list(extra_w) if k == 0 else []
                        if k == 0 and pp_mark[i] is not None:
                            w.append(pp_mark[i])
                        S.emit("tensor",
                               lambda eng, kk=k, s3=slab3, oc=octile, st=srcT,
                                      c=sc0, nn=sn, pb=psb: eng.matmul(
                                   pb[:, 0:nn], s3[:, kk, oc * 128:(oc + 1) * 128],
                                   st[:, kk, c:c + nn],
                                   start=(kk == 0), stop=(kk == NDT - 1)),
                               wait=w, inc=[(cPE, 1)] if k == NDT - 1 else [])
                    return psb, i, (cPE, cPE.n)

                hm = [dict(), dict()]
                gl_pf = [None]
                glast = {"nlf_r": None, "k_r": None, "q_r": None, "v_r": None,
                         "f_r": None}

                pcg = precast_chunks()
                for g in range(8):
                    zsl, zrdy, zi = load_slab(aWkT, g * 256, 256)
                    qsl, qrdy, qi = load_slab(aWqT, g * 256, 256)
                    f_marks = []
                    for oc in range(2):
                        for seg in segs:
                            psb, i, pmk = proj_accum(zsl, oc, seg,
                                                     extra_w=[zrdy, ln1_own,
                                                              ln1_halo])
                            toff = seg[3]
                            w = [pmk]
                            if glast["f_r"] is not None:
                                w.append(glast["f_r"])
                            S.emit("scalar",
                                   lambda eng, o=oc, to=toff, nn=seg[2], pb=psb:
                                   eng.activation(f3[:, o, to:to + nn],
                                                  pb[:, 0:nn], AF.Sigmoid),
                                   wait=w, inc=[(cACT, 1)])
                            f_marks.append(cACT.n)
                            pp_mark[i] = (cACT, cACT.n)
                    slab_done(zi, (cPE, cPE.n))
                    k_marks = []
                    for hh in range(2):
                        w = [(cACT, f_marks[-1])]
                        if glast["k_r"] is not None:
                            w.append(glast["k_r"])
                        S.emit("scalar",
                               lambda eng, h=hh: eng.activation(
                                   k3[:, h, :], f3[:, h, :], AF.Sigmoid,
                                   scale=-1.0),
                               wait=w, inc=[(cACT, 1)])
                        k_marks.append(cACT.n)
                    segs_own = [(xn3, 0, 512, 0), (xn3, 512, 512, 512)]
                    qsg_free = None
                    for oc in range(2):
                        for seg in segs_own:
                            psb, i, pmk = proj_accum(qsl, oc, seg,
                                                     extra_w=[qrdy])
                            w = [pmk] + ([qsg_free] if qsg_free else [])
                            S.emit("scalar",
                                   lambda eng, nn=seg[2], pb=psb: eng.activation(
                                       qsg[:, 0:nn], pb[:, 0:nn], AF.Sigmoid),
                                   wait=w, inc=[(cACT, 1)])
                            am = cACT.n
                            w = [(cACT, am)]
                            if glast["q_r"] is not None and oc == 0 and seg[1] == 0:
                                w.append(glast["q_r"])
                            S.emit("vector",
                                   lambda eng, o=oc, to=seg[3], nn=seg[2], pb=psb:
                                   eng.tensor_mul(q3[:, o, to:to + nn],
                                                  pb[:, 0:nn], qsg[:, 0:nn]),
                                   wait=w, inc=[(cDVE, 1)])
                            qsg_free = (cDVE, cDVE.n)
                            pp_mark[i] = (cDVE, cDVE.n)
                    slab_done(qi, (cPE, cPE.n))
                    vsl, vrdy, vi = load_slab(aWvT, g * 256, 256)
                    for tt in range(9):
                        if tt == 0:
                            lhs_src, lc = xnh3, 0
                        else:
                            lhs_src, lc = xn3, (tt - 1) * 128
                        i = ppi[0]
                        ppi[0] ^= 1
                        psb = pp[i]
                        for k in range(NDT):
                            w = [vrdy] if k == 0 else []
                            if k == 0 and pp_mark[i] is not None:
                                w.append(pp_mark[i])
                            S.emit("tensor",
                                   lambda eng, kk=k, ls=lhs_src, c=lc, pb=psb,
                                          vs=vsl: eng.matmul(
                                       pb[:, 0:256], ls[:, kk, c:c + 128],
                                       vs[:, kk, 0:256],
                                       start=(kk == 0), stop=(kk == NDT - 1)),
                                   wait=w, inc=[(cPE, 1)] if k == NDT - 1 else [])
                        pmk = (cPE, cPE.n)
                        w = [pmk]
                        if glast["v_r"] is not None and tt == 0:
                            w.append(glast["v_r"])
                        S.emit("scalar",
                               lambda eng, t=tt, pb=psb: eng.copy(
                                   v3[:, t, :], pb[:, 0:256]),
                               wait=w, inc=[(cACT, 1)])
                        pp_mark[i] = (cACT, cACT.n)
                    slab_done(vi, (cPE, cPE.n))
                    v_evac_m = cACT.n
                    nlf_marks = []
                    for hh in range(2):
                        for half in range(4):
                            c0h = half * 288
                            w = [(cACT, f_marks[-1])]
                            S.emit("scalar",
                                   lambda eng, h=hh, c=c0h: eng.activation(
                                       usc[:, 0:288], f3[:, h, c:c + 288],
                                       AF.Exp, scale=-1.0),
                                   wait=w, inc=[(cACT, 1)])
                            um = cACT.n
                            w = [(cACT, um)]
                            if glast["nlf_r"] is not None:
                                w.append(glast["nlf_r"])
                            S.emit("scalar",
                                   lambda eng, h=hh, c=c0h: eng.activation(
                                       nlf3[:, h, c:c + 288], usc[:, 0:288],
                                       AF.Ln, bias=1.0),
                                   wait=w, inc=[(cACT, 1)])
                            nlf_marks.append(cACT.n)
                    nlf_m = cACT.n

                    for hh in range(2):
                        w = []
                        if hm[hh].get("S_mm") is not None:
                            w += [(cPE, hm[hh]["S_mm"]), (cDVE, hm[hh]["S_scale"]),
                                  (cPE, hm[hh]["o_mm"])]
                        S.emit("vector",
                               lambda eng, h=hh: eng.memset(
                                   Sst[:, h * 128:(h + 1) * 128], 0.0),
                               wait=w, inc=[(cDVE, 1)])
                        hm[hh]["S_scale"] = cDVE.n

                    M = [{k: [None] * NCH for k in
                          ("scan", "eCp", "eCn", "eCl", "kdec", "tr", "evac",
                           "qdec", "scores", "mask", "o_mm", "S_mm", "S_add",
                           "S_scale", "oevac")} for _ in range(2)]
                    P = [hm[0], hm[1]]
                    scb = {"mask": P[0].get("sc_mask")}

                    def g4(arr, c):
                        return arr[c] if c >= 0 else None

                    def sub_scan(c, hh):
                        m = M[hh]
                        o = ct[:, (hh * 4 + c % 4) * CH:(hh * 4 + c % 4 + 1) * CH]
                        w = [(cACT, nlf_m)]
                        for mk in (g4(m["eCp"], c - 4), g4(m["eCn"], c - 4)):
                            if mk is not None:
                                w.append((cACT, mk))
                        if c < 4 and P[hh].get("ct_r") is not None:
                            w.append(P[hh]["ct_r"])
                        S.emit("vector",
                               lambda eng, h=hh, cc=c, oo=o:
                               eng.tensor_tensor_scan(
                                   oo, nlf3[:, h, cc * CH:(cc + 1) * CH],
                                   zer64[:, 0:CH], 0.0, ALU.add, ALU.add),
                               wait=w, inc=[(cDVE, 1)])
                        m["scan"][c] = cDVE.n

                    def sub_exps(c, hh):
                        m = M[hh]
                        base = (hh * 4 + c % 4) * CH
                        ctS = ct[:, base:base + CH]
                        w = [(cDVE, m["scan"][c])]
                        mk = g4(m["kdec"], c - 4)
                        if mk is not None:
                            w.append((cDVE, mk))
                        if c < 4 and P[hh].get("exp_r") is not None:
                            w.append(P[hh]["exp_r"])
                        S.emit("scalar",
                               lambda eng, o=eCp[:, base:base + CH], i0=ctS:
                               eng.activation(o, i0, AF.Exp),
                               wait=w, inc=[(cACT, 1)])
                        m["eCp"][c] = cACT.n
                        w = [(cDVE, m["scan"][c])]
                        mk = g4(m["qdec"], c - 4)
                        if mk is not None:
                            w.append((cDVE, mk))
                        mk = g4(m["eCl"], c - 4)
                        if mk is not None:
                            w.append((cDVE, mk))
                        if c < 4 and P[hh].get("exp_r") is not None:
                            w.append(P[hh]["exp_r"])
                        S.emit("scalar",
                               lambda eng, o=eCn[:, base:base + CH], i0=ctS:
                               eng.activation(o, i0, AF.Exp, scale=-1.0),
                               wait=w, inc=[(cACT, 1)])
                        m["eCn"][c] = cACT.n
                        w = [(cACT, m["eCn"][c])]
                        mk = g4(m["S_scale"], c - 8)
                        if mk is not None:
                            w.append((cDVE, mk))
                        if c < 8 and P[hh].get("eCl_r") is not None:
                            w.append(P[hh]["eCl_r"])
                        S.emit("vector",
                               lambda eng, h=hh, cc=c, ib=base: eng.tensor_copy(
                                   eClb[:, h * 8 + cc % 8:h * 8 + cc % 8 + 1],
                                   eCn[:, ib + CH - 1:ib + CH]),
                               wait=w, inc=[(cDVE, 1)])
                        m["eCl"][c] = cDVE.n

                    def sub_kdec(c, hh):
                        m = M[hh]
                        base = (hh * 4 + c % 4) * CH
                        w = [(cACT, m["eCp"][c]), (cACT, k_marks[hh])]
                        for mk in (g4(m["tr"], c - 4), g4(m["scores"], c - 4)):
                            if mk is not None:
                                w.append((cPE, mk))
                        if c < 4 and P[hh].get("kdec_r") is not None:
                            w.append(P[hh]["kdec_r"])
                        S.emit("vector",
                               lambda eng, h=hh, cc=c,
                                      o=kdec[:, base:base + CH],
                                      i1=eCp[:, base:base + CH]:
                               eng.tensor_mul(
                                   o, k3[:, h, cc * CH:(cc + 1) * CH], i1),
                               wait=w, inc=[(cDVE, 1)])
                        m["kdec"][c] = cDVE.n

                    def sub_tr(c, hh):
                        m = M[hh]
                        base = (hh * 4 + c % 4) * CH
                        w = [(cDVE, m["kdec"][c]), (dIn, in_done)]
                        if gl_pf[0] is not None:
                            w.append(gl_pf[0])
                        S.emit("tensor",
                               lambda eng, h=hh, i0=kdec[:, base:base + CH]:
                               eng.transpose(
                                   PF[0:64, h * 128:h * 128 + 128], i0, idb[:]),
                               wait=w, inc=[(cPE, 1)])
                        m["tr"][c] = cPE.n
                        bbu = (c % 4) * 32
                        bbu = (c % 2) * 64
                        w = [(cPE, m["tr"][c])]
                        mk = g4(m["S_mm"], c - 4)
                        if mk is not None:
                            w.append((cPE, mk))
                        if c < 4 and P[hh].get("kdecU_r") is not None:
                            w.append(P[hh]["kdecU_r"])
                        S.emit("vector",
                               lambda eng, h=hh, b=bbu, d4_=(c % 4) // 2:
                               eng.tensor_copy(
                                   kdecU[b:b + 64,
                                         (h * 2 + d4_) * 128:
                                         (h * 2 + d4_ + 1) * 128],
                                   PF[0:64, h * 128:h * 128 + 128]),
                               wait=w, inc=[(cDVE, 1)])
                        m["evac"][c] = cDVE.n
                        gl_pf[0] = (cDVE, cDVE.n)
                        if c >= HCH:
                            co = c - HCH
                            w = [(cACT, m["eCn"][c])]
                            mk = g4(m["o_mm"], c - 4)
                            if mk is not None:
                                w.append((cPE, mk))
                            if c < HCH + 4 and P[hh].get("qdec_r") is not None:
                                w.append(P[hh]["qdec_r"])
                            S.emit("vector",
                                   lambda eng, h=hh, t0=co * CH,
                                          o=qdec[:, base:base + CH],
                                          i1=eCn[:, base:base + CH]:
                                   eng.tensor_mul(
                                       o, q3[:, h, t0:t0 + CH], i1),
                                   wait=w, inc=[(cDVE, 1)])
                            m["qdec"][c] = cDVE.n

                    def sub_scores(c, hh):
                        if c < HCH:
                            return
                        m = M[hh]
                        base = (hh * 4 + c % 4) * CH
                        pr = c % 2
                        w = [(cDVE, m["qdec"][c]), (cDVE, m["kdec"][c])]
                        if scb["mask"] is not None:
                            w.append(scb["mask"])
                        S.emit("tensor",
                               lambda eng, h=hh, i0=kdec[:, base:base + CH],
                                      i1=qdec[:, base:base + CH]: eng.matmul(
                                   PE_[0:CH, h * CH:(h + 1) * CH],
                                   i0, i1, start=True, stop=True),
                               wait=w, inc=[(cPE, 1)])
                        m["scores"][c] = cPE.n
                        col = ((c % 2) * 2 + hh) * CH
                        wmk = [(cPE, m["scores"][c])]
                        mk = g4(m["o_mm"], c - 2)
                        if mk is not None:
                            wmk.append((cPE, mk))
                        elif P[hh].get("pm_r") is not None:
                            wmk.append(P[hh]["pm_r"])
                        S.emit("vector",
                               lambda eng, cl=col, h=hh: eng.tensor_mul(
                                   pm[0:CH, cl:cl + CH],
                                   PE_[0:CH, h * CH:(h + 1) * CH],
                                   maskT[:]),
                               wait=wmk, inc=[(cDVE, 1)])
                        m["mask"][c] = cDVE.n
                        scb["mask"] = (cDVE, cDVE.n)
                        if pr:
                            S.emit("vector",
                                   lambda eng, cl=col: eng.tensor_copy(
                                       pm[64:64 + CH, cl:cl + CH],
                                       pm[0:CH, cl:cl + CH]),
                                   wait=[(cDVE, m["mask"][c])], inc=[(cDVE, 1)])
                            m["mask"][c] = cDVE.n

                    def sub_oS(c, hh):
                        m = M[hh]
                        base = (hh * 4 + c % 4) * CH
                        pr = c % 2
                        bbu = pr * 64
                        own = c >= HCH
                        if own:
                            w = [(cDVE, m["mask"][c]), (cACT, v_evac_m)]
                            mk = g4(m["oevac"], c - 1)
                            if mk is not None:
                                w.append((cACT, mk))
                            elif P[hh].get("oevac") is not None:
                                w.append(P[hh]["oevac"])
                            col2 = ((c % 2) * 2 + hh) * CH
                            S.emit("tensor",
                                   lambda eng, h=hh, cc=c, b=bbu, cl=col2:
                                   eng.matmul(
                                       PA[:, h * 512:h * 512 + CH],
                                       v3[b:b + 64, cc // 2,
                                          h * 128:(h + 1) * 128],
                                       pm[b:b + CH, cl:cl + CH],
                                       start=True, stop=False),
                                   wait=w)
                            wS = g4(m["S_scale"], c - 1)
                            wS = [(cDVE, wS)] if wS is not None else (
                                [P[hh]["S_scale0"]] if P[hh].get("S_scale0")
                                else [])
                            S.emit("tensor",
                                   lambda eng, h=hh,
                                          i1=qdec[:, base:base + CH]:
                                   eng.matmul(
                                       PA[:, h * 512:h * 512 + CH],
                                       Sst[:, h * 128:(h + 1) * 128],
                                       i1, start=False, stop=True),
                                   wait=wS, inc=[(cPE, 1)])
                            m["o_mm"][c] = cPE.n
                        w = [(cDVE, m["evac"][c]), (cACT, v_evac_m)]
                        mk = g4(m["S_add"], c - 1)
                        if mk is not None:
                            w.append((cDVE, mk))
                        elif P[hh].get("S_add") is not None:
                            w.append(P[hh]["S_add"])
                        S.emit("tensor",
                               lambda eng, h=hh, cc=c, b=bbu, d4_=(c % 4) // 2:
                               eng.matmul(
                                   PB[:, h * 512:h * 512 + 128],
                                   kdecU[b:b + 64, (h * 2 + d4_) * 128:
                                         (h * 2 + d4_ + 1) * 128],
                                   v3[b:b + 64, cc // 2,
                                      h * 128:(h + 1) * 128],
                                   start=True, stop=True),
                               wait=w, inc=[(cPE, 1)])
                        m["S_mm"][c] = cPE.n
                        S.emit("vector",
                               lambda eng, h=hh: eng.tensor_add(
                                   ssc[:, h * 128:(h + 1) * 128],
                                   PB[:, h * 512:h * 512 + 128],
                                   Sst[:, h * 128:(h + 1) * 128]),
                               wait=[(cPE, m["S_mm"][c])], inc=[(cDVE, 1)])
                        m["S_add"][c] = cDVE.n
                        w = [(cDVE, m["S_add"][c]), (cDVE, m["eCl"][c])]
                        if own:
                            w.append((cPE, m["o_mm"][c]))
                        S.emit("vector",
                               lambda eng, h=hh, cc=c: eng.tensor_scalar_mul(
                                   Sst[:, h * 128:(h + 1) * 128],
                                   ssc[:, h * 128:(h + 1) * 128],
                                   eClb[:, h * 8 + cc % 8:h * 8 + cc % 8 + 1]),
                               wait=w, inc=[(cDVE, 1)])
                        m["S_scale"][c] = cDVE.n
                        if own:
                            S.emit("scalar",
                                   lambda eng, h=hh, gg=g, t0=(c - HCH) * CH:
                                   eng.copy(
                                       o3[:, 2 * gg + h, t0:t0 + CH],
                                       PA[:, h * 512:h * 512 + CH]),
                                   wait=[(cPE, m["o_mm"][c])], inc=[(cACT, 1)])
                            m["oevac"][c] = cACT.n

                    for i in range(NCH + 5):
                        for hh in range(2):
                            if i < NCH:
                                sub_scan(i, hh)
                        for hh in range(2):
                            if 1 <= i <= NCH:
                                sub_exps(i - 1, hh)
                        for hh in range(2):
                            if 2 <= i <= NCH + 1:
                                sub_kdec(i - 2, hh)
                        for hh in range(2):
                            if 3 <= i <= NCH + 2:
                                sub_tr(i - 3, hh)
                        for hh in range(2):
                            if 4 <= i <= NCH + 3:
                                sub_scores(i - 4, hh)
                        for hh in range(2):
                            if 5 <= i <= NCH + 4:
                                sub_oS(i - 5, hh)
                        if 2 <= i < 14:
                            next(pcg, None)

                    for hh in range(2):
                        m = M[hh]
                        P[hh]["ct_r"] = (cACT, cACT.n)
                        P[hh]["exp_r"] = (cDVE, cDVE.n)
                        P[hh]["eCl_r"] = (cDVE, cDVE.n)
                        P[hh]["kdec_r"] = (cPE, cPE.n)
                        P[hh]["kdecU_r"] = (cPE, cPE.n)
                        P[hh]["qdec_r"] = (cPE, cPE.n)
                        P[hh]["mask"] = (cDVE, cDVE.n)
                        P[hh]["oevac"] = (cACT, cACT.n)
                        P[hh]["S_add"] = (cDVE, cDVE.n)
                        P[hh]["S_scale0"] = (cDVE, m["S_scale"][NCH - 1])
                        P[hh]["sc_mask"] = scb["mask"]
                        P[hh]["pm_r"] = (cPE, cPE.n)
                        hm[hh]["S_mm"] = m["S_mm"][NCH - 1]
                        hm[hh]["S_scale"] = m["S_scale"][NCH - 1]
                        hm[hh]["o_mm"] = m["o_mm"][NCH - 1]
                    glast["f_r"] = (cACT, cACT.n)
                    glast["k_r"] = (cDVE, cDVE.n)
                    glast["q_r"] = (cDVE, cDVE.n)
                    glast["v_r"] = (cPE, cPE.n)
                    glast["nlf_r"] = (cDVE, cDVE.n)

                for _ in range(200):
                    if next(pcg, None) is None:
                        break
                ln_state["sums_free"] = (cACT, cACT.n)
                ln_state["pa_free"] = (cACT, cACT.n)
                gla_done = [(cACT, cACT.n), (cDVE, cDVE.n), (cPE, cPE.n)]
                rms_m = emit_norm(o3, xn3, T, rms=True, extra_wait=gla_done)

                pp_mark2 = [None, None]
                ppj = [0]
                for oc in range(NDT):
                    osl, ordy, oi = load_slab(aWoT, oc * 128, 128)
                    for sc in (0, 512):
                        i = ppj[0]
                        ppj[0] ^= 1
                        psb = pp[i]
                        for k in range(NDT):
                            w = [ordy, rms_m] if k == 0 else []
                            if k == 0 and pp_mark2[i] is not None:
                                w.append(pp_mark2[i])
                            if k == 0 and pp_mark[i] is not None:
                                w.append(pp_mark[i])
                                pp_mark[i] = None
                            S.emit("tensor",
                                   lambda eng, kk=k, o=oc, c=sc, pb=psb, os_=osl:
                                   eng.matmul(
                                       pb[:, 0:512],
                                       os_[:, kk, 0:128],
                                       xn3[:, kk, c:c + 512],
                                       start=(kk == 0), stop=(kk == NDT - 1)),
                                   wait=w, inc=[(cPE, 1)] if k == NDT - 1 else [])
                        pmk = (cPE, cPE.n)
                        S.emit("vector",
                               lambda eng, o=oc, c=sc, pb=psb: eng.tensor_add(
                                   x3[:, o, c:c + 512], x3[:, o, c:c + 512],
                                   pb[:, 0:512]),
                               wait=[pmk], inc=[(cDVE, 1)])
                        pp_mark2[i] = (cDVE, cDVE.n)
                    slab_done(oi, (cPE, cPE.n))

        ffn_ctr = [0]

        def emit_ffn():
            ffn_ctr[0] += 1
            sfx = str(ffn_ctr[0])
            with ExitStack() as ph:
                er = lambda nm, shape, dt: ph.enter_context(
                    nc.sbuf_tensor(nm + sfx, list(shape), dt, side="right"))
                gated = er("ffn_gated", [128, 64 * 512], BF16)
                dsl = er("ffn_dsl", [128, 2 * 8 * 256], BF16)
                gs = er("ffn_gs", [128, 512], BF16)
                g3 = gated[:].rearrange("p (h t) -> p h t", t=512)
                d4 = dsl[:].rearrange("p (i k c) -> p i k c", i=2, k=8)
                ln_m = emit_norm(x3, xn3, T,
                                 extra_wait=[(cPE, cPE.n), (cDVE, cDVE.n)])
                rot = [PC, PD, PE_]
                rot_mark = [None, None, None]
                gu_slots = [
                    (wsl3[0], 0), (wsl3[0], 128), (wsl3[1], 0), (wsl3[1], 128)]
                slot_mark = [None, None, None, None]
                slot_rdy = [None, None, None, None]
                dsl_mark = [None, None]
                pa_rot = [PA, PB]
                pa_mark = [None, None]
                gs_free = [None]
                ri = [0]
                for half in range(2):
                    hc0 = half * 512
                    for hto in range(64):
                        sg = (hto % 2) * 2
                        su = sg + 1
                        for slot, wd in ((sg, fwg_bf), (su, fwu_bf)):
                            sbuf3, c0s = gu_slots[slot]
                            w = [slot_mark[slot]] if slot_mark[slot] else []
                            if half == 0 and hto < 2:
                                w.append(pc_state["done"])
                            src = wd.ap().rearrange("(k p) c -> p k c", p=128)[
                                :, :, hto * 128:(hto + 1) * 128]
                            S.emit("sync",
                                   lambda eng, b3=sbuf3, cc=c0s, ss=src:
                                   eng.dma_start(out=b3[:, :, cc:cc + 128],
                                                 in_=ss),
                                   wait=w, inc=[(dW, 16)])
                            slot_rdy[slot] = (dW, dW.n)
                        jg = ri[0] % 3
                        ju = (ri[0] + 1) % 3
                        ri[0] += 2
                        marks = {}
                        for slot, j in ((sg, jg), (su, ju)):
                            sbuf3, c0s = gu_slots[slot]
                            pb = rot[j]
                            for k in range(NDT):
                                w = []
                                if k == 0:
                                    w.append(slot_rdy[slot])
                                    w.append(ln_m)
                                    if rot_mark[j] is not None:
                                        w.append(rot_mark[j])
                                S.emit("tensor",
                                       lambda eng, kk=k, b3=sbuf3, cc=c0s,
                                              pbb=pb, c=hc0: eng.matmul(
                                           pbb[:, 0:512],
                                           b3[:, kk, cc:cc + 128],
                                           xn3[:, kk, c:c + 512],
                                           start=(kk == 0),
                                           stop=(kk == NDT - 1)),
                                       wait=w,
                                       inc=[(cPE, 1)] if k == NDT - 1 else [])
                            marks[slot] = (cPE, cPE.n)
                            slot_mark[slot] = (cPE, cPE.n)
                        wsil = [marks[sg]]
                        if gs_free[0] is not None:
                            wsil.append(gs_free[0])
                        S.emit("scalar",
                               lambda eng, pbb=rot[jg]: eng.activation(
                                   gs[:], pbb[:, 0:512], AF.Silu),
                               wait=wsil, inc=[(cACT, 1)])
                        am = cACT.n
                        rot_mark[jg] = (cACT, am)
                        wmul = [(cACT, am), marks[su]]
                        if half == 1 and hto == 0:
                            wmul.append((cPE, cPE.n))
                            if DBG == "gated":
                                wmul.append(dbg_hold["gated_done"])
                        S.emit("vector",
                               lambda eng, h=hto, pbb=rot[ju]: eng.tensor_mul(
                                   g3[:, h, :], gs[:], pbb[:, 0:512]),
                               wait=wmul, inc=[(cDVE, 1)])
                        rot_mark[ju] = (cDVE, cDVE.n)
                        gs_free[0] = (cDVE, cDVE.n)
                        if hto == 63:
                            gated_m = (cDVE, cDVE.n)
                    if DBG == "gated" and half == 0:
                        og = out_d.rearrange("(g p) t -> p g t", p=128)
                        S.emit("gpsimd", lambda eng, o=og: eng.dma_start(
                            out=o[:, :, 0:512], in_=g3[:, 0:16, :]),
                            wait=[gated_m], inc=[(dG, 16)])
                        S.emit("gpsimd", lambda eng, o=og: eng.dma_start(
                            out=o[:, :, 512:1024], in_=g3[:, 16:32, :]),
                            wait=[gated_m], inc=[(dG, 16)])
                        dbg_hold["gated_done"] = (dG, dG.n)
                    for op_ in range(8):
                        pi = op_ % 2
                        pab = pa_rot[pi]
                        dn_marks = []
                        for kc in range(8):
                            di = kc % 2
                            w = [dsl_mark[di]] if dsl_mark[di] else []
                            src = fwd_bf.ap().rearrange(
                                "(k p) c -> p k c", p=128)[
                                :, kc * 8:(kc + 1) * 8,
                                op_ * 256:(op_ + 1) * 256]
                            S.emit("sync",
                                   lambda eng, dd=di, ss=src: eng.dma_start(
                                       out=d4[:, dd, :, :], in_=ss),
                                   wait=w, inc=[(dW, 16)])
                            drdy = (dW, dW.n)
                            for oc2 in range(2):
                                for kk in range(8):
                                    hid = kc * 8 + kk
                                    w = []
                                    if kk == 0 and oc2 == 0:
                                        w.append(drdy)
                                        w.append(gated_m)
                                    if hid == 0 and kk == 0 and oc2 == 0 \
                                            and pa_mark[pi] is not None:
                                        w.append(pa_mark[pi])
                                    lastk = (oc2 == 1 and kk == 7)
                                    S.emit("tensor",
                                           lambda eng, dd=di, k2=kk, o2=oc2,
                                                  hh=hid, pbb=pab: eng.matmul(
                                               pbb[:, o2 * 512:o2 * 512 + 512],
                                               d4[:, dd, k2, o2 * 128:
                                                  o2 * 128 + 128],
                                               g3[:, hh, :],
                                               start=(hh == 0),
                                               stop=(hh == 63)),
                                           wait=w,
                                           inc=[(cPE, 1)] if (lastk or hid == 63)
                                           else [])
                                    if hid == 63:
                                        dn_marks.append((cPE, cPE.n))
                            dsl_mark[di] = (cPE, cPE.n)
                        adds = []
                        for oc2 in range(2):
                            oc = op_ * 2 + oc2
                            S.emit("vector",
                                   lambda eng, o=oc, o2=oc2, c=hc0, pbb=pab:
                                   eng.tensor_add(
                                       x3[:, o, c:c + 512],
                                       x3[:, o, c:c + 512],
                                       pbb[:, o2 * 512:o2 * 512 + 512]),
                                   wait=[dn_marks[oc2]], inc=[(cDVE, 1)])
                            adds.append(cDVE.n)
                        pa_mark[pi] = (cDVE, adds[-1])

        def emit_attn_b():
            with ExitStack() as ph:
                er = lambda nm, shape, dt: ph.enter_context(
                    nc.sbuf_tensor(nm, list(shape), dt, side="right"))
                cosT = er("b_cosT", [128, T], F32)
                sinT = er("b_sinT", [128, T], F32)
                sgn = er("b_sgn", [128, 1], F32)
                QrT = er("b_QrT", [128, 16 * T], BF16)
                KTb = er("b_KTb", [128, 4096], BF16)
                vcb = er("b_vcb", [128, 32 * 192], BF16)
                pTb = er("b_pT", [128, 2 * 1024], BF16)
                qraw = er("b_qraw", [128, 1024], BF16)
                t1 = er("b_t1", [128, 1024], BF16)
                t2 = er("b_t2", [128, 1024], BF16)
                rs = er("b_rs", [128, 512], F32)
                fin = er("b_fin", [128, 512], BF16)
                u_sc = er("b_usc", [1, 64], F32)
                negU = er("b_negU", [128, 4], F32)
                kbr = er("b_kbr", [1, 32], BF16)

                qr3 = QrT[:].rearrange("p (h t) -> p h t", h=16)
                kt3 = KTb[:].rearrange("p (x t) -> p x t", x=4)
                vc3 = vcb[:].rearrange("p (x c) -> p x c", c=192)
                vc8 = vcb[:].rearrange("p (x c) -> p x c", c=768)
                pt3 = pTb[:].rearrange("p (i t) -> p i t", i=2)

                dma("sync", dIn, cosT[:], cosT_in)
                dma("sync", dIn, sinT[:], sinT_in)
                dma("sync", dIn, sgn[:], sgn_in)
                tbl_m = (dIn, dIn.n)

                ln_m = emit_norm(x3, xn3, T,
                                 extra_wait=[(cPE, cPE.n), (cDVE, cDVE.n)])

                pp = [PA, PB]
                pp_mark = [None, None]
                ppi = [0]

                def proj16(slab3, octile, wait0):
                    i = ppi[0]
                    ppi[0] ^= 1
                    psb = pp[i]
                    for k in range(NDT):
                        w = list(wait0) if k == 0 else []
                        if k == 0 and pp_mark[i] is not None:
                            w.append(pp_mark[i])
                        S.emit("tensor",
                               lambda eng, kk=k, s3=slab3, oc=octile, pb=psb:
                               eng.matmul(
                                   pb[:, 0:512],
                                   s3[:, kk, oc * 128:(oc + 1) * 128],
                                   xn3[:, kk, 0:512],
                                   start=(kk == 0), stop=False),
                               wait=w)
                        S.emit("tensor",
                               lambda eng, kk=k, s3=slab3, oc=octile, pb=psb:
                               eng.matmul(
                                   pb[:, 512:1024],
                                   s3[:, kk, oc * 128:(oc + 1) * 128],
                                   xn3[:, kk, 512:1024],
                                   start=(kk == 0), stop=(kk == NDT - 1)),
                               wait=[], inc=[(cPE, 1)] if k == NDT - 1 else [])
                    return psb, i, (cPE, cPE.n)

                rst = {"m": []}

                def emit_rope(psrc, pmk, dsts):
                    w = [pmk] + list(rst["m"])
                    S.emit("scalar", lambda eng, pb=psrc: eng.copy(
                        qraw[:], pb[:, 0:T]), wait=w, inc=[(cACT, 1)])
                    ev = cACT.n
                    S.emit("vector", lambda eng: eng.tensor_mul(
                        t1[:], qraw[:], cosT[:]),
                           wait=[(cACT, ev), tbl_m], inc=[(cDVE, 1)])
                    S.emit("vector", lambda eng: eng.tensor_mul(
                        t2[:], qraw[:], sinT[:]),
                           wait=[(cACT, ev)], inc=[(cDVE, 1)])
                    m2 = cDVE.n
                    for j in range(4):
                        sb0 = (j ^ 1) * 32
                        S.emit("vector",
                               lambda eng, jj=j, sq=sb0: eng.tensor_copy(
                                   qraw[jj * 32:jj * 32 + 32, :],
                                   t2[sq:sq + 32, :]),
                               wait=[(cDVE, m2)], inc=[(cDVE, 1)])
                    m3 = cDVE.n
                    S.emit("vector", lambda eng: eng.scalar_tensor_tensor(
                        t2[:], qraw[:], sgn[:, 0:1], t1[:],
                        ALU.mult, ALU.add),
                           wait=[(cDVE, m3)], inc=[(cDVE, 1)])
                    mr = cDVE.n
                    for hh in range(2):
                        S.emit("vector",
                               lambda eng, h=hh, ad=dsts[hh]: eng.tensor_copy(
                                   ad, t2[h * 64:h * 64 + 64, :]),
                               wait=[(cDVE, mr)], inc=[(cDVE, 1)])
                    rst["m"] = [(cDVE, cDVE.n)]
                    return (cDVE, cDVE.n)

                for ov in range(2):
                    ksl, krdy, ki = load_slab(bWkT, ov * 256, 256)
                    for oc2 in range(2):
                        ot = ov * 2 + oc2
                        psb, i, pmk = proj16(ksl, oc2, [krdy, ln_m])
                        mk = emit_rope(psb, pmk,
                                       [kt3[0:64, ot, :], kt3[64:128, ot, :]])
                        pp_mark[i] = mk
                    slab_done(ki, (cPE, cPE.n))
                kr_done = rst["m"][0]
                k2m = []
                for ot in range(4):
                    w = [kr_done] if ot == 0 else [(cDVE, k2m[-1])]
                    S.emit("scalar", lambda eng, o=ot: eng.activation(
                        t1[:], kt3[:, o, :], AF.Square),
                        wait=w, inc=[(cACT, 1)])
                    sqm = cACT.n
                    wpe = [(cACT, sqm)]
                    if ot > 0:
                        wpe.append((cDVE, k2m[-1]))
                    S.emit("tensor", lambda eng: eng.matmul(
                        PE_[0:1, 0:512], ones1[:], t1[:, 0:512],
                        start=True, stop=False), wait=wpe)
                    S.emit("tensor", lambda eng: eng.matmul(
                        PE_[0:1, 0:512], ones1[:], t1[:, 512:1024],
                        start=False, stop=True), inc=[(cPE, 1)])
                    pm2 = cPE.n
                    S.emit("vector", lambda eng, o=ot: eng.reduce_max(
                        u_sc[0:1, 16 + o:17 + o], PE_[0:1, 0:512],
                        axis=mybir.AxisListType.X),
                        wait=[(cPE, pm2)], inc=[(cDVE, 1)])
                    k2m.append(cDVE.n)
                S.emit("vector", lambda eng: eng.tensor_copy(
                    kbr[0:1, 0:4], u_sc[0:1, 16:20]),
                    wait=[(cDVE, k2m[-1])], inc=[(cDVE, 1)])
                kbr_m = (cDVE, cDVE.n)

                pf_free2 = None
                vtr_last = [None]
                vlast_sq = (cDVE, cDVE.n)
                for ov in range(2):
                    vsl, vrdy, vi = load_slab(bWvT, ov * 256, 256)
                    for oc2 in range(2):
                        ot = ov * 2 + oc2
                        psb, i, pmk = proj16(vsl, oc2, [vrdy, ln_m])
                        w = [pmk, kr_done, (cACT, cACT.n)]
                        if vtr_last[0] is not None:
                            w.append(vtr_last[0])
                        S.emit("scalar", lambda eng, pb=psb: eng.copy(
                            t2[:], pb[:, 0:T]), wait=w, inc=[(cACT, 1)])
                        evm = cACT.n
                        pp_mark[i] = (cACT, evm)
                        for tt in range(8):
                            w = [(cACT, evm), (dIn, in_done)]
                            if pf_free2 is not None:
                                w.append(pf_free2)
                            S.emit("tensor",
                                   lambda eng, t=tt: eng.transpose(
                                       PF[:, 0:128],
                                       t2[:, t * 128:(t + 1) * 128], idb[:]),
                                   wait=w, inc=[(cPE, 1)])
                            trm = cPE.n
                            vtr_last[0] = (cPE, trm)
                            S.emit("vector",
                                   lambda eng, t=tt, o=ot: eng.tensor_copy(
                                       vc8[:, t, o * 128:(o + 1) * 128],
                                       PF[:, 0:128]),
                                   wait=[(cPE, trm)], inc=[(cDVE, 1)])
                            pf_free2 = (cDVE, cDVE.n)
                    slab_done(vi, (cPE, cPE.n))
                v_done = (cDVE, cDVE.n)
                rst["m"] = [vtr_last[0], (cDVE, cDVE.n)]

                S.emit("sync", lambda eng: eng.dma_start(
                    out=cc_in_k.ap()[0:512, :].rearrange(
                        "(x p) t -> p x t", p=128),
                    in_=kt3[:, :, :]), wait=[kr_done], inc=[(dG, 16)])
                S.emit("sync", lambda eng: eng.dma_start(
                    out=cc_in_k.ap()[512:513, 0:4], in_=kbr[0:1, 0:4]),
                    wait=[kbr_m], inc=[(dG, 16)])
                S.emit("sync", lambda eng: eng.dma_start(
                    out=cc_in_v.ap().rearrange("(x p) c -> p x c", p=128),
                    in_=vc8[:, :, 0:512]),
                    wait=[v_done], inc=[(dG, 16)])
                gm = dG.n
                S.emit("gpsimd", lambda eng: eng.collective_compute(
                    "AllGather", ALU.bypass, replica_groups=RG,
                    ins=[cc_in_k.ap().opt()], outs=[cc_out_k.ap().opt()]),
                    wait=[(dG, gm)], inc=[(cCC, 1)])
                S.emit("gpsimd", lambda eng: eng.collective_compute(
                    "AllGather", ALU.bypass, replica_groups=RG,
                    ins=[cc_in_v.ap().opt()], outs=[cc_out_v.ap().opt()]),
                    inc=[(cCC, 1)])
                cc_m = (cCC, cCC.n)

                q2m = []
                for oq in range(8):
                    qsl, qrdy, qi = load_slab(bWqT, oq * 256, 256)
                    for oc2 in range(2):
                        ot = oq * 2 + oc2
                        dsts = []
                        for hh in range(2):
                            q = 2 * ot + hh
                            tile = (q // 8) * 4 + (q % 4)
                            base = ((q // 4) % 2) * 64
                            dsts.append(qr3[base:base + 64, tile, :])
                        psb, i, pmk = proj16(qsl, oc2, [qrdy, ln_m])
                        mk = emit_rope(psb, pmk, dsts)
                        pp_mark[i] = mk
                        S.emit("scalar", lambda eng: eng.activation(
                            t1[:], t2[:], AF.Square),
                            wait=[mk], inc=[(cACT, 1)])
                        sqm = cACT.n
                        wq2 = [(cACT, sqm)]
                        if q2m:
                            wq2.append((cDVE, q2m[-1]))
                        else:
                            wq2.append((cDVE, k2m[-1]))
                        S.emit("tensor", lambda eng: eng.matmul(
                            PE_[0:1, 0:512], ones1[:], t1[:, 0:512],
                            start=True, stop=False), wait=wq2)
                        S.emit("tensor", lambda eng: eng.matmul(
                            PE_[0:1, 0:512], ones1[:], t1[:, 512:1024],
                            start=False, stop=True), inc=[(cPE, 1)])
                        pm2 = cPE.n
                        S.emit("vector", lambda eng, o=ot: eng.reduce_max(
                            u_sc[0:1, o:o + 1], PE_[0:1, 0:512],
                            axis=mybir.AxisListType.X),
                            wait=[(cPE, pm2)], inc=[(cDVE, 1)])
                        q2m.append(cDVE.n)
                        rst["m"] = [(cACT, sqm), (cDVE, cDVE.n)]
                    slab_done(qi, (cPE, cPE.n))
                q_done = (cDVE, cDVE.n)

                S.emit("sync", lambda eng: eng.dma_start(
                    out=kbr[0:1, 0:16].rearrange("p (r c) -> p r c", r=4),
                    in_=cc_out_k.ap().rearrange("(r x) t -> x r t", r=4)[
                        512:513, :, 0:4]),
                    wait=[cc_m], inc=[(dG, 16)])
                kb_m = (dG, dG.n)
                S.emit("vector", lambda eng: eng.tensor_max(
                    u_sc[0:1, 52:56], kbr[0:1, 0:4], kbr[0:1, 4:8]),
                    wait=[kb_m, (cDVE, k2m[-1])], inc=[(cDVE, 1)])
                S.emit("vector", lambda eng: eng.tensor_max(
                    u_sc[0:1, 56:60], kbr[0:1, 8:12], kbr[0:1, 12:16]),
                    wait=[(cDVE, cDVE.n)], inc=[(cDVE, 1)])
                S.emit("vector", lambda eng: eng.tensor_max(
                    u_sc[0:1, 16:20], u_sc[0:1, 52:56], u_sc[0:1, 56:60]),
                    wait=[(cDVE, cDVE.n)], inc=[(cDVE, 1)])
                q2v = u_sc[0:1, 0:16].rearrange("p (u t) -> p u t", u=4)
                S.emit("vector", lambda eng: eng.tensor_max(
                    u_sc[0:1, 20:24], q2v[:, :, 0], q2v[:, :, 1]),
                    wait=[(cDVE, q2m[-1]), (cDVE, cDVE.n)], inc=[(cDVE, 1)])
                S.emit("vector", lambda eng: eng.tensor_max(
                    u_sc[0:1, 24:28], q2v[:, :, 2], q2v[:, :, 3]),
                    wait=[(cDVE, cDVE.n)], inc=[(cDVE, 1)])
                S.emit("vector", lambda eng: eng.tensor_max(
                    u_sc[0:1, 28:32], u_sc[0:1, 20:24], u_sc[0:1, 24:28]),
                    wait=[(cDVE, cDVE.n)], inc=[(cDVE, 1)])
                S.emit("scalar", lambda eng: eng.activation(
                    u_sc[0:1, 32:36], u_sc[0:1, 28:32], AF.Ln),
                    wait=[(cDVE, cDVE.n)], inc=[(cACT, 1)])
                S.emit("scalar", lambda eng: eng.activation(
                    u_sc[0:1, 36:40], u_sc[0:1, 16:20], AF.Ln),
                    wait=[(cACT, cACT.n)], inc=[(cACT, 1)])
                S.emit("vector", lambda eng: eng.tensor_add(
                    u_sc[0:1, 40:44], u_sc[0:1, 32:36], u_sc[0:1, 36:40]),
                    wait=[(cACT, cACT.n)], inc=[(cDVE, 1)])
                S.emit("scalar", lambda eng: eng.activation(
                    u_sc[0:1, 44:48], u_sc[0:1, 40:44], AF.Exp, scale=0.5),
                    wait=[(cDVE, cDVE.n)], inc=[(cACT, 1)])
                S.emit("vector", lambda eng: eng.tensor_scalar_mul(
                    u_sc[0:1, 48:52], u_sc[0:1, 44:48], -0.125),
                    wait=[(cACT, cACT.n)], inc=[(cDVE, 1)])
                nu_m = cDVE.n
                S.emit("tensor", lambda eng: eng.matmul(
                    PE_[:, 0:4], onesF[:], u_sc[0:1, 48:52],
                    start=True, stop=True),
                    wait=[(cDVE, nu_m)], inc=[(cPE, 1)])
                S.emit("scalar", lambda eng: eng.copy(negU[:], PE_[:, 0:4]),
                    wait=[(cPE, cPE.n)], inc=[(cACT, 1)])
                nu_done = (cACT, cACT.n)

                SP = [PA, PB]
                sp_mark = [(cACT, cACT.n), (cACT, cACT.n)]
                pt_mark = [None, None]
                oacc_mark = [None, None]
                kv_mark = None
                wo_marks = []
                wo_pp = [PC, PD]
                wo_mark = [None, None]
                woi = [0]
                for qt in range(2):
                    att_done = []
                    for u in range(4):
                        w = [cc_m, (cDVE, k2m[-1]), kbr_m]
                        if kv_mark is not None:
                            w.append(kv_mark)
                        for par in range(2):
                            kvh = 2 * u + par
                            S.emit("sync",
                                   lambda eng, v=kvh, b=par * 64: eng.dma_start(
                                       out=KTb[b:b + 64, :].rearrange(
                                           "p (r t) -> p r t", r=4),
                                       in_=cc_out_k.ap().rearrange(
                                           "(r x) t -> x r t", r=4)[
                                           v * 64:(v + 1) * 64, :, :]),
                                   wait=w if par == 0 else [],
                                   inc=[(dG, 16)])
                            S.emit("sync",
                                   lambda eng, v=kvh, c0=par * 128:
                                   eng.dma_start(
                                       out=vc3[:, :, c0:c0 + 64] if c0 == 0
                                       else vc3[:, :, 128:192],
                                       in_=cc_out_v.ap().rearrange(
                                           "(x p) c -> p x c", p=128)[
                                           :, :, v * 64:(v + 1) * 64]),
                                   wait=[], inc=[(dG, 16)])
                        kv_rdy = (dG, dG.n)
                        if qt == 0 and u == 0:
                            S.emit("vector", lambda eng: eng.memset(
                                vc3[:, :, 64:128], 1.0),
                                wait=[v_done, (dG, gm)], inc=[(cDVE, 1)])
                        ones_m = (cDVE, cDVE.n)
                        for sp_ in range(4):
                            qa = 8 * u + sp_
                            tile = u * 4 + sp_
                            exp_marks = {}
                            for kt in range(33):
                                pj = kt % 2
                                spb = SP[pj]
                                if kt < 32:
                                    w = [kv_rdy, ones_m, q_done, nu_done]
                                    if sp_mark[pj] is not None:
                                        w.append(sp_mark[pj])
                                    S.emit("tensor",
                                           lambda eng, k2=kt, tl=tile, pb=spb,
                                                  c=qt * 512: eng.matmul(
                                               pb[:, 0:512],
                                               KTb[0:64, k2 * 128:(k2 + 1) * 128],
                                               qr3[0:64, tl, c:c + 512],
                                               start=True, stop=True),
                                           wait=w)
                                    S.emit("tensor",
                                           lambda eng, k2=kt, tl=tile, pb=spb,
                                                  c=qt * 512: eng.matmul(
                                               pb[:, 512:1024],
                                               KTb[64:128, k2 * 128:(k2 + 1) * 128],
                                               qr3[64:128, tl, c:c + 512],
                                               start=True, stop=True),
                                           wait=[], inc=[(cPE, 1)])
                                    mm1_m = cPE.n
                                    w = [(cPE, mm1_m)]
                                    if pt_mark[pj] is not None:
                                        w.append(pt_mark[pj])
                                    S.emit("scalar",
                                           lambda eng, pb=spb, pj2=pj, uu=u:
                                           eng.activation(
                                               pt3[:, pj2, :], pb[:, 0:1024],
                                               AF.Exp, scale=0.125,
                                               bias=negU[:, uu:uu + 1]),
                                           wait=w, inc=[(cACT, 1)])
                                    exp_marks[kt] = cACT.n
                                    sp_mark[pj] = (cACT, cACT.n)
                                if kt >= 1:
                                    k2p = kt - 1
                                    pjp = k2p % 2
                                    w = [(cACT, exp_marks[k2p])]
                                    if k2p == 0 and oacc_mark[0] is not None:
                                        w += [oacc_mark[0], oacc_mark[1]]
                                    S.emit("tensor",
                                           lambda eng, k2=k2p, pj2=pjp: eng.matmul(
                                               PC[:, 0:512],
                                               vc3[:, k2, 0:128],
                                               pt3[:, pj2, 0:512],
                                               start=(k2 == 0), stop=(k2 == 31)),
                                           wait=w)
                                    S.emit("tensor",
                                           lambda eng, k2=k2p, pj2=pjp: eng.matmul(
                                               PD[:, 0:512],
                                               vc3[:, k2, 64:192],
                                               pt3[:, pj2, 512:1024],
                                               start=(k2 == 0), stop=(k2 == 31)),
                                           wait=[],
                                           inc=[(cPE, 1)] if k2p == 31 else [])
                                    pt_mark[pjp] = (cPE, cPE.n)
                            mm2_m = (cPE, cPE.n)
                            for hh, (pb, orow, srow) in enumerate(
                                    ((PC, 0, 64), (PD, 64, 0))):
                                q = qa + 4 * hh
                                S.emit("vector",
                                       lambda eng, p2=pb, orw=orow, sr=srow:
                                       eng.reciprocal(
                                           rs[orw:orw + 64, :],
                                           p2[sr:sr + 64, 0:512]),
                                       wait=[mm2_m], inc=[(cDVE, 1)])
                                rm = cDVE.n
                                S.emit("vector",
                                       lambda eng, p2=pb, orw=orow:
                                       eng.tensor_mul(
                                           fin[orw:orw + 64, :],
                                           p2[orw:orw + 64, 0:512],
                                           rs[orw:orw + 64, :]),
                                       wait=[(cDVE, rm)], inc=[(cDVE, 1)])
                                fm = cDVE.n
                                nt, nb = q // 2, (q % 2) * 64
                                S.emit("vector",
                                       lambda eng, t3=nt, b3=nb, orw=orow,
                                              c=qt * 512: eng.tensor_copy(
                                           xn3[b3:b3 + 64, t3, c:c + 512],
                                           fin[orw:orw + 64, :]),
                                       wait=[(cDVE, fm)], inc=[(cDVE, 1)])
                                att_done.append(cDVE.n)
                            oacc_mark[0] = (cDVE, cDVE.n)
                            oacc_mark[1] = (cDVE, cDVE.n)
                        kv_mark = (cPE, cPE.n)
                    att_m = (cDVE, att_done[-1])
                    for oc in range(NDT):
                        osl, ordy, oi = load_slab(bWoT, oc * 128, 128)
                        i = woi[0]
                        woi[0] ^= 1
                        psb = wo_pp[i]
                        for k in range(NDT):
                            w = [ordy, att_m] if k == 0 else []
                            if k == 0 and wo_mark[i] is not None:
                                w.append(wo_mark[i])
                            S.emit("tensor",
                                   lambda eng, kk=k, pb=psb, os_=osl,
                                          c=qt * 512: eng.matmul(
                                       pb[:, 0:512], os_[:, kk, 0:128],
                                       xn3[:, kk, c:c + 512],
                                       start=(kk == 0), stop=(kk == NDT - 1)),
                                   wait=w, inc=[(cPE, 1)] if k == NDT - 1 else [])
                        pmk = (cPE, cPE.n)
                        S.emit("vector",
                               lambda eng, o=oc, pb=psb, c=qt * 512:
                               eng.tensor_add(
                                   x3[:, o, c:c + 512], x3[:, o, c:c + 512],
                                   pb[:, 0:512]),
                               wait=[pmk], inc=[(cDVE, 1)])
                        wo_mark[i] = (cDVE, cDVE.n)
                        slab_done(oi, pmk)
                        wo_marks.append(cDVE.n)

        if NSUB == -1:
            barrier()
            emit_ffn()
        if NSUB >= 1:
            barrier()
            emit_attn_a()
        if NSUB >= 2:
            barrier()
            emit_ffn()
        if NSUB >= 3:
            barrier()
            emit_attn_b()
        if NSUB >= 4:
            barrier()
            emit_ffn()

        if DBG == "attnT":
            S.emit("gpsimd", lambda eng: eng.dma_start(
                out=out_d.rearrange("(d p) t -> p d t", p=128), in_=xn3),
                wait=[(c, c.n) for c in allch], inc=[(dOut, 16)])
            S.emit("sync", lambda eng: None, wait=[(dOut, dOut.n)])
        elif DBG == "gated":
            S.emit("sync", lambda eng: None,
                   wait=[(c, c.n) for c in allch] + [(dG, dG.n)])
        elif DBG in ("xn", "o", "q", "k", "nlf", "v"):
            src_dbg = {"xn": xn3}.get(DBG)
            if src_dbg is None:
                src_dbg = dbg_hold[DBG]
            S.emit("gpsimd", lambda eng, sd=src_dbg: eng.dma_start(
                out=out_d.rearrange("(d p) t -> p d t", p=128), in_=sd),
                wait=[(c, c.n) for c in allch], inc=[(dOut, 16)])
        else:
            S.emit("sync", lambda eng: eng.dma_start(
                out=out_d.rearrange("(d p) t -> p d t", p=128), in_=x3),
                wait=[(c, c.n) for c in allch], inc=[(dOut, 16)])
        S.emit("sync", lambda eng: None, wait=[(dOut, dOut.n)])

        with nc.Block() as block:
            @block.sync
            def _(eng):
                for f in S.q["sync"]:
                    f(eng)

            @block.gpsimd
            def _(eng):
                for f in S.q["gpsimd"]:
                    f(eng)

            @block.vector
            def _(eng):
                for f in S.q["vector"]:
                    f(eng)

            @block.scalar
            def _(eng):
                for f in S.q["scalar"]:
                    f(eng)

            @block.tensor
            def _(eng):
                for f in S.q["tensor"]:
                    f(eng)

    return nc


def _host_inputs(L0_attn_a_x, weights):
    x = np.asarray(L0_attn_a_x)
    s, t = np.meshgrid(np.arange(CH), np.arange(CH), indexing="ij")
    maskT = (t >= s).astype(BFNP)
    idb = np.eye(128, dtype=BFNP)
    ones1 = np.ones((128, 1), dtype=BFNP)
    onesF = np.ones((1, 128), dtype=np.float32)
    hsum2 = np.zeros((128, 2), dtype=BFNP)
    hsum2[0:64, 0] = 1
    hsum2[64:128, 1] = 1
    inv_freq = 1.0 / ROPE_THETA ** (np.arange(0, AD, 2, dtype=np.float32) / AD)

    in_maps = []
    for c in range(NCORES):
        b, q = c // 4, c % 4
        t0 = q * T
        own = x[b, t0:t0 + T]
        if t0 == 0:
            halo = np.zeros((HALO, D), dtype=np.float32)
        else:
            halo = x[b, t0 - HALO:t0]
        pos = np.arange(t0, t0 + T, dtype=np.float32)
        ang = np.outer(inv_freq, pos)
        cosT = np.tile(np.concatenate([np.cos(ang), np.cos(ang)], 0), (2, 1))
        sinT = np.tile(np.concatenate([np.sin(ang), np.sin(ang)], 0), (2, 1))
        m = {
            "xT": np.ascontiguousarray(own.T),
            "xTh": np.ascontiguousarray(halo.T),
            "maskT": maskT, "idb": idb, "ones1": ones1, "onesF": onesF,
            "hsum2": hsum2,
            "cosT": np.ascontiguousarray(cosT, dtype=np.float32),
            "sinT": np.ascontiguousarray(sinT, dtype=np.float32),
            "fconst": np.array([[EPS_LN, EPS_RMS, 0.0, 0.0]], dtype=np.float32),
            "sgn": np.ascontiguousarray(
                np.where((np.arange(128) % 64) < 32, -1.0, 1.0)
                .astype(np.float32).reshape(128, 1)),
        }
        m.update(weights)
        in_maps.append(m)
    return in_maps


def kernel(L0_attn_a_x, L0_attn_a_Wq, L0_attn_a_Wk, L0_attn_a_Wv, L0_attn_a_Wo,
           L0_attn_a_norm_w, L0_attn_b_Wq, L0_attn_b_Wk, L0_attn_b_Wv,
           L0_attn_b_Wo, L0_ffn_w_gate, L0_ffn_w_up, L0_ffn_w_down):
    tr = lambda w: np.ascontiguousarray(np.asarray(w).T)
    weights = {
        "aWqT": tr(L0_attn_a_Wq), "aWkT": tr(L0_attn_a_Wk),
        "aWvT": tr(L0_attn_a_Wv), "aWoT": tr(L0_attn_a_Wo),
        "anw": np.ascontiguousarray(
            np.asarray(L0_attn_a_norm_w).reshape(NDT, 128).T),
        "bWqT": tr(L0_attn_b_Wq), "bWkT": tr(L0_attn_b_Wk),
        "bWvT": tr(L0_attn_b_Wv), "bWoT": tr(L0_attn_b_Wo),
        "fWgT": tr(L0_ffn_w_gate), "fWuT": tr(L0_ffn_w_up),
        "fWdT": tr(L0_ffn_w_down),
    }
    in_maps = _host_inputs(L0_attn_a_x, weights)
    nc = build_program()
    res = run_bass_kernel_spmd(nc, in_maps, core_ids=list(range(NCORES)))
    out = np.empty((B, N, D), dtype=np.float32)
    for c in range(NCORES):
        b, q = c // 4, c % 4
        out[b, q * T:(q + 1) * T] = res.results[c]["oT"].T
    return out
